# version 07
# baseline (speedup 1.0000x reference)
import contextlib
import numpy as np
import ml_dtypes
import concourse.bass as bass
import concourse.mybir as mybir
from concourse.bass import ds
from concourse.bass_utils import run_bass_kernel_spmd

F32 = mybir.dt.float32
BF16 = mybir.dt.bfloat16
I32 = mybir.dt.int32
ALU = mybir.AluOpType
ACT = mybir.ActivationFunctionType
BF = ml_dtypes.bfloat16

D_MODEL = 1024
D_INNER = 2048
DOWN = 1024
S = 16
L = 8192
NB = 4
TC = 63
TILE = 126
QUAD = 4

N_P4 = 32
DEBUG_PHASE = 4


def _tiles(n, sz):
    out = []
    t = 0
    while t < n:
        ln = min(sz, n - t)
        out.append((t, ln))
        t += ln
    return out


P1T = _tiles(L, TILE)
CHUNKS = _tiles(L, TC)
NCHUNK = len(CHUNKS)
QUADS = [list(range(i, min(i + QUAD, NCHUNK))) for i in range(0, NCHUNK, QUAD)]
T128T = _tiles(L, 128)

SBUF_ALLOCS = [
    ("WinT_sb", [128, 8 * 2048], BF16),
    ("WxT_sb", [128, 8 * 33], BF16),
    ("WoutT_sb", [128, 8 * 1024], BF16),
    ("baseWT_sb", [128, 8 * 1024], BF16),
    ("sharedAT_sb", [128, 8 * 16], BF16),
    ("sharedBT_sb", [16, 1024], BF16),
    ("routerWT_sb", [128, 8 * 8], BF16),
    ("routedAT_sb", [128, 8 * 96], BF16),
    ("routedBT_sb", [96, 1024], BF16),
    ("mtri_sb", [128, 2 * 64], BF16),
    ("mtriD_sb", [128, 2 * 64], F32),
    ("ident_sb", [128, 128], BF16),
    ("onesrow_sb", [1, 128], F32),
    ("onesrowb_sb", [1, 128], BF16),
    ("convw_sb", [128, 8 * 4], BF16),
    ("convb_sb", [128, 8], F32),
    ("acol_sb", [128, 2 * S], F32),
    ("arow_sb", [1, S], F32),
    ("wdtrep_sb", [128, 512], F32),
    ("bdtrep_sb", [128, 512], F32),
    ("drep_sb", [128, DOWN], BF16),
    ("lnwrep_sb", [128, DOWN], BF16),
    ("lnbrep_sb", [128, DOWN], BF16),
    ("bbrep_sb", [1, D_MODEL], BF16),
    ("rbrow_sb", [1, 8], BF16),
    ("slot_sb", [1, 1], I32),
    ("xiraw", [128, DOWN], BF16),
    ("szt", [128, DOWN], BF16),
    ("win", [128, 8 * (TILE + 3)], BF16),

    ("xi2tile", [128, 8 * TILE], BF16),
    ("xdTbuf", [33, L], BF16),
    ("xi2hs_t", [128, 2 * 512], BF16),
    ("xdhs", [128, 2 * QUAD * 33], F32),
    ("dthsq", [128, 2 * QUAD * 512], F32),
    ("dtxhsq", [128, 2 * QUAD * 512], BF16),
    ("xi2hsq", [128, 2 * QUAD * 512], BF16),
    ("dsbq", [128, 2 * QUAD * 512], F32),
    ("em_sb", [128, 2 * QUAD * 512], BF16),
    ("ep_sb", [128, 2 * QUAD * 512], BF16),
    ("ptmp", [128, 512], BF16),
    ("wts_sb", [128, 4 * 512], BF16),
    ("hets_sb", [128, 4 * 512], BF16),
    ("carrysave", [32, 512], BF16),
    ("accq", [128, QUAD * 512], BF16),
    ("y_ld", [128, DOWN], BF16),
    ("xi2_ld", [128, DOWN], BF16),
    ("sz_ld", [128, DOWN], BF16),
    ("statsbuf", [128, 192], F32),
    ("statspeer", [128, 128], F32),
    ("gt1", [128, DOWN], BF16),
    ("gt2", [128, DOWN], BF16),
    ("og_sb", [128, DOWN], BF16),
    ("ogT_sb", [128, DOWN], BF16),
    ("hshT_sb", [16, 128], BF16),
    ("hrw_sb", [128, 96], BF16),
    ("hrwT_sb", [96, 128], BF16),
    ("r6", [128, 48], F32),
    ("sc1", [128, 16], F32),
    ("epscol", [128, 1], F32),
]

SEM_NAMES = [
    "ser_v", "ser_s", "ser_g", "ld_w", "ld_x", "mm_xz", "ev_xz", "tp_xi", "ev_xi", "cv", "tp_hs", "ev_hs",
    "mm_xd", "ev_xd", "sp_dma", "x1_dma", "bar", "p2_ld", "tp_c", "ev_c",
    "p2l0", "p2l1", "p2l2", "p2l3", "y0", "y1", "y2", "y3", "y4", "y5", "y6", "y7",
    "dt_act", "dt_dve", "mm_d", "ev_dq", "e_act", "w_dve", "cr_dma", "mm_g",
    "he_dve", "acc_dve", "y_dma", "p3_ld", "st_dve", "x2_dma", "gt_dve",
    "tp_og", "ev_og", "mm_wo", "ev_wo", "x3_dma", "p4_ld", "p4_mm", "p4_dve",
    "p4_act", "p4_tp", "out_dma", "dbg_dma", "x1b_dma", "dbg2_dma", "p3t_ld", "p4g_ld",
] + [f"cr{i}" for i in range(8)] + [f"sp{i}" for i in range(4)]


class Sched:
    ENGINES = ("sync", "tensor", "vector", "scalar", "gpsimd")

    def __init__(self, nc, sems):
        self.nc = nc
        self.sems = sems
        self.cnt = {}
        self.cur = None
        self.eng = None

    def begin(self, eng_name, eng):
        self.cur = eng_name
        self.eng = eng
        self.cnt = {k: 0 for k in self.sems}

    def inc(self, sem, amt):
        self.cnt[sem] += amt
        return self.cnt[sem]

    def on(self, eng_name):
        return self.cur == eng_name

    SERIAL = {"vector": "ser_v", "scalar": "ser_s", "gpsimd": "ser_g"}

    def op(self, eng_name, fn, sem=None, amt=1):
        ser = Sched.SERIAL.get(eng_name)
        if ser is not None and sem is None:
            sem, amt = ser, 1
        if self.cur == eng_name:
            if ser is not None:
                self.eng.wait_ge(self.sems[ser], self.cnt[ser])
            inst = fn(self.eng)
            if sem is not None:
                inst.then_inc(self.sems[sem], amt)
        if sem is not None:
            self.inc(sem, amt)
        return self.cnt[sem] if sem is not None else None

    def vop(self, fn):
        return self.op("vector", fn)

    def sop(self, fn):
        return self.op("scalar", fn)

    def dma(self, eng_name, out, in_, sem, **kw):
        self.wait(eng_name, sem, self.cnt[sem])
        self.op(eng_name, lambda e: e.dma_start(out=out, in_=in_, **kw), sem, 16)

    def wait(self, eng_name, sem, val=None):
        v = self.cnt[sem] if val is None else val
        assert v <= self.cnt[sem], (sem, v, self.cnt[sem])
        if self.cur == eng_name:
            self.eng.wait_ge(self.sems[sem], v)

    def wait_all(self, sem, val=None):
        for e in Sched.ENGINES:
            self.wait(e, sem, val)


def build_program(debug_phase=DEBUG_PHASE):
    nc = bass.Bass("TRN2", target_bir_lowering=False)
    dtt = nc.dram_tensor
    g = {}

    def din(name, shape, dtype):
        g[name] = dtt(name, shape, dtype, kind="ExternalInput").ap()

    din("xT", [D_MODEL, L], F32)
    din("x_rows", [L // 2, D_MODEL], F32)
    din("W_inT", [D_MODEL, 2 * DOWN], F32)
    din("convw", [DOWN, 4], F32)
    din("convb", [128, 8], F32)
    din("W_xT", [DOWN, 33], F32)
    din("W_dt_own", [1, DOWN], F32)
    din("b_dt_own", [1, DOWN], F32)
    din("A_log_in", [1, S], F32)
    din("D_own", [1, DOWN], F32)
    din("lnw_own", [1, DOWN], F32)
    din("lnb_own", [1, DOWN], F32)
    din("W_outT", [DOWN, D_MODEL], F32)
    din("baseWT", [D_MODEL, D_MODEL], F32)
    din("base_b", [1, D_MODEL], F32)
    din("sharedAT", [D_MODEL, 16], F32)
    din("sharedBT", [16, D_MODEL], F32)
    din("routerWT", [D_MODEL, 8], F32)
    din("router_b", [1, 8], F32)
    din("routedAT", [D_MODEL, 96], F32)
    din("routedBT", [96, D_MODEL], F32)
    din("mtri_aug", [128, 64], BF16)
    din("mtri_tail", [128, 64], BF16)
    din("mtriD", [128, 64], F32)
    din("mtriD_tail", [128, 64], F32)
    din("ident", [128, 128], BF16)
    din("onesrow", [1, 128], F32)
    din("onesrowb", [1, 128], BF16)
    din("slot", [1, 1], I32)

    g["out"] = dtt("out", [L // 2, D_MODEL], F32, kind="ExternalOutput").ap()
    g["xi2hs_hbm"] = dtt("xi2hs", [NCHUNK, 128, 512], BF16, kind="Internal").ap()
    g["xi2t_hbm"] = dtt("xi2t", [DOWN, L], BF16, kind="Internal").ap()
    g["sz_hbm"] = dtt("szsp", [L, DOWN], BF16, kind="Internal").ap()
    g["y_hbm"] = dtt("ysp", [L, DOWN], BF16, kind="Internal").ap()
    g["shr_xd"] = dtt("shr_xd", [66, L], BF16, kind="Internal", addr_space="Shared").ap()
    g["shr_st"] = dtt("shr_st", [256, 128], F32, kind="Internal", addr_space="Shared").ap()
    g["shr_out"] = dtt("shr_out", [2 * L, D_MODEL], BF16, kind="Internal", addr_space="Shared").ap()
    g["myout_hbm"] = dtt("myout", [L, D_MODEL], BF16, kind="Internal").ap()
    g["loc0_hbm"] = dtt("loc0", [L // 2, D_MODEL], BF16, kind="Internal").ap()
    g["loc1_hbm"] = dtt("loc1", [L // 2, D_MODEL], BF16, kind="Internal").ap()
    g["cbar_in"] = dtt("cbar_in", [1, 16], F32, kind="Internal").ap()
    g["cbar_out"] = dtt("cbar_out", [1, 16], F32, kind="Internal").ap()
    RG = [[0, 1], [2, 3], [4, 5], [6, 7]]

    if debug_phase == 1:
        g["d_xi2t"] = dtt("d_xi2t", [DOWN, L], BF16, kind="ExternalOutput").ap()
        g["d_sz"] = dtt("d_sz", [L, DOWN], BF16, kind="ExternalOutput").ap()
        g["d_xdT"] = dtt("d_xdT", [33, L], BF16, kind="ExternalOutput").ap()
    if debug_phase == 2:
        g["d_y"] = dtt("d_y", [L, DOWN], BF16, kind="ExternalOutput").ap()

    stack = contextlib.ExitStack()
    with stack:
        for name, shape, dty in SBUF_ALLOCS:
            g[name] = stack.enter_context(nc.sbuf_tensor(name, shape, dty))
        g["ps_a"] = stack.enter_context(nc.psum_tensor("ps_a", [128, 2048], F32))
        g["ps_t1"] = stack.enter_context(nc.psum_tensor("ps_t1", [128, 1024], BF16))
        g["ps_xd"] = stack.enter_context(nc.psum_tensor("ps_xd", [33, TILE], F32))
        g["ps_g"] = stack.enter_context(nc.psum_tensor("ps_g", [128, 2 * 512], F32))
        g["op_sb"] = g["accq"][:, 0:1024]
        g["xTb"] = g["em_sb"][:, 0:2 * 8 * TILE]
        g["cvtmp"] = g["ep_sb"][:, 0:8 * TILE]
        g["cvacc"] = g["ep_sb"][:, 8 * TILE:2 * 8 * TILE]
        g["p4a"] = g["y_ld"]
        g["p4b"] = g["xi2_ld"]
        g["x1n_sb"] = g["sz_ld"]
        g["x1nT_sb"] = g["gt1"]
        g["xrow_sb"] = g["dthsq"][:, 0:1024]
        g["x1_sb"] = g["dthsq"][:, 1024:2048]
        g["x2_sb"] = g["dsbq"][:, 0:1024]
        sems = {}
        for s_ in SEM_NAMES:
            sems[s_] = stack.enter_context(nc.semaphore(s_))
        sch = Sched(nc, sems)

        def walk():
            emit_setup(sch, g, nc, debug_phase)
            emit_p1(sch, g, nc, debug_phase)
            if debug_phase >= 2:
                emit_x1(sch, g, nc, RG)
                emit_p2(sch, g, nc, debug_phase)
            if debug_phase >= 3:
                emit_p3(sch, g, nc, RG, debug_phase)
            if debug_phase >= 4:
                emit_p4(sch, g, nc, RG)
            emit_tail(sch, g, debug_phase)

        with nc.Block() as block:
            @block.sync
            def _(eng):
                sch.begin("sync", eng)
                walk()

            @block.tensor
            def _(eng):
                sch.begin("tensor", eng)
                walk()

            @block.vector
            def _(eng):
                sch.begin("vector", eng)
                walk()

            @block.scalar
            def _(eng):
                sch.begin("scalar", eng)
                walk()

            @block.gpsimd
            def _(eng):
                sch.begin("gpsimd", eng)
                walk()
    return nc


def emit_setup(sch, g, nc, debug_phase):
    G = "gpsimd"

    def ld(dst, src):
        sch.dma(G, dst, src, "ld_w")

    ld(g["WinT_sb"][:].rearrange("p (a o) -> p a o", a=8),
       g["W_inT"].rearrange("(a p) o -> p a o", p=128))
    ld(g["WxT_sb"][:].rearrange("p (a o) -> p a o", a=8),
       g["W_xT"].rearrange("(a p) o -> p a o", p=128))
    ld(g["WoutT_sb"][:].rearrange("p (a o) -> p a o", a=8),
       g["W_outT"].rearrange("(a p) o -> p a o", p=128))
    ld(g["baseWT_sb"][:].rearrange("p (a o) -> p a o", a=8),
       g["baseWT"].rearrange("(a p) o -> p a o", p=128))
    ld(g["sharedAT_sb"][:].rearrange("p (a o) -> p a o", a=8),
       g["sharedAT"].rearrange("(a p) o -> p a o", p=128))
    ld(g["sharedBT_sb"][:], g["sharedBT"])
    ld(g["routerWT_sb"][:].rearrange("p (a o) -> p a o", a=8),
       g["routerWT"].rearrange("(a p) o -> p a o", p=128))
    ld(g["routedAT_sb"][:].rearrange("p (a o) -> p a o", a=8),
       g["routedAT"].rearrange("(a p) o -> p a o", p=128))
    ld(g["routedBT_sb"][:], g["routedBT"])
    ld(g["mtri_sb"][:, 0:64], g["mtri_aug"])
    ld(g["mtri_sb"][:, 64:128], g["mtri_tail"])
    ld(g["mtriD_sb"][:, 0:64], g["mtriD"])
    ld(g["mtriD_sb"][:, 64:128], g["mtriD_tail"])
    ld(g["ident_sb"][:], g["ident"])
    ld(g["onesrow_sb"][:], g["onesrow"])
    ld(g["onesrowb_sb"][:], g["onesrowb"])
    ld(g["convw_sb"][:].rearrange("p (a k) -> p a k", a=8),
       g["convw"].rearrange("(a p) k -> p a k", p=128))
    ld(g["convb_sb"][:], g["convb"])
    ld(g["wdtrep_sb"][0:64, :], g["W_dt_own"][:, 0:512].partition_broadcast(64))
    ld(g["wdtrep_sb"][64:128, :], g["W_dt_own"][:, 512:1024].partition_broadcast(64))
    ld(g["bdtrep_sb"][0:64, :], g["b_dt_own"][:, 0:512].partition_broadcast(64))
    ld(g["bdtrep_sb"][64:128, :], g["b_dt_own"][:, 512:1024].partition_broadcast(64))
    ld(g["drep_sb"][:], g["D_own"].partition_broadcast(128))
    ld(g["lnwrep_sb"][:], g["lnw_own"].partition_broadcast(128))
    ld(g["lnbrep_sb"][:], g["lnb_own"].partition_broadcast(128))
    ld(g["bbrep_sb"][:], g["base_b"])
    ld(g["rbrow_sb"][:], g["router_b"])
    ld(g["arow_sb"][:], g["A_log_in"])
    ld(g["slot_sb"][:], g["slot"])
    n_w = sch.cnt["ld_w"]

    sch.wait("scalar", "ld_w", n_w)
    n1 = sch.sop(lambda e: e.activation(
        out=g["acol_sb"][0:1, 0:S], in_=g["arow_sb"][:], func=ACT.Exp))
    n2 = sch.sop(lambda e: e.copy(
        out=g["arow_sb"][:], in_=g["acol_sb"][0:1, 0:S]))
    sch.wait("tensor", "ser_s", n2)
    sch.op("tensor", lambda e: e.matmul(
        g["ps_a"][:, 0:S], g["onesrow_sb"][:], g["arow_sb"][:],
        start=True, stop=True), "mm_d")
    sch.wait("vector", "mm_d", sch.cnt["mm_d"])
    sch.vop(lambda e: e.tensor_scalar_mul(
        out=g["acol_sb"][:, 0:S], in0=g["ps_a"][:, 0:S], scalar1=-1.0))
    sch.vop(lambda e: e.tensor_scalar_mul(
        out=g["acol_sb"][:, S:2 * S], in0=g["ps_a"][:, 0:S], scalar1=1.0))
    sch.vop(lambda e: e.memset(g["carrysave"][:], 0.0))
    sch.vop(lambda e: e.memset(g["epscol"][:], 1e-5))
    nz = sch.vop(lambda e: e.memset(g["xi2hsq"][:, 0:128], 0.0))
    sch.wait("tensor", "ser_v", nz)
    sch.wait("tensor", "ld_w", sch.cnt["ld_w"])

    def mk_zps(e):
        last = None
        for kb in range(8):
            last = e.transpose(g["ps_t1"][:, kb * 128:(kb + 1) * 128],
                               g["xi2hsq"][:, 0:128], g["ident_sb"][:])
        return last
    sch.op("tensor", mk_zps, "tp_xi")
    sch.vop(lambda e: e.memset(g["win"][:], 0.0))
    n_ms = sch.vop(lambda e: e.memset(g["statsbuf"][:], 0.0))
    sch.wait("gpsimd", "ser_v", n_ms)
    sch.dma("gpsimd", g["cbar_in"], g["statsbuf"][0:1, 0:16], "ld_w")
    sch.wait_all("ser_v", n_ms)
    sch.wait_all("ld_w", sch.cnt["ld_w"])


def emit_p1(sch, g, nc, debug_phase):
    sems = sch.sems
    r = {"mm_xz": [0], "silu_z": [0], "xi_cp": [0], "wincp": [0], "conv": [0],
         "silu2": [0], "ev_xd": [0], "ev_hs": [0], "sp": [(0, 0, 0, 0)], "dbg": [0],
         "spd": []}
    for m, (t0, ln) in enumerate(P1T):
        buf = m % 2
        sch.wait("gpsimd", "mm_xz", r["mm_xz"][max(0, m - 1)])
        sch.dma("gpsimd",
                g["xTb"][:, buf * 8 * TILE:(buf + 1) * 8 * TILE]
                .rearrange("p (a t) -> p a t", a=8)[:, :, 0:ln],
                g["xT"][:, t0:t0 + ln].rearrange("(a p) t -> p a t", p=128),
                "ld_x")
        n_ldx = sch.cnt["ld_x"]
        sch.wait("tensor", "ld_x", n_ldx)
        sch.wait("tensor", "ser_s", r["silu_z"][m])
        sch.wait("tensor", "ser_v", r["xi_cp"][m])

        def mk_xz(e, buf=buf, ln=ln):
            last = None
            for k in range(8):
                lhs = g["xTb"][:, buf * 8 * TILE + k * TILE: buf * 8 * TILE + k * TILE + ln]
                for oc in range(4):
                    last = e.matmul(
                        g["ps_a"][0:ln, oc * 512:(oc + 1) * 512], lhs,
                        g["WinT_sb"][:, k * 2048 + oc * 512: k * 2048 + (oc + 1) * 512],
                        start=(k == 0), stop=(k == 7))
            return last
        sch.op("tensor", mk_xz, "mm_xz")
        r["mm_xz"].append(sch.cnt["mm_xz"])
        n_mmxz = sch.cnt["mm_xz"]
        sch.wait("scalar", "mm_xz", n_mmxz)
        for i_ in range(4):
            sch.wait("scalar", f"sp{i_}", r["sp"][m][i_] if m > 0 else 0)
        if debug_phase == 1:
            sch.wait("scalar", "dbg_dma", r["dbg"][m])
        ns = sch.sop(lambda e, ln=ln: e.activation(
            out=g["szt"][0:ln, :], in_=g["ps_a"][0:ln, DOWN:2 * DOWN], func=ACT.Silu))
        r["silu_z"].append(ns)
        sch.wait("vector", "mm_xz", n_mmxz)
        nv = sch.vop(lambda e, ln=ln: e.tensor_copy(
            out=g["xiraw"][0:ln, :], in_=g["ps_a"][0:ln, 0:DOWN]))
        r["xi_cp"].append(nv)
        sch.wait("sync", "ser_s", ns)
        sch.dma("sync", g["sz_hbm"][t0:t0 + ln, :], g["szt"][0:ln, :], f"sp{m % 4}")
        if debug_phase == 1:
            sch.dma("sync", g["d_sz"][t0:t0 + ln, :], g["szt"][0:ln, :], "dbg_dma")
        sch.wait("tensor", "ser_v", nv)
        sch.wait("tensor", "ser_v", r["ev_hs"][m])

        def mk_tp(e, ln=ln):
            last = None
            for kb in range(8):
                last = e.transpose(g["ps_t1"][:, kb * 128: kb * 128 + ln],
                                   g["xiraw"][0:ln, kb * 128:(kb + 1) * 128],
                                   g["ident_sb"][0:ln, 0:ln])
            return last
        sch.op("tensor", mk_tp, "tp_xi")
        n_tpxi = sch.cnt["tp_xi"]
        sch.wait("vector", "tp_xi", n_tpxi)
        nw = sch.vop(lambda e, ln=ln: e.tensor_copy(
            out=g["win"][:].rearrange("p (a t) -> p a t", a=8)[:, :, 3:3 + ln],
            in_=g["ps_t1"][:].rearrange("p (a t) -> p a t", a=8)[:, :, 0:ln]))
        r["wincp"].append(nw)
        wv = g["win"][:].rearrange("p (a t) -> p a t", a=8)
        cwv = g["convw_sb"][:].rearrange("p (a k) -> p a k", a=8)

        def cv_mul0(e, ln=ln):
            return e.tensor_tensor(
                out=g["cvacc"][:].rearrange("p (a t) -> p a t", a=8)[:, :, 0:ln],
                in0=wv[:, :, 0:ln],
                in1=cwv[:, :, 0:1].broadcast_to([128, 8, ln]), op=ALU.mult)
        sch.vop(cv_mul0)
        for k in (1, 2, 3):
            def cv_mul(e, k=k, ln=ln):
                return e.tensor_tensor(
                    out=g["cvtmp"][:].rearrange("p (a t) -> p a t", a=8)[:, :, 0:ln],
                    in0=wv[:, :, k:k + ln],
                    in1=cwv[:, :, k:k + 1].broadcast_to([128, 8, ln]), op=ALU.mult)
            sch.vop(cv_mul)

            def cv_add(e, ln=ln):
                av = g["cvacc"][:].rearrange("p (a t) -> p a t", a=8)[:, :, 0:ln]
                return e.tensor_tensor(out=av, in0=av,
                                       in1=g["cvtmp"][:].rearrange("p (a t) -> p a t", a=8)[:, :, 0:ln],
                                       op=ALU.add)
            sch.vop(cv_add)
        nc_conv = sch.vop(lambda e, ln=ln: e.tensor_copy(
            out=g["win"][:].rearrange("p (a t) -> p a t", a=8)[:, :, 0:3],
            in_=wv[:, :, ln:ln + 3]))
        r["conv"].append(nc_conv)
        sch.wait("scalar", "ser_v", nc_conv)
        sch.wait("scalar", "mm_xd", m)
        sch.wait("scalar", "tp_hs", m)
        sch.wait("scalar", "sp_dma", r["spd"][m - 1] if m > 0 else 0)
        for i_ in range(4):
            sch.wait("scalar", f"sp{i_}", r["sp"][m][i_] if m > 0 else 0)

        def mk_silu(e, ln=ln):
            last = None
            for kb in range(8):
                last = e.activation(out=g["xi2tile"][:, kb * TILE: kb * TILE + ln],
                                    in_=g["cvacc"][:, kb * TILE: kb * TILE + ln],
                                    func=ACT.Silu, bias=g["convb_sb"][:, kb:kb + 1])
            return last
        ns2 = sch.sop(mk_silu)
        r["silu2"].append(ns2)
        sch.wait("tensor", "ser_s", ns2)
        sch.wait("tensor", "ser_v", r["ev_xd"][m])

        def mk_xd(e, ln=ln):
            last = None
            for kb in range(8):
                last = e.matmul(g["ps_xd"][:, 0:ln],
                                g["WxT_sb"][:, kb * 33:(kb + 1) * 33],
                                g["xi2tile"][:, kb * TILE: kb * TILE + ln],
                                start=(kb == 0), stop=(kb == 7))
            return last
        sch.op("tensor", mk_xd, "mm_xd")
        n_mmxd = sch.cnt["mm_xd"]
        sch.wait("vector", "mm_xd", n_mmxd)
        nxd = sch.vop(lambda e, t0=t0, ln=ln: e.tensor_copy(
            out=g["xdTbuf"][:, t0:t0 + ln], in_=g["ps_xd"][:, 0:ln]))
        r["ev_xd"].append(nxd)
        sch.wait("sync", "ser_s", ns2)
        sch.dma("sync",
                g["xi2t_hbm"][:, t0:t0 + ln].rearrange("(a p) t -> p a t", p=128),
                g["xi2tile"][:].rearrange("p (a t) -> p a t", a=8)[:, :, 0:ln],
                "sp_dma")
        if debug_phase == 1:
            sch.dma("sync",
                    g["d_xi2t"][:, t0:t0 + ln].rearrange("(a p) t -> p a t", p=128),
                    g["xi2tile"][:].rearrange("p (a t) -> p a t", a=8)[:, :, 0:ln],
                    "dbg_dma")
        nch = 2 if ln == TILE else 1
        cl = TC if ln == TILE else ln
        sch.wait("tensor", "mm_xd", n_mmxd)
        sch.wait("tensor", "ser_v", nw)

        def mk_hst(e, nch=nch, cl=cl):
            last = None
            for ci in range(nch):
                for h in range(2):
                    for db in range(4):
                        kb = h * 4 + db
                        last = e.transpose(
                            g["ps_t1"][64 * h: 64 * h + cl,
                                       ci * 512 + db * 128: ci * 512 + db * 128 + 128],
                            g["xi2tile"][:, kb * TILE + ci * TC: kb * TILE + ci * TC + cl],
                            g["ident_sb"][:])
            return last
        sch.op("tensor", mk_hst, "tp_hs")
        n_tph = sch.cnt["tp_hs"]
        sch.wait("vector", "tp_hs", n_tph)
        for i_ in range(4):
            sch.wait("vector", f"sp{i_}", r["sp"][m][i_] if m > 0 else 0)
        nhs = sch.vop(lambda e, nch=nch: e.tensor_copy(
            out=g["xi2hs_t"][:, 0:nch * 512], in_=g["ps_t1"][:, 0:nch * 512]))
        r["ev_hs"].append(nhs)
        sch.wait("sync", "ser_v", nhs)
        for ci in range(nch):
            sch.dma("sync", g["xi2hs_hbm"][2 * m + ci],
                    g["xi2hs_t"][:, ci * 512:(ci + 1) * 512], f"sp{m % 4}")
        r["sp"].append(tuple(sch.cnt[f"sp{i}"] for i in range(4)))
        r["spd"].append(sch.cnt["sp_dma"])
        r["dbg"].append(sch.cnt["dbg_dma"])
    if debug_phase == 1:
        nxd_all = r["ev_xd"][-1]
        sch.wait("sync", "ser_v", nxd_all)
        sch.dma("sync", g["d_xdT"], g["xdTbuf"][:], "dbg_dma")


def emit_x1(sch, g, nc, RG):
    sems = sch.sems
    n_evxd = sch.cnt["ev_xd"]
    sch.wait("sync", "ev_xd", n_evxd)
    if sch.on("sync"):
        myslot = nc.values_load(g["slot_sb"][0:1, 0:1], engines=[mybir.EngineType.SP],
                                min_val=0, max_val=1, skip_runtime_bounds_check=True)
        g["myslot_sync"] = myslot
        nc.sync.dma_start(out=g["shr_xd"][0:33, :], in_=g["xdTbuf"][:],
                          cond=(myslot == 0)).then_inc(sems["x1_dma"], 16)
        nc.sync.dma_start(out=g["shr_xd"][33:66, :], in_=g["xdTbuf"][:],
                          cond=myslot).then_inc(sems["x1_dma"], 16)
    sch.inc("x1_dma", 32)
    sch.wait("gpsimd", "x1_dma", sch.cnt["x1_dma"])
    sch.op("gpsimd", lambda e: e.collective_compute(
        "AllReduce", ALU.add, ins=[g["cbar_in"][:]], outs=[g["cbar_out"][:]],
        replica_groups=RG), "bar")
    sch.wait("sync", "bar", sch.cnt["bar"])
    if sch.on("sync"):
        myslot = g["myslot_sync"]
        for half, dst in ((0, g["em_sb"]), (1, g["ep_sb"])):
            nc.sync.dma_start(out=dst[0:33, 0:4096],
                              in_=g["shr_xd"][33:66, half * 4096:(half + 1) * 4096],
                              cond=(myslot == 0)).then_inc(sems["x1_dma"], 16)
            nc.sync.dma_start(out=dst[0:33, 0:4096],
                              in_=g["shr_xd"][0:33, half * 4096:(half + 1) * 4096],
                              cond=myslot).then_inc(sems["x1_dma"], 16)
    sch.inc("x1_dma", 64)
    sch.wait("vector", "x1_dma", sch.cnt["x1_dma"])
    sch.vop(lambda e: e.tensor_tensor(out=g["xdTbuf"][:, 0:4096], in0=g["xdTbuf"][:, 0:4096],
                                      in1=g["em_sb"][0:33, 0:4096], op=ALU.add))
    nadd = sch.vop(lambda e: e.tensor_tensor(out=g["xdTbuf"][:, 4096:8192],
                                             in0=g["xdTbuf"][:, 4096:8192],
                                             in1=g["ep_sb"][0:33, 0:4096], op=ALU.add))
    sch.wait_all("ser_v", nadd)


def emit_p2(sch, g, nc, debug_phase):
    sems = sch.sems
    for i_ in range(4):
        sch.wait("gpsimd", f"sp{i_}", sch.cnt[f"sp{i_}"])
    he_cnt = [0] * S
    slot_rd = {}
    wslot_g = {}
    psg_rd = [0, 0]
    em_war = [0, 0]
    last_xdev = [0]
    g_cnt = [0] * S
    q_dtx = [0, 0]
    q_mmd = [0, 0]
    q_eact = [0, 0]
    q_evq = [0, 0]
    q_ydma_prev = [[0, 0] for _ in range(QUAD)]
    ld_cnt = [[0] * QUAD, [0] * QUAD]

    def emit_loads(qi2):
        rot2 = qi2 % 2
        sch.wait("gpsimd", "ser_v", q_dtx[rot2])
        for ci2, c2 in enumerate(QUADS[qi2]):
            sch.dma("gpsimd",
                    g["xi2hsq"][:, (rot2 * QUAD + ci2) * 512:(rot2 * QUAD + ci2 + 1) * 512],
                    g["xi2hs_hbm"][c2], f"p2l{ci2}")
            ld_cnt[rot2][ci2] = sch.cnt[f"p2l{ci2}"]

    emit_loads(0)
    for qi, quad in enumerate(QUADS):
        rot = qi % 2
        qn = len(quad)
        for ci, c in enumerate(quad):
            t0, cl = CHUNKS[c]
            sch.wait("tensor", "ser_v", last_xdev[0])

            def mk_xdt(e, t0=t0, cl=cl):
                e.transpose(g["ps_t1"][0:cl, 0:33], g["xdTbuf"][:, t0:t0 + cl],
                            g["ident_sb"][0:33, 0:33])
                return e.transpose(g["ps_t1"][64:64 + cl, 0:33],
                                   g["xdTbuf"][:, t0:t0 + cl],
                                   g["ident_sb"][0:33, 0:33])
            sch.op("tensor", mk_xdt, "tp_c")
            sch.wait("vector", "tp_c", sch.cnt["tp_c"])
            last_xdev[0] = sch.vop(lambda e, ci=ci, rot=rot: e.tensor_copy(
                out=g["xdhs"][:, (rot * QUAD + ci) * 33:(rot * QUAD + ci + 1) * 33],
                in_=g["ps_t1"][:, 0:33]))
        sch.wait("vector", "mm_d", q_mmd[rot])
        dtx_last = 0
        for ci, c in enumerate(quad):
            slot = rot * QUAD + ci
            dsl = g["dthsq"][:, slot * 512:(slot + 1) * 512]
            dtin_col = g["xdhs"][:, slot * 33: slot * 33 + 1]
            n1 = sch.vop(lambda e, dsl=dsl, dtin_col=dtin_col: e.scalar_tensor_tensor(
                out=dsl, in0=g["wdtrep_sb"][:], scalar=dtin_col, in1=g["bdtrep_sb"][:],
                op0=ALU.mult, op1=ALU.add))
            sch.wait("scalar", "ser_v", n1)
            sch.sop(lambda e, dsl=dsl: e.activation(out=dsl, in_=dsl, func=ACT.Exp))
            n2 = sch.sop(lambda e, dsl=dsl: e.activation(out=dsl, in_=dsl,
                                                         func=ACT.Ln, bias=1.0))
            sch.wait("vector", "ser_s", n2)
            sch.vop(lambda e, dsl=dsl: e.tensor_scalar(
                out=dsl, in0=dsl, scalar1=1e-4, scalar2=10.0, op0=ALU.max, op1=ALU.min))
            sch.wait("vector", f"p2l{ci}", ld_cnt[rot][ci])
            dtx_last = sch.vop(lambda e, slot=slot, dsl=dsl: e.tensor_tensor(
                out=g["dtxhsq"][:, slot * 512:(slot + 1) * 512], in0=dsl,
                in1=g["xi2hsq"][:, slot * 512:(slot + 1) * 512], op=ALU.mult))
        q_dtx[rot] = dtx_last
        if qi + 1 < len(QUADS):
            emit_loads(qi + 1)
        sch.wait("tensor", "ser_v", dtx_last)
        sch.wait("tensor", "ser_v", q_evq[rot])
        sch.wait("tensor", "ser_v", q_evq[1 - rot])

        def mk_dmm(e, quad=quad, rot=rot):
            last = None
            for ci, c in enumerate(quad):
                tail = (CHUNKS[c][1] != TC)
                lhsc = g["mtriD_sb"][:, 64:128] if tail else g["mtriD_sb"][:, 0:64]
                for h in range(2):
                    last = e.matmul(
                        g["ps_a"][64 * h:64 * h + 64, ci * 512:(ci + 1) * 512],
                        lhsc[64 * h:64 * h + 64, :],
                        g["dthsq"][64 * h:64 * h + 64,
                                   (rot * QUAD + ci) * 512:(rot * QUAD + ci + 1) * 512],
                        start=True, stop=True, tile_position=(64 * h, 64 * h))
            return last
        sch.op("tensor", mk_dmm, "mm_d")
        q_mmd[rot] = sch.cnt["mm_d"]
        sch.wait("vector", "mm_d", q_mmd[rot])
        sch.wait("vector", "ser_s", q_eact[rot])
        nev = sch.vop(lambda e, rot=rot, qn=qn: e.tensor_copy(
            out=g["dsbq"][:, rot * 2048: rot * 2048 + qn * 512],
            in_=g["ps_a"][:, 0:qn * 512]))
        q_evq[rot] = nev
        for sp in range(0, S, 2):
            lanes = (sp, sp + 1)
            sch.wait("scalar", "ser_v", nev)
            sch.wait("scalar", "ser_v", max(em_war))
            n_eact = 0
            for li, s_ in enumerate(lanes):
                sch.sop(lambda e, li=li, s_=s_, rot=rot, qn=qn: e.activation(
                    out=g["em_sb"][:, li * 2048: li * 2048 + qn * 512],
                    in_=g["dsbq"][:, rot * 2048: rot * 2048 + qn * 512],
                    func=ACT.Exp, scale=g["acol_sb"][:, S + s_:S + s_ + 1]))
                n_eact = sch.sop(lambda e, li=li, s_=s_, rot=rot, qn=qn: e.activation(
                    out=g["ep_sb"][:, li * 2048: li * 2048 + qn * 512],
                    in_=g["dsbq"][:, rot * 2048: rot * 2048 + qn * 512],
                    func=ACT.Exp, scale=g["acol_sb"][:, s_:s_ + 1]))
            w_pipe = {s2: {} for s2 in lanes}
            def emit_pw(ci2):
                c2 = quad[ci2]
                slot2 = rot * QUAD + ci2
                for li2, s2 in enumerate(lanes):
                    sch.wait("vector", "ser_s", n_eact)
                    wslot_pre = li2 * 2 + quad[ci2] % 2
                    sch.wait("vector", "mm_g", wslot_g.get(wslot_pre, 0))
                    Bcol = g["xdhs"][:, slot2 * 33 + 1 + s2: slot2 * 33 + 2 + s2]
                    wslot = li2 * 2 + c2 % 2
                    sch.vop(lambda e, slot2=slot2, Bcol=Bcol: e.tensor_scalar(
                        out=g["ptmp"][:], in0=g["dtxhsq"][:, slot2 * 512:(slot2 + 1) * 512],
                        scalar1=Bcol, scalar2=10.0, op0=ALU.mult, op1=ALU.min))
                    nw2 = sch.vop(lambda e, li2=li2, ci2=ci2, wslot=wslot: e.scalar_tensor_tensor(
                        out=g["wts_sb"][:, wslot * 512:(wslot + 1) * 512], in0=g["ptmp"][:],
                        scalar=-10.0,
                        in1=g["em_sb"][:, li2 * 2048 + ci2 * 512: li2 * 2048 + (ci2 + 1) * 512],
                        op0=ALU.max, op1=ALU.mult))
                    w_pipe[s2][ci2] = nw2

            emit_pw(0)
            if qn > 1:
                emit_pw(1)
            for ci, c in enumerate(quad):
                t0, cl = CHUNKS[c]
                tail = (cl != TC)
                slot = rot * QUAD + ci
                qstart = (ci == 0)
                for li, s_ in enumerate(lanes):
                    wslot = li * 2 + c % 2
                    crs = f"cr{sp // 2}"
                    sch.wait("sync", "ser_v", w_pipe[s_][ci])
                    if qstart:
                        srcs = [g["carrysave"][2 * s_:2 * s_ + 1, :],
                                g["carrysave"][2 * s_ + 1:2 * s_ + 2, :]]
                    else:
                        pslot = li * 2 + (c - 1) % 2
                        pcl = CHUNKS[c - 1][1]
                        sch.wait("sync", "ser_v", he_cnt[s_])
                        het_p = g["hets_sb"][:, pslot * 512:(pslot + 1) * 512]
                        srcs = [het_p[pcl - 1:pcl, :], het_p[63 + pcl:64 + pcl, :]]
                    wt = g["wts_sb"][:, wslot * 512:(wslot + 1) * 512]
                    sch.dma("sync", wt[63:64, :], srcs[0], crs)
                    sch.dma("sync", wt[127:128, :], srcs[1], crs)
                    n_cr = sch.cnt[crs]
                    if not qstart:
                        slot_rd[pslot] = (crs, n_cr)
                    sch.wait("tensor", crs, n_cr)
                    sch.wait("tensor", "ser_v", max(he_cnt[s_], psg_rd[li]))
                    lhsg = g["mtri_sb"][:, 64:128] if tail else g["mtri_sb"][:, 0:64]

                    def mk_g(e, li=li, lhsg=lhsg, wt=wt):
                        last = None
                        for h in range(2):
                            last = e.matmul(
                                g["ps_g"][64 * h:64 * h + 64, li * 512:(li + 1) * 512],
                                lhsg[64 * h:64 * h + 64, :], wt[64 * h:64 * h + 64, :],
                                start=True, stop=True, tile_position=(64 * h, 64 * h))
                        return last
                    sch.op("tensor", mk_g, "mm_g")
                    g_cnt[s_] = sch.cnt["mm_g"]
                    wslot_g[wslot] = sch.cnt["mm_g"]
                for li, s_ in enumerate(lanes):
                    hslot = li * 2 + c % 2
                    het = g["hets_sb"][:, hslot * 512:(hslot + 1) * 512]
                    sch.wait("vector", "mm_g", g_cnt[s_])
                    sch.wait("vector", f"cr{sp // 2}", sch.cnt[f"cr{sp // 2}"])
                    if hslot in slot_rd:
                        sch.wait("vector", *slot_rd[hslot])
                    nhe = sch.vop(lambda e, het=het, li=li, ci=ci: e.tensor_tensor(
                        out=het,
                        in0=g["ep_sb"][:, li * 2048 + ci * 512: li * 2048 + (ci + 1) * 512],
                        in1=g["ps_g"][:, li * 512:(li + 1) * 512], op=ALU.mult))
                    he_cnt[s_] = nhe
                    psg_rd[li] = nhe
                    Ccol = g["xdhs"][:, slot * 33 + 17 + s_: slot * 33 + 18 + s_]
                    adst = g["accq"][:, ci * 512:(ci + 1) * 512]
                    if s_ == 0:
                        sch.wait("vector", f"y{ci * 2}", q_ydma_prev[ci][0])
                        sch.wait("vector", f"y{ci * 2 + 1}", q_ydma_prev[ci][1])
                        if debug_phase == 2:
                            sch.wait("vector", "dbg2_dma", sch.cnt["dbg2_dma"])
                        sch.vop(lambda e, adst=adst, het=het, Ccol=Ccol: e.tensor_scalar(
                            out=adst, in0=het, scalar1=Ccol, scalar2=None, op0=ALU.mult))
                    else:
                        sch.vop(lambda e, adst=adst, het=het, Ccol=Ccol:
                                e.scalar_tensor_tensor(
                                    out=adst, in0=het, scalar=Ccol, in1=adst,
                                    op0=ALU.mult, op1=ALU.add))
                    if ci == qn - 1 and c != NCHUNK - 1:
                        crs = f"cr{sp // 2}"
                        sch.wait("sync", "ser_v", nhe)
                        sch.dma("sync", g["carrysave"][2 * s_:2 * s_ + 1, :],
                                het[cl - 1:cl, :], crs)
                        sch.dma("sync", g["carrysave"][2 * s_ + 1:2 * s_ + 2, :],
                                het[63 + cl:64 + cl, :], crs)
                        slot_rd[hslot] = (crs, sch.cnt[crs])
                if ci + 2 < qn:
                    emit_pw(ci + 2)
            em_war[0] = he_cnt[sp]
            em_war[1] = he_cnt[sp + 1]
            q_eact[rot] = n_eact
        nacc_fin = sch.cnt["ser_v"]
        sch.wait("gpsimd", "ser_v", nacc_fin)
        for ci, c in enumerate(quad):
            t0, cl = CHUNKS[c]
            acc_fin = g["accq"][:, ci * 512:(ci + 1) * 512]
            sch.dma("gpsimd", g["y_hbm"][t0:t0 + cl, 0:512], acc_fin[0:cl, :],
                    f"y{ci * 2}")
            sch.dma("gpsimd", g["y_hbm"][t0:t0 + cl, 512:1024], acc_fin[64:64 + cl, :],
                    f"y{ci * 2 + 1}")
            q_ydma_prev[ci] = [sch.cnt[f"y{ci * 2}"], sch.cnt[f"y{ci * 2 + 1}"]]
            if debug_phase == 2:
                sch.dma("gpsimd", g["d_y"][t0:t0 + cl, 0:512], acc_fin[0:cl, :], "dbg2_dma")
                sch.dma("gpsimd", g["d_y"][t0:t0 + cl, 512:1024], acc_fin[64:64 + cl, :], "dbg2_dma")


def emit_p3(sch, g, nc, RG, debug_phase):
    sems = sch.sems
    for i_ in range(8):
        sch.wait("gpsimd", f"y{i_}", sch.cnt[f"y{i_}"])
    for e_ in ("gpsimd", "sync"):
        sch.wait(e_, "sp_dma", sch.cnt["sp_dma"])
        for i_ in range(4):
            sch.wait(e_, f"sp{i_}", sch.cnt[f"sp{i_}"])
    sch.wait("scalar", "mm_d", sch.cnt["mm_d"])
    st_tile = []
    for mt, (t0, ln) in enumerate(T128T):
        sch.wait("gpsimd", "ser_v", st_tile[-1] if st_tile else 0)
        sch.dma("gpsimd", g["y_ld"][0:ln, :], g["y_hbm"][t0:t0 + ln, :], "p3_ld")
        n_ld = sch.cnt["p3_ld"]
        sch.wait("vector", "p3_ld", n_ld)
        sch.vop(lambda e, mt=mt, ln=ln: e.reduce_sum(
            out=g["statsbuf"][0:ln, 2 * mt:2 * mt + 1], in_=g["y_ld"][0:ln, :],
            axis=mybir.AxisListType.X))
        sch.wait("scalar", "p3_ld", n_ld)
        nsq = sch.sop(lambda e, ln=ln: e.activation(
            out=g["x1_sb"][0:ln, :], in_=g["y_ld"][0:ln, :], func=ACT.Square))
        sch.wait("vector", "ser_s", nsq)
        nst = sch.vop(lambda e, mt=mt, ln=ln: e.reduce_sum(
            out=g["statsbuf"][0:ln, 2 * mt + 1:2 * mt + 2], in_=g["x1_sb"][0:ln, :],
            axis=mybir.AxisListType.X))
        st_tile.append(nst)
    sch.wait("sync", "ser_v", st_tile[-1])
    if sch.on("sync"):
        myslot = nc.values_load(g["slot_sb"][0:1, 0:1], engines=[mybir.EngineType.SP],
                                min_val=0, max_val=1, skip_runtime_bounds_check=True)
        g["myslot_sync"] = myslot
        nc.sync.dma_start(out=g["shr_st"][0:128, :], in_=g["statsbuf"][:, 0:128],
                          cond=(myslot == 0)).then_inc(sems["x2_dma"], 16)
        nc.sync.dma_start(out=g["shr_st"][128:256, :], in_=g["statsbuf"][:, 0:128],
                          cond=myslot).then_inc(sems["x2_dma"], 16)
    sch.inc("x2_dma", 32)
    sch.wait("gpsimd", "x2_dma", sch.cnt["x2_dma"])
    sch.op("gpsimd", lambda e: e.collective_compute(
        "AllReduce", ALU.add, ins=[g["cbar_in"][:]], outs=[g["cbar_out"][:]],
        replica_groups=RG), "bar")
    sch.wait("sync", "bar", sch.cnt["bar"])
    if sch.on("sync"):
        myslot = g["myslot_sync"]
        nc.sync.dma_start(out=g["statspeer"][:], in_=g["shr_st"][128:256, :],
                          cond=(myslot == 0)).then_inc(sems["x2_dma"], 16)
        nc.sync.dma_start(out=g["statspeer"][:], in_=g["shr_st"][0:128, :],
                          cond=myslot).then_inc(sems["x2_dma"], 16)
    sch.inc("x2_dma", 32)
    sch.wait("vector", "x2_dma", sch.cnt["x2_dma"])
    sv = g["statsbuf"][:, 0:128].rearrange("p (m two) -> p m two", two=2)
    sch.vop(lambda e: e.tensor_tensor(out=g["statsbuf"][:, 0:128],
                                      in0=g["statsbuf"][:, 0:128],
                                      in1=g["statspeer"][:, 0:128], op=ALU.add))
    sch.vop(lambda e: e.tensor_scalar_mul(
        out=g["statsbuf"][:, 128:192], in0=sv[:, :, 0], scalar1=1.0 / D_INNER))
    sch.vop(lambda e: e.tensor_scalar_mul(
        out=g["statspeer"][:, 0:64], in0=sv[:, :, 1], scalar1=1.0 / D_INNER))
    sch.vop(lambda e: e.tensor_tensor(out=g["statspeer"][:, 64:128],
                                      in0=g["statsbuf"][:, 128:192],
                                      in1=g["statsbuf"][:, 128:192], op=ALU.mult))
    nvar = sch.vop(lambda e: e.tensor_tensor(out=g["statspeer"][:, 0:64],
                                             in0=g["statspeer"][:, 0:64],
                                             in1=g["statspeer"][:, 64:128], op=ALU.subtract))
    sch.wait("scalar", "ser_v", nvar)
    nsq2 = sch.sop(lambda e: e.activation(out=g["statspeer"][:, 64:128],
                                          in_=g["statspeer"][:, 0:64],
                                          func=ACT.Sqrt, bias=g["epscol"][:]))
    sch.wait("vector", "ser_s", nsq2)
    nrst = sch.vop(lambda e: e.reciprocal(out=g["statsbuf"][:, 0:64],
                                          in_=g["statspeer"][:, 64:128]))
    gt_done = []
    ogT_prev = [0]
    for mt, (t0, ln) in enumerate(T128T):
        sch.wait("gpsimd", "ser_v", gt_done[-1] if gt_done else nrst)
        sch.dma("gpsimd", g["y_ld"][0:ln, :], g["y_hbm"][t0:t0 + ln, :], "p3_ld")
        sch.dma("gpsimd", g["sz_ld"][0:ln, :], g["sz_hbm"][t0:t0 + ln, :], "p3_ld")
        sch.wait("sync", "ser_v", gt_done[-1] if gt_done else nrst)
        sch.wait("sync", "p3t_ld", sch.cnt["p3t_ld"])
        if sch.on("sync"):
            for kb in range(8):
                sch.eng.dma_start_transpose(
                    g["xi2_ld"][0:ln, kb * 128:(kb + 1) * 128],
                    g["xi2t_hbm"][kb * 128:(kb + 1) * 128, t0:t0 + ln]).then_inc(sems["p3t_ld"], 16)
        sch.inc("p3t_ld", 8 * 16)
        n_ld = sch.cnt["p3_ld"]
        sch.wait("vector", "p3_ld", n_ld)
        sch.wait("vector", "p3t_ld", sch.cnt["p3t_ld"])
        sch.wait("vector", "ser_v", nrst)
        mu_c = g["statsbuf"][:, 128 + mt:129 + mt]
        rs_c = g["statsbuf"][:, mt:mt + 1]
        sch.vop(lambda e, ln=ln, mu_c=mu_c, rs_c=rs_c: e.tensor_scalar(
            out=g["gt1"][0:ln, :], in0=g["y_ld"][0:ln, :], scalar1=mu_c[0:ln, :],
            scalar2=rs_c[0:ln, :], op0=ALU.subtract, op1=ALU.mult))
        sch.vop(lambda e, ln=ln: e.tensor_tensor(
            out=g["gt1"][0:ln, :], in0=g["gt1"][0:ln, :], in1=g["lnwrep_sb"][0:ln, :],
            op=ALU.mult))
        sch.vop(lambda e, ln=ln: e.tensor_tensor(
            out=g["gt2"][0:ln, :], in0=g["xi2_ld"][0:ln, :], in1=g["drep_sb"][0:ln, :],
            op=ALU.mult))
        sch.vop(lambda e, ln=ln: e.tensor_tensor(
            out=g["gt1"][0:ln, :], in0=g["gt1"][0:ln, :], in1=g["gt2"][0:ln, :],
            op=ALU.add))
        sch.vop(lambda e, ln=ln: e.tensor_tensor(
            out=g["gt1"][0:ln, :], in0=g["gt1"][0:ln, :], in1=g["lnbrep_sb"][0:ln, :],
            op=ALU.add))
        nog = sch.vop(lambda e, ln=ln: e.tensor_tensor(
            out=g["og_sb"][0:ln, :], in0=g["gt1"][0:ln, :], in1=g["sz_ld"][0:ln, :],
            op=ALU.mult))
        sch.wait("tensor", "ser_v", nog)
        sch.wait("tensor", "ser_v", ogT_prev[0])

        def mk_ogt(e, ln=ln):
            last = None
            for kb in range(8):
                last = e.transpose(g["ps_t1"][:, kb * 128: kb * 128 + ln],
                                   g["og_sb"][0:ln, kb * 128:(kb + 1) * 128],
                                   g["ident_sb"][0:ln, 0:ln])
            return last
        sch.op("tensor", mk_ogt, "tp_og")
        sch.wait("vector", "tp_og", sch.cnt["tp_og"])
        nevog = sch.vop(lambda e: e.tensor_copy(out=g["ogT_sb"][:], in_=g["ps_t1"][:]))
        ogT_prev[0] = nevog
        sch.wait("tensor", "ser_v", nevog)
        sch.wait("tensor", "ser_v", sch.cnt["ser_v"])

        def mk_wo(e, ln=ln):
            last = None
            for kb in range(8):
                for oc in range(2):
                    last = e.matmul(
                        g["ps_a"][0:ln, oc * 512:(oc + 1) * 512],
                        g["ogT_sb"][:, kb * 128:(kb + 1) * 128][:, 0:ln],
                        g["WoutT_sb"][:, kb * 1024 + oc * 512: kb * 1024 + (oc + 1) * 512],
                        start=(kb == 0), stop=(kb == 7))
            return last
        sch.op("tensor", mk_wo, "mm_wo")
        n_wo = sch.cnt["mm_wo"]
        sch.wait("vector", "mm_wo", n_wo)
        sch.wait("vector", "x3_dma", len(gt_done) * 16)
        nop_ = sch.vop(lambda e, ln=ln: e.tensor_copy(
            out=g["op_sb"][0:ln, :], in_=g["ps_a"][0:ln, 0:1024]))
        gt_done.append(nop_)
        sch.wait("sync", "ser_v", nop_)
        sch.dma("sync", g["myout_hbm"][t0:t0 + ln, :], g["op_sb"][0:ln, :], "x3_dma")
    sch.wait("sync", "x3_dma", sch.cnt["x3_dma"])
    if sch.on("sync"):
        myslot = g["myslot_sync"]
        nc.sync.dma_start(out=g["shr_out"][0:L, :], in_=g["myout_hbm"][:, :],
                          cond=(myslot == 0)).then_inc(sems["x3_dma"], 16)
        nc.sync.dma_start(out=g["shr_out"][L:2 * L, :], in_=g["myout_hbm"][:, :],
                          cond=myslot).then_inc(sems["x3_dma"], 16)
    sch.inc("x3_dma", 32)
    sch.wait("gpsimd", "x3_dma", sch.cnt["x3_dma"])
    sch.op("gpsimd", lambda e: e.collective_compute(
        "AllReduce", ALU.add, ins=[g["cbar_in"][:]], outs=[g["cbar_out"][:]],
        replica_groups=RG), "bar")


def emit_p4(sch, g, nc, RG):
    sems = sch.sems
    nbar = sch.cnt["bar"]
    sch.wait_all("bar", nbar)
    AX = mybir.AxisListType.X
    if sch.on("sync"):
        myslot = g["myslot_sync"]
        for base, dst in ((0, g["loc0_hbm"]), (L, g["loc1_hbm"])):
            nc.sync.dma_start(out=dst[:, :], in_=g["shr_out"][base:base + 4096, :],
                              cond=(myslot == 0)).then_inc(sems["p4_ld"], 16)
            nc.sync.dma_start(out=dst[:, :], in_=g["shr_out"][base + 4096:base + 8192, :],
                              cond=myslot).then_inc(sems["p4_ld"], 16)
    sch.inc("p4_ld", 64)
    prev = [0]
    prevT = [0]
    for mt in range(N_P4):
        t0 = mt * 128
        sch.wait("sync", "ser_v", prev[0])
        sch.dma("sync", g["p4a"][:, 0:1024], g["loc0_hbm"][t0:t0 + 128, :], "p4_ld")
        sch.dma("sync", g["p4b"][:, 0:1024], g["loc1_hbm"][t0:t0 + 128, :], "p4_ld")
        sch.dma("gpsimd", g["xrow_sb"], g["x_rows"][t0:t0 + 128, :], "p4g_ld")
        n_ld = sch.cnt["p4_ld"]
        sch.wait("vector", "p4_ld", n_ld)
        sch.wait("vector", "p4g_ld", sch.cnt["p4g_ld"])
        sch.vop(lambda e: e.tensor_tensor(out=g["p4a"][:, 0:1024], in0=g["p4a"][:, 0:1024],
                                          in1=g["p4b"][:, 0:1024], op=ALU.add))
        nx1 = sch.vop(lambda e: e.tensor_tensor(out=g["x1_sb"], in0=g["xrow_sb"],
                                                in1=g["p4a"][:, 0:1024], op=ALU.add))
        sch.vop(lambda e: e.reduce_sum(out=g["sc1"][:, 0:1], in_=g["x1_sb"], axis=AX))
        sch.wait("scalar", "ser_v", nx1)
        sch.wait("scalar", "out_dma", mt * 16)
        nsq = sch.sop(lambda e: e.activation(out=g["x2_sb"], in_=g["x1_sb"], func=ACT.Square))
        sch.wait("vector", "ser_s", nsq)
        sch.vop(lambda e: e.reduce_sum(out=g["sc1"][:, 1:2], in_=g["x2_sb"], axis=AX))
        sch.vop(lambda e: e.tensor_scalar_mul(out=g["sc1"][:, 2:3], in0=g["sc1"][:, 0:1],
                                              scalar1=1.0 / D_MODEL))
        sch.vop(lambda e: e.tensor_scalar_mul(out=g["sc1"][:, 3:4], in0=g["sc1"][:, 1:2],
                                              scalar1=1.0 / D_MODEL))
        sch.vop(lambda e: e.tensor_tensor(out=g["sc1"][:, 4:5], in0=g["sc1"][:, 2:3],
                                          in1=g["sc1"][:, 2:3], op=ALU.mult))
        nvar = sch.vop(lambda e: e.tensor_tensor(out=g["sc1"][:, 3:4], in0=g["sc1"][:, 3:4],
                                                 in1=g["sc1"][:, 4:5], op=ALU.subtract))
        sch.wait("scalar", "ser_v", nvar)
        nsd = sch.sop(lambda e: e.activation(out=g["sc1"][:, 4:5], in_=g["sc1"][:, 3:4],
                                             func=ACT.Sqrt, bias=g["epscol"][:]))
        sch.wait("vector", "ser_s", nsd)
        sch.vop(lambda e: e.reciprocal(out=g["sc1"][:, 5:6], in_=g["sc1"][:, 4:5]))
        nx1n = sch.vop(lambda e: e.tensor_scalar(
            out=g["x1n_sb"][:, 0:1024], in0=g["x1_sb"], scalar1=g["sc1"][:, 2:3],
            scalar2=g["sc1"][:, 5:6], op0=ALU.subtract, op1=ALU.mult))
        sch.wait("tensor", "ser_v", nx1n)
        sch.wait("tensor", "ser_v", prevT[0])

        def mk_xt(e):
            last = None
            for kb in range(8):
                last = e.transpose(g["ps_t1"][:, kb * 128:(kb + 1) * 128],
                                   g["x1n_sb"][:, kb * 128:(kb + 1) * 128][0:128, :],
                                   g["ident_sb"][:])
            return last
        sch.op("tensor", mk_xt, "p4_tp")
        sch.wait("vector", "p4_tp", sch.cnt["p4_tp"])
        nxT = sch.vop(lambda e: e.tensor_copy(out=g["x1nT_sb"][:, 0:1024], in_=g["ps_t1"][:]))
        sch.wait("tensor", "ser_v", nxT)

        def mk_ffn1(e):
            last = None
            for kb in range(8):
                for oc in range(2):
                    last = e.matmul(
                        g["ps_a"][:, oc * 512:(oc + 1) * 512],
                        g["x1nT_sb"][:, kb * 128:(kb + 1) * 128],
                        g["baseWT_sb"][:, kb * 1024 + oc * 512: kb * 1024 + (oc + 1) * 512],
                        start=(kb == 0), stop=False)
            for oc in range(2):
                last = e.matmul(g["ps_a"][:, oc * 512:(oc + 1) * 512],
                                g["onesrowb_sb"][:],
                                g["bbrep_sb"][:, oc * 512:(oc + 1) * 512],
                                start=False, stop=False)
            for kb in range(8):
                last = e.matmul(g["ps_g"][0:16, 0:128],
                                g["sharedAT_sb"][:, kb * 16:(kb + 1) * 16],
                                g["x1nT_sb"][:, kb * 128:(kb + 1) * 128],
                                start=(kb == 0), stop=(kb == 7))
            for kb in range(8):
                last = e.matmul(g["ps_g"][:, 512:520],
                                g["x1nT_sb"][:, kb * 128:(kb + 1) * 128],
                                g["routerWT_sb"][:, kb * 8:(kb + 1) * 8],
                                start=(kb == 0), stop=False)
            last = e.matmul(g["ps_g"][:, 512:520], g["onesrowb_sb"][:],
                            g["rbrow_sb"][:], start=False, stop=True)
            for kb in range(8):
                last = e.matmul(g["ps_g"][:, 640:736],
                                g["x1nT_sb"][:, kb * 128:(kb + 1) * 128],
                                g["routedAT_sb"][:, kb * 96:(kb + 1) * 96],
                                start=(kb == 0), stop=(kb == 7))
            return last
        sch.op("tensor", mk_ffn1, "p4_mm")
        nmm1 = sch.cnt["p4_mm"]
        sch.wait("vector", "p4_mm", nmm1)
        sch.vop(lambda e: e.tensor_copy(out=g["hshT_sb"][:], in_=g["ps_g"][0:16, 0:128]))
        sch.vop(lambda e: e.tensor_copy(out=g["r6"][:, 0:8], in_=g["ps_g"][:, 512:520]))
        sch.vop(lambda e: e.reduce_max(out=g["r6"][:, 8:9], in_=g["r6"][:, 0:8], axis=AX))
        sch.vop(lambda e: e.tensor_scalar_mul(out=g["r6"][:, 9:10], in0=g["r6"][:, 8:9],
                                              scalar1=-1.0))
        nneg = sch.cnt["ser_v"]
        sch.wait("scalar", "ser_v", nneg)
        nexp = sch.sop(lambda e: e.activation(out=g["r6"][:, 16:24], in_=g["r6"][:, 0:8],
                                              func=ACT.Exp, bias=g["r6"][:, 9:10]))
        sch.wait("vector", "ser_s", nexp)
        sch.vop(lambda e: e.reduce_sum(out=g["r6"][:, 10:11], in_=g["r6"][:, 16:24], axis=AX))
        sch.vop(lambda e: e.reduce_max(out=g["r6"][:, 11:12], in_=g["r6"][:, 16:24], axis=AX))
        sch.vop(lambda e: e.tensor_scalar(out=g["r6"][:, 24:32], in0=g["r6"][:, 16:24],
                                          scalar1=g["r6"][:, 11:12], scalar2=None,
                                          op0=ALU.is_equal))
        sch.vop(lambda e: e.scalar_tensor_tensor(out=g["r6"][:, 24:32], in0=g["r6"][:, 24:32],
                                                 scalar=-1e30, in1=g["r6"][:, 16:24],
                                                 op0=ALU.mult, op1=ALU.add))
        sch.vop(lambda e: e.reduce_max(out=g["r6"][:, 12:13], in_=g["r6"][:, 24:32], axis=AX))
        sch.vop(lambda e: e.tensor_scalar_mul(out=g["r6"][:, 13:14], in0=g["r6"][:, 10:11],
                                              scalar1=1e-6))
        sch.vop(lambda e: e.tensor_tensor(out=g["r6"][:, 14:15], in0=g["r6"][:, 11:12],
                                          in1=g["r6"][:, 12:13], op=ALU.add))
        nden = sch.vop(lambda e: e.tensor_tensor(out=g["r6"][:, 14:15], in0=g["r6"][:, 14:15],
                                                 in1=g["r6"][:, 13:14], op=ALU.add))
        sch.vop(lambda e: e.reciprocal(out=g["r6"][:, 15:16], in_=g["r6"][:, 14:15]))
        sch.vop(lambda e: e.tensor_scalar(out=g["r6"][:, 32:40], in0=g["r6"][:, 16:24],
                                          scalar1=g["r6"][:, 12:13], scalar2=None,
                                          op0=ALU.is_ge))
        nwts = sch.vop(lambda e: e.scalar_tensor_tensor(
            out=g["r6"][:, 40:48], in0=g["r6"][:, 16:24], scalar=g["r6"][:, 15:16],
            in1=g["r6"][:, 32:40], op0=ALU.mult, op1=ALU.mult))
        nhrw = sch.vop(lambda e: e.tensor_tensor(
            out=g["hrw_sb"][:].rearrange("p (x r) -> p x r", x=6),
            in0=g["ps_g"][:, 640:736].rearrange("p (x r) -> p x r", x=6),
            in1=g["r6"][:, 40:46].unsqueeze(2).broadcast_to([128, 6, 16]), op=ALU.mult))
        sch.wait("tensor", "ser_v", nhrw)

        def mk_hrt(e):
            return e.transpose(g["ps_t1"][0:96, 0:128], g["hrw_sb"][:], g["ident_sb"][:])
        sch.op("tensor", mk_hrt, "p4_tp")
        sch.wait("vector", "p4_tp", sch.cnt["p4_tp"])
        nhrT = sch.vop(lambda e: e.tensor_copy(out=g["hrwT_sb"][:], in_=g["ps_t1"][0:96, 0:128]))
        prevT[0] = nhrT
        sch.wait("tensor", "ser_v", nhrT)

        def mk_ffn2(e):
            last = None
            for oc in range(2):
                last = e.matmul(g["ps_a"][:, oc * 512:(oc + 1) * 512],
                                g["hshT_sb"][:],
                                g["sharedBT_sb"][:, oc * 512:(oc + 1) * 512],
                                start=False, stop=False)
            for oc in range(2):
                last = e.matmul(g["ps_a"][:, oc * 512:(oc + 1) * 512],
                                g["hrwT_sb"][:],
                                g["routedBT_sb"][:, oc * 512:(oc + 1) * 512],
                                start=False, stop=True)
            return last
        sch.op("tensor", mk_ffn2, "p4_mm")
        nmm2 = sch.cnt["p4_mm"]
        sch.wait("vector", "p4_mm", nmm2)
        sch.wait("vector", "out_dma", mt * 16)
        nx2 = sch.vop(lambda e: e.tensor_tensor(out=g["x2_sb"], in0=g["x1n_sb"][:, 0:1024],
                                                in1=g["ps_a"][:, 0:1024], op=ALU.add))
        sch.vop(lambda e: e.reduce_sum(out=g["sc1"][:, 0:1], in_=g["x2_sb"], axis=AX))
        sch.wait("scalar", "ser_v", nx2)
        nsq2 = sch.sop(lambda e: e.activation(out=g["x1_sb"], in_=g["x2_sb"], func=ACT.Square))
        sch.wait("vector", "ser_s", nsq2)
        sch.vop(lambda e: e.reduce_sum(out=g["sc1"][:, 1:2], in_=g["x1_sb"], axis=AX))
        sch.vop(lambda e: e.tensor_scalar_mul(out=g["sc1"][:, 2:3], in0=g["sc1"][:, 0:1],
                                              scalar1=1.0 / D_MODEL))
        sch.vop(lambda e: e.tensor_scalar_mul(out=g["sc1"][:, 3:4], in0=g["sc1"][:, 1:2],
                                              scalar1=1.0 / D_MODEL))
        sch.vop(lambda e: e.tensor_tensor(out=g["sc1"][:, 4:5], in0=g["sc1"][:, 2:3],
                                          in1=g["sc1"][:, 2:3], op=ALU.mult))
        nvar2 = sch.vop(lambda e: e.tensor_tensor(out=g["sc1"][:, 3:4], in0=g["sc1"][:, 3:4],
                                                  in1=g["sc1"][:, 4:5], op=ALU.subtract))
        sch.wait("scalar", "ser_v", nvar2)
        nsd2 = sch.sop(lambda e: e.activation(out=g["sc1"][:, 4:5], in_=g["sc1"][:, 3:4],
                                              func=ACT.Sqrt, bias=g["epscol"][:]))
        sch.wait("vector", "ser_s", nsd2)
        sch.vop(lambda e: e.reciprocal(out=g["sc1"][:, 5:6], in_=g["sc1"][:, 4:5]))
        nout = sch.vop(lambda e: e.tensor_scalar(
            out=g["x2_sb"], in0=g["x2_sb"], scalar1=g["sc1"][:, 2:3],
            scalar2=g["sc1"][:, 5:6], op0=ALU.subtract, op1=ALU.mult))
        prev[0] = nout
        sch.wait("gpsimd", "ser_v", nout)
        sch.dma("gpsimd", g["out"][t0:t0 + 128, :], g["x2_sb"][:], "out_dma")


def emit_tail(sch, g, debug_phase):
    for e in Sched.ENGINES:
        for i_ in range(4):
            sch.wait(e, f"sp{i_}", sch.cnt[f"sp{i_}"])
        for i_ in range(8):
            sch.wait(e, f"cr{i_}", sch.cnt[f"cr{i_}"])
        if debug_phase in (1, 2):
            sch.wait(e, "dbg_dma", sch.cnt["dbg_dma"])
            sch.wait(e, "dbg2_dma", sch.cnt["dbg2_dma"])
        if debug_phase >= 2:
            for i_ in range(8):
                sch.wait(e, f"y{i_}", sch.cnt[f"y{i_}"])
        if debug_phase >= 4:
            sch.wait(e, "out_dma", sch.cnt["out_dma"])



def _consts():
    tri = np.tril(np.ones((63, 63), np.float32))
    mtri_aug = np.zeros((64, 64), np.float32)
    mtri_aug[0:63, 0:63] = tri.T
    mtri_aug[63, 0:63] = 1.0
    mtri_tail = np.zeros((64, 64), np.float32)
    mtri_tail[0:2, 0:2] = np.tril(np.ones((2, 2), np.float32)).T
    mtri_tail[63, 0:2] = 1.0
    mtriD = np.zeros((64, 64), np.float32)
    mtriD[0:63, 0:63] = tri.T
    mtriD_tail = np.zeros((64, 64), np.float32)
    mtriD_tail[0:2, 0:2] = np.tril(np.ones((2, 2), np.float32)).T
    return tuple(np.vstack([m, m]) for m in (mtri_aug, mtri_tail, mtriD, mtriD_tail))


def make_inputs(core, kw):
    b, j = core // 2, core % 2
    own = slice(j * DOWN, (j + 1) * DOWN)
    x = np.asarray(kw["L0_attn_x"])
    W_in = np.asarray(kw["L0_attn_W_in"])
    mtri_aug, mtri_tail, mtriD, mtriD_tail = _consts()
    rows = np.concatenate([W_in[j * DOWN:(j + 1) * DOWN],
                           W_in[D_INNER + j * DOWN: D_INNER + (j + 1) * DOWN]], 0)
    routedAT = np.ascontiguousarray(
        np.asarray(kw["L0_ffn_routed_A"]).transpose(2, 0, 1).reshape(D_MODEL, 96))
    routedBT = np.ascontiguousarray(
        np.asarray(kw["L0_ffn_routed_B"]).transpose(0, 2, 1).reshape(96, D_MODEL))
    routerWT = np.zeros((D_MODEL, 8), np.float32)
    routerWT[:, 0:6] = np.asarray(kw["L0_ffn_router_W"]).T
    router_b = np.full((1, 8), -1e30, np.float32)
    router_b[0, 0:6] = np.asarray(kw["L0_ffn_router_b"])
    f = np.float32

    def c(a):
        return np.ascontiguousarray(a).astype(f)
    return {
        "xT": c(x[b].T),
        "x_rows": c(x[b, j * (L // 2):(j + 1) * (L // 2)]),
        "W_inT": c(rows.T),
        "convw": c(np.asarray(kw["L0_attn_conv_w"])[own, 0, :]),
        "convb": c(np.asarray(kw["L0_attn_conv_b"])[own].reshape(8, 128).T),
        "W_xT": c(np.asarray(kw["L0_attn_W_x"])[:, own].T),
        "W_dt_own": c(np.asarray(kw["L0_attn_W_dt"])[None, own, 0]),
        "b_dt_own": c(np.asarray(kw["L0_attn_b_dt"])[None, own]),
        "A_log_in": c(np.asarray(kw["L0_attn_A_log"])[None, :]),
        "D_own": c(np.asarray(kw["L0_attn_D_param"])[None, own]),
        "lnw_own": c(np.asarray(kw["L0_attn_ln_w"])[None, own]),
        "lnb_own": c(np.asarray(kw["L0_attn_ln_b"])[None, own]),
        "W_outT": c(np.asarray(kw["L0_attn_W_out"])[:, own].T),
        "baseWT": c(np.asarray(kw["L0_ffn_base_W"]).T),
        "base_b": c(np.asarray(kw["L0_ffn_base_b"])[None, :]),
        "sharedAT": c(np.asarray(kw["L0_ffn_shared_A"]).T),
        "sharedBT": c(np.asarray(kw["L0_ffn_shared_B"]).T),
        "routerWT": routerWT,
        "router_b": router_b,
        "routedAT": c(routedAT),
        "routedBT": c(routedBT),
        "mtri_aug": mtri_aug.astype(BF),
        "mtri_tail": mtri_tail.astype(BF),
        "mtriD": mtriD,
        "mtriD_tail": mtriD_tail,
        "ident": np.eye(128, dtype=f).astype(BF),
        "onesrow": np.ones((1, 128), f),
        "onesrowb": np.ones((1, 128), f).astype(BF),
        "slot": np.array([[j]], np.int32),
    }


def kernel(**kw):
    nc = build_program()
    in_maps = [make_inputs(c, kw) for c in range(8)]
    res = run_bass_kernel_spmd(nc, in_maps, core_ids=list(range(8)))
    out = np.empty((NB, L, D_MODEL), np.float32)
    for c in range(8):
        b, j = c // 2, c % 2
        out[b, j * (L // 2):(j + 1) * (L // 2)] = res.results[c]["out"]
    return out


# version 08
# speedup vs baseline: 1.1748x; 1.1748x over previous
import contextlib
import numpy as np
import ml_dtypes
import concourse.bass as bass
import concourse.mybir as mybir
from concourse.bass import ds
from concourse.bass_utils import run_bass_kernel_spmd

F32 = mybir.dt.float32
BF16 = mybir.dt.bfloat16
I32 = mybir.dt.int32
ALU = mybir.AluOpType
ACT = mybir.ActivationFunctionType
BF = ml_dtypes.bfloat16

D_MODEL = 1024
D_INNER = 2048
DOWN = 1024
S = 16
L = 8192
NB = 4
TC = 63
TILE = 126
QUAD = 4

N_P4 = 32
DEBUG_PHASE = 4


def _tiles(n, sz):
    out = []
    t = 0
    while t < n:
        ln = min(sz, n - t)
        out.append((t, ln))
        t += ln
    return out


P1T = _tiles(L, TILE)
CHUNKS = _tiles(L, TC)
NCHUNK = len(CHUNKS)
QUADS = [list(range(i, min(i + QUAD, NCHUNK))) for i in range(0, NCHUNK, QUAD)]
T128T = _tiles(L, 128)

SBUF_ALLOCS = [
    ("WinT_sb", [128, 8 * 2048], BF16),
    ("WxT_sb", [128, 8 * 33], BF16),
    ("WoutT_sb", [128, 8 * 1024], BF16),
    ("baseWT_sb", [128, 8 * 1024], BF16),
    ("sharedAT_sb", [128, 8 * 16], BF16),
    ("sharedBT_sb", [16, 1024], BF16),
    ("routerWT_sb", [128, 8 * 8], BF16),
    ("routedAT_sb", [128, 8 * 96], BF16),
    ("routedBT_sb", [96, 1024], BF16),
    ("mtri_sb", [128, 2 * 64], BF16),
    ("mtriD_sb", [128, 2 * 64], F32),
    ("ident_sb", [128, 128], BF16),
    ("onesrow_sb", [1, 128], F32),
    ("onesrowb_sb", [1, 128], BF16),
    ("convw_sb", [128, 8 * 4], BF16),
    ("convb_sb", [128, 8], F32),
    ("acol_sb", [128, 2 * S], F32),
    ("arow_sb", [1, S], F32),
    ("wdtrep_sb", [128, 512], F32),
    ("bdtrep_sb", [128, 512], F32),
    ("drep_sb", [128, DOWN], BF16),
    ("lnwrep_sb", [128, DOWN], BF16),
    ("lnbrep_sb", [128, DOWN], BF16),
    ("bbrep_sb", [1, D_MODEL], BF16),
    ("rbrow_sb", [1, 8], BF16),
    ("slot_sb", [1, 1], I32),
    ("xiraw", [128, DOWN], BF16),
    ("szt", [128, DOWN], BF16),
    ("win", [128, 8 * (TILE + 3)], BF16),

    ("xi2tile", [128, 8 * TILE], BF16),
    ("xdTbuf", [33, L], BF16),
    ("xi2hs_t", [128, 2 * 512], BF16),
    ("xdhs", [128, 2 * QUAD * 33], F32),
    ("dthsq", [128, 2 * QUAD * 512], F32),
    ("dtxhsq", [128, 2 * QUAD * 512], BF16),
    ("xi2hsq", [128, 2 * QUAD * 512], BF16),
    ("dsbq", [128, 2 * QUAD * 512], F32),
    ("em_sb", [128, 2 * QUAD * 512], BF16),
    ("ep_sb", [128, 2 * QUAD * 512], BF16),
    ("ptmp", [128, 512], BF16),
    ("wts_sb", [128, 4 * 512], BF16),
    ("hets_sb", [128, 4 * 512], BF16),
    ("carrysave", [32, 512], BF16),
    ("accq", [128, QUAD * 512], BF16),
    ("y_ld", [128, DOWN], BF16),
    ("xi2_ld", [128, DOWN], BF16),
    ("sz_ld", [128, DOWN], BF16),
    ("statsbuf", [128, 192], F32),
    ("statspeer", [128, 128], F32),
    ("gt1", [128, DOWN], BF16),
    ("gt2", [128, DOWN], BF16),
    ("og_sb", [128, DOWN], BF16),
    ("ogT_sb", [128, DOWN], BF16),
    ("hshT_sb", [16, 128], BF16),
    ("hrw_sb", [128, 96], BF16),
    ("hrwT_sb", [96, 128], BF16),
    ("r6", [128, 48], F32),
    ("sc1", [128, 16], F32),
    ("epscol", [128, 1], F32),
]

SEM_NAMES = [
    "ser_v", "ser_s", "ser_g", "ld_w", "ld_x", "mm_xz", "ev_xz", "tp_xi", "ev_xi", "cv", "tp_hs", "ev_hs",
    "mm_xd", "ev_xd", "sp_dma", "x1_dma", "bar", "p2_ld", "tp_c", "ev_c",
    "p2l0", "p2l1", "p2l2", "p2l3", "y0", "y1", "y2", "y3", "y4", "y5", "y6", "y7",
    "dt_act", "dt_dve", "mm_d", "ev_dq", "e_act", "w_dve", "cr_dma", "mm_g",
    "he_dve", "acc_dve", "y_dma", "p3_ld", "st_dve", "x2_dma", "gt_dve",
    "tp_og", "ev_og", "mm_wo", "ev_wo", "x3_dma", "p4_ld", "p4_mm", "p4_dve",
    "p4_act", "p4_tp", "out_dma", "dbg_dma", "x1b_dma", "dbg2_dma", "p3t_ld", "p4g_ld",
] + [f"cr{i}" for i in range(8)] + [f"cs{i}" for i in range(8)] + [f"sp{i}" for i in range(4)]


class Sched:
    ENGINES = ("sync", "tensor", "vector", "scalar", "gpsimd")

    def __init__(self, nc, sems):
        self.nc = nc
        self.sems = sems
        self.cnt = {}
        self.cur = None
        self.eng = None

    def begin(self, eng_name, eng):
        self.cur = eng_name
        self.eng = eng
        self.cnt = {k: 0 for k in self.sems}

    def inc(self, sem, amt):
        self.cnt[sem] += amt
        return self.cnt[sem]

    def on(self, eng_name):
        return self.cur == eng_name

    SERIAL = {"vector": "ser_v", "scalar": "ser_s", "gpsimd": "ser_g"}

    def op(self, eng_name, fn, sem=None, amt=1):
        ser = Sched.SERIAL.get(eng_name)
        if ser is not None and sem is None:
            sem, amt = ser, 1
        if self.cur == eng_name:
            if ser is not None:
                self.eng.wait_ge(self.sems[ser], self.cnt[ser])
            inst = fn(self.eng)
            if sem is not None:
                inst.then_inc(self.sems[sem], amt)
        if sem is not None:
            self.inc(sem, amt)
        return self.cnt[sem] if sem is not None else None

    def vop(self, fn):
        return self.op("vector", fn)

    def sop(self, fn):
        return self.op("scalar", fn)

    def dma(self, eng_name, out, in_, sem, **kw):
        self.wait(eng_name, sem, self.cnt[sem])
        self.op(eng_name, lambda e: e.dma_start(out=out, in_=in_, **kw), sem, 16)

    def wait(self, eng_name, sem, val=None):
        v = self.cnt[sem] if val is None else val
        assert v <= self.cnt[sem], (sem, v, self.cnt[sem])
        if self.cur == eng_name:
            self.eng.wait_ge(self.sems[sem], v)

    def wait_all(self, sem, val=None):
        for e in Sched.ENGINES:
            self.wait(e, sem, val)


def build_program(debug_phase=DEBUG_PHASE):
    nc = bass.Bass("TRN2", target_bir_lowering=False)
    dtt = nc.dram_tensor
    g = {}

    def din(name, shape, dtype):
        g[name] = dtt(name, shape, dtype, kind="ExternalInput").ap()

    din("xT", [D_MODEL, L], F32)
    din("x_rows", [L // 2, D_MODEL], F32)
    din("W_inT", [D_MODEL, 2 * DOWN], F32)
    din("convw", [DOWN, 4], F32)
    din("convb", [128, 8], F32)
    din("W_xT", [DOWN, 33], F32)
    din("W_dt_own", [1, DOWN], F32)
    din("b_dt_own", [1, DOWN], F32)
    din("A_log_in", [1, S], F32)
    din("D_own", [1, DOWN], F32)
    din("lnw_own", [1, DOWN], F32)
    din("lnb_own", [1, DOWN], F32)
    din("W_outT", [DOWN, D_MODEL], F32)
    din("baseWT", [D_MODEL, D_MODEL], F32)
    din("base_b", [1, D_MODEL], F32)
    din("sharedAT", [D_MODEL, 16], F32)
    din("sharedBT", [16, D_MODEL], F32)
    din("routerWT", [D_MODEL, 8], F32)
    din("router_b", [1, 8], F32)
    din("routedAT", [D_MODEL, 96], F32)
    din("routedBT", [96, D_MODEL], F32)
    din("mtri_aug", [128, 64], BF16)
    din("mtri_tail", [128, 64], BF16)
    din("mtriD", [128, 64], F32)
    din("mtriD_tail", [128, 64], F32)
    din("ident", [128, 128], BF16)
    din("onesrow", [1, 128], F32)
    din("onesrowb", [1, 128], BF16)
    din("slot", [1, 1], I32)

    g["out"] = dtt("out", [L // 2, D_MODEL], F32, kind="ExternalOutput").ap()
    g["xi2hs_hbm"] = dtt("xi2hs", [NCHUNK, 128, 512], BF16, kind="Internal").ap()
    g["xi2t_hbm"] = dtt("xi2t", [DOWN, L], BF16, kind="Internal").ap()
    g["sz_hbm"] = dtt("szsp", [L, DOWN], BF16, kind="Internal").ap()
    g["y_hbm"] = dtt("ysp", [L, DOWN], BF16, kind="Internal").ap()
    g["shr_xd"] = dtt("shr_xd", [66, L], BF16, kind="Internal", addr_space="Shared").ap()
    g["shr_st"] = dtt("shr_st", [256, 128], F32, kind="Internal", addr_space="Shared").ap()
    g["shr_out"] = dtt("shr_out", [2 * L, D_MODEL], BF16, kind="Internal", addr_space="Shared").ap()
    g["myout_hbm"] = dtt("myout", [L, D_MODEL], BF16, kind="Internal").ap()
    g["loc0_hbm"] = dtt("loc0", [L // 2, D_MODEL], BF16, kind="Internal").ap()
    g["loc1_hbm"] = dtt("loc1", [L // 2, D_MODEL], BF16, kind="Internal").ap()
    g["cbar_in"] = dtt("cbar_in", [1, 16], F32, kind="Internal").ap()
    g["cbar_out"] = dtt("cbar_out", [1, 16], F32, kind="Internal").ap()
    RG = [[0, 1], [2, 3], [4, 5], [6, 7]]

    if debug_phase == 1:
        g["d_xi2t"] = dtt("d_xi2t", [DOWN, L], BF16, kind="ExternalOutput").ap()
        g["d_sz"] = dtt("d_sz", [L, DOWN], BF16, kind="ExternalOutput").ap()
        g["d_xdT"] = dtt("d_xdT", [33, L], BF16, kind="ExternalOutput").ap()
    if debug_phase == 2:
        g["d_y"] = dtt("d_y", [L, DOWN], BF16, kind="ExternalOutput").ap()

    stack = contextlib.ExitStack()
    with stack:
        for name, shape, dty in SBUF_ALLOCS:
            g[name] = stack.enter_context(nc.sbuf_tensor(name, shape, dty))
        g["ps_a"] = stack.enter_context(nc.psum_tensor("ps_a", [128, 2048], F32))
        g["ps_t1"] = stack.enter_context(nc.psum_tensor("ps_t1", [128, 1024], BF16))
        g["ps_xd"] = stack.enter_context(nc.psum_tensor("ps_xd", [33, TILE], F32))
        g["ps_g"] = stack.enter_context(nc.psum_tensor("ps_g", [128, 2 * 512], F32))
        g["op_sb"] = g["accq"][:, 0:1024]
        g["xTb"] = g["em_sb"][:, 0:2 * 8 * TILE]
        g["cvtmp"] = g["ep_sb"][:, 0:8 * TILE]
        g["cvacc"] = g["ep_sb"][:, 8 * TILE:2 * 8 * TILE]
        g["p4a"] = g["y_ld"]
        g["p4b"] = g["xi2_ld"]
        g["x1n_sb"] = g["sz_ld"]
        g["x1nT_sb"] = g["gt1"]
        g["xrow_sb"] = g["dthsq"][:, 0:1024]
        g["x1_sb"] = g["dthsq"][:, 1024:2048]
        g["x2_sb"] = g["dsbq"][:, 0:1024]
        sems = {}
        for s_ in SEM_NAMES:
            sems[s_] = stack.enter_context(nc.semaphore(s_))
        sch = Sched(nc, sems)

        def walk():
            emit_setup(sch, g, nc, debug_phase)
            emit_p1(sch, g, nc, debug_phase)
            if debug_phase >= 2:
                emit_x1(sch, g, nc, RG)
                emit_p2(sch, g, nc, debug_phase)
            if debug_phase >= 3:
                emit_p3(sch, g, nc, RG, debug_phase)
            if debug_phase >= 4:
                emit_p4(sch, g, nc, RG)
            emit_tail(sch, g, debug_phase)

        with nc.Block() as block:
            @block.sync
            def _(eng):
                sch.begin("sync", eng)
                walk()

            @block.tensor
            def _(eng):
                sch.begin("tensor", eng)
                walk()

            @block.vector
            def _(eng):
                sch.begin("vector", eng)
                walk()

            @block.scalar
            def _(eng):
                sch.begin("scalar", eng)
                walk()

            @block.gpsimd
            def _(eng):
                sch.begin("gpsimd", eng)
                walk()
    return nc


def emit_setup(sch, g, nc, debug_phase):
    G = "gpsimd"

    def ld(dst, src):
        sch.dma(G, dst, src, "ld_w")

    ld(g["WinT_sb"][:].rearrange("p (a o) -> p a o", a=8),
       g["W_inT"].rearrange("(a p) o -> p a o", p=128))
    ld(g["WxT_sb"][:].rearrange("p (a o) -> p a o", a=8),
       g["W_xT"].rearrange("(a p) o -> p a o", p=128))
    ld(g["WoutT_sb"][:].rearrange("p (a o) -> p a o", a=8),
       g["W_outT"].rearrange("(a p) o -> p a o", p=128))
    ld(g["baseWT_sb"][:].rearrange("p (a o) -> p a o", a=8),
       g["baseWT"].rearrange("(a p) o -> p a o", p=128))
    ld(g["sharedAT_sb"][:].rearrange("p (a o) -> p a o", a=8),
       g["sharedAT"].rearrange("(a p) o -> p a o", p=128))
    ld(g["sharedBT_sb"][:], g["sharedBT"])
    ld(g["routerWT_sb"][:].rearrange("p (a o) -> p a o", a=8),
       g["routerWT"].rearrange("(a p) o -> p a o", p=128))
    ld(g["routedAT_sb"][:].rearrange("p (a o) -> p a o", a=8),
       g["routedAT"].rearrange("(a p) o -> p a o", p=128))
    ld(g["routedBT_sb"][:], g["routedBT"])
    ld(g["mtri_sb"][:, 0:64], g["mtri_aug"])
    ld(g["mtri_sb"][:, 64:128], g["mtri_tail"])
    ld(g["mtriD_sb"][:, 0:64], g["mtriD"])
    ld(g["mtriD_sb"][:, 64:128], g["mtriD_tail"])
    ld(g["ident_sb"][:], g["ident"])
    ld(g["onesrow_sb"][:], g["onesrow"])
    ld(g["onesrowb_sb"][:], g["onesrowb"])
    ld(g["convw_sb"][:].rearrange("p (a k) -> p a k", a=8),
       g["convw"].rearrange("(a p) k -> p a k", p=128))
    ld(g["convb_sb"][:], g["convb"])
    ld(g["wdtrep_sb"][0:64, :], g["W_dt_own"][:, 0:512].partition_broadcast(64))
    ld(g["wdtrep_sb"][64:128, :], g["W_dt_own"][:, 512:1024].partition_broadcast(64))
    ld(g["bdtrep_sb"][0:64, :], g["b_dt_own"][:, 0:512].partition_broadcast(64))
    ld(g["bdtrep_sb"][64:128, :], g["b_dt_own"][:, 512:1024].partition_broadcast(64))
    ld(g["drep_sb"][:], g["D_own"].partition_broadcast(128))
    ld(g["lnwrep_sb"][:], g["lnw_own"].partition_broadcast(128))
    ld(g["lnbrep_sb"][:], g["lnb_own"].partition_broadcast(128))
    ld(g["bbrep_sb"][:], g["base_b"])
    ld(g["rbrow_sb"][:], g["router_b"])
    ld(g["arow_sb"][:], g["A_log_in"])
    ld(g["slot_sb"][:], g["slot"])
    n_w = sch.cnt["ld_w"]

    sch.wait("scalar", "ld_w", n_w)
    n1 = sch.sop(lambda e: e.activation(
        out=g["acol_sb"][0:1, 0:S], in_=g["arow_sb"][:], func=ACT.Exp))
    n2 = sch.sop(lambda e: e.copy(
        out=g["arow_sb"][:], in_=g["acol_sb"][0:1, 0:S]))
    sch.wait("tensor", "ser_s", n2)
    sch.op("tensor", lambda e: e.matmul(
        g["ps_a"][:, 0:S], g["onesrow_sb"][:], g["arow_sb"][:],
        start=True, stop=True), "mm_d")
    sch.wait("vector", "mm_d", sch.cnt["mm_d"])
    sch.vop(lambda e: e.tensor_scalar_mul(
        out=g["acol_sb"][:, 0:S], in0=g["ps_a"][:, 0:S], scalar1=-1.0))
    sch.vop(lambda e: e.tensor_scalar_mul(
        out=g["acol_sb"][:, S:2 * S], in0=g["ps_a"][:, 0:S], scalar1=1.0))
    sch.vop(lambda e: e.memset(g["carrysave"][:], 0.0))
    sch.vop(lambda e: e.memset(g["epscol"][:], 1e-5))
    nz = sch.vop(lambda e: e.memset(g["xi2hsq"][:, 0:128], 0.0))
    sch.wait("tensor", "ser_v", nz)
    sch.wait("tensor", "ld_w", sch.cnt["ld_w"])

    def mk_zps(e):
        last = None
        for kb in range(8):
            last = e.transpose(g["ps_t1"][:, kb * 128:(kb + 1) * 128],
                               g["xi2hsq"][:, 0:128], g["ident_sb"][:])
        return last
    sch.op("tensor", mk_zps, "tp_xi")
    sch.vop(lambda e: e.memset(g["win"][:], 0.0))
    n_ms = sch.vop(lambda e: e.memset(g["statsbuf"][:], 0.0))
    sch.wait("gpsimd", "ser_v", n_ms)
    sch.dma("gpsimd", g["cbar_in"], g["statsbuf"][0:1, 0:16], "ld_w")
    sch.wait_all("ser_v", n_ms)
    sch.wait_all("ld_w", sch.cnt["ld_w"])


def emit_p1(sch, g, nc, debug_phase):
    sems = sch.sems
    r = {"mm_xz": [0], "silu_z": [0], "xi_cp": [0], "wincp": [0], "conv": [0],
         "silu2": [0], "ev_xd": [0], "ev_hs": [0], "sp": [(0, 0, 0, 0)], "dbg": [0],
         "spd": []}
    for m, (t0, ln) in enumerate(P1T):
        buf = m % 2
        sch.wait("gpsimd", "mm_xz", r["mm_xz"][max(0, m - 1)])
        sch.dma("gpsimd",
                g["xTb"][:, buf * 8 * TILE:(buf + 1) * 8 * TILE]
                .rearrange("p (a t) -> p a t", a=8)[:, :, 0:ln],
                g["xT"][:, t0:t0 + ln].rearrange("(a p) t -> p a t", p=128),
                "ld_x")
        n_ldx = sch.cnt["ld_x"]
        sch.wait("tensor", "ld_x", n_ldx)
        sch.wait("tensor", "ser_s", r["silu_z"][m])
        sch.wait("tensor", "ser_v", r["xi_cp"][m])

        def mk_xz(e, buf=buf, ln=ln):
            last = None
            for k in range(8):
                lhs = g["xTb"][:, buf * 8 * TILE + k * TILE: buf * 8 * TILE + k * TILE + ln]
                for oc in range(4):
                    last = e.matmul(
                        g["ps_a"][0:ln, oc * 512:(oc + 1) * 512], lhs,
                        g["WinT_sb"][:, k * 2048 + oc * 512: k * 2048 + (oc + 1) * 512],
                        start=(k == 0), stop=(k == 7))
            return last
        sch.op("tensor", mk_xz, "mm_xz")
        r["mm_xz"].append(sch.cnt["mm_xz"])
        n_mmxz = sch.cnt["mm_xz"]
        sch.wait("scalar", "mm_xz", n_mmxz)
        for i_ in range(4):
            sch.wait("scalar", f"sp{i_}", r["sp"][m][i_] if m > 0 else 0)
        if debug_phase == 1:
            sch.wait("scalar", "dbg_dma", r["dbg"][m])
        ns = sch.sop(lambda e, ln=ln: e.activation(
            out=g["szt"][0:ln, :], in_=g["ps_a"][0:ln, DOWN:2 * DOWN], func=ACT.Silu))
        r["silu_z"].append(ns)
        sch.wait("vector", "mm_xz", n_mmxz)
        nv = sch.vop(lambda e, ln=ln: e.tensor_copy(
            out=g["xiraw"][0:ln, :], in_=g["ps_a"][0:ln, 0:DOWN]))
        r["xi_cp"].append(nv)
        sch.wait("sync", "ser_s", ns)
        sch.dma("sync", g["sz_hbm"][t0:t0 + ln, :], g["szt"][0:ln, :], f"sp{m % 4}")
        if debug_phase == 1:
            sch.dma("sync", g["d_sz"][t0:t0 + ln, :], g["szt"][0:ln, :], "dbg_dma")
        sch.wait("tensor", "ser_v", nv)
        sch.wait("tensor", "ser_v", r["ev_hs"][m])

        def mk_tp(e, ln=ln):
            last = None
            for kb in range(8):
                last = e.transpose(g["ps_t1"][:, kb * 128: kb * 128 + ln],
                                   g["xiraw"][0:ln, kb * 128:(kb + 1) * 128],
                                   g["ident_sb"][0:ln, 0:ln])
            return last
        sch.op("tensor", mk_tp, "tp_xi")
        n_tpxi = sch.cnt["tp_xi"]
        sch.wait("vector", "tp_xi", n_tpxi)
        nw = sch.vop(lambda e, ln=ln: e.tensor_copy(
            out=g["win"][:].rearrange("p (a t) -> p a t", a=8)[:, :, 3:3 + ln],
            in_=g["ps_t1"][:].rearrange("p (a t) -> p a t", a=8)[:, :, 0:ln]))
        r["wincp"].append(nw)
        wv = g["win"][:].rearrange("p (a t) -> p a t", a=8)
        cwv = g["convw_sb"][:].rearrange("p (a k) -> p a k", a=8)

        def cv_mul0(e, ln=ln):
            return e.tensor_tensor(
                out=g["cvacc"][:].rearrange("p (a t) -> p a t", a=8)[:, :, 0:ln],
                in0=wv[:, :, 0:ln],
                in1=cwv[:, :, 0:1].broadcast_to([128, 8, ln]), op=ALU.mult)
        sch.vop(cv_mul0)
        for k in (1, 2, 3):
            def cv_mul(e, k=k, ln=ln):
                return e.tensor_tensor(
                    out=g["cvtmp"][:].rearrange("p (a t) -> p a t", a=8)[:, :, 0:ln],
                    in0=wv[:, :, k:k + ln],
                    in1=cwv[:, :, k:k + 1].broadcast_to([128, 8, ln]), op=ALU.mult)
            sch.vop(cv_mul)

            def cv_add(e, ln=ln):
                av = g["cvacc"][:].rearrange("p (a t) -> p a t", a=8)[:, :, 0:ln]
                return e.tensor_tensor(out=av, in0=av,
                                       in1=g["cvtmp"][:].rearrange("p (a t) -> p a t", a=8)[:, :, 0:ln],
                                       op=ALU.add)
            sch.vop(cv_add)
        nc_conv = sch.vop(lambda e, ln=ln: e.tensor_copy(
            out=g["win"][:].rearrange("p (a t) -> p a t", a=8)[:, :, 0:3],
            in_=wv[:, :, ln:ln + 3]))
        r["conv"].append(nc_conv)
        sch.wait("scalar", "ser_v", nc_conv)
        sch.wait("scalar", "mm_xd", m)
        sch.wait("scalar", "tp_hs", m)
        sch.wait("scalar", "sp_dma", r["spd"][m - 1] if m > 0 else 0)
        for i_ in range(4):
            sch.wait("scalar", f"sp{i_}", r["sp"][m][i_] if m > 0 else 0)

        def mk_silu(e, ln=ln):
            last = None
            for kb in range(8):
                last = e.activation(out=g["xi2tile"][:, kb * TILE: kb * TILE + ln],
                                    in_=g["cvacc"][:, kb * TILE: kb * TILE + ln],
                                    func=ACT.Silu, bias=g["convb_sb"][:, kb:kb + 1])
            return last
        ns2 = sch.sop(mk_silu)
        r["silu2"].append(ns2)
        sch.wait("tensor", "ser_s", ns2)
        sch.wait("tensor", "ser_v", r["ev_xd"][m])

        def mk_xd(e, ln=ln):
            last = None
            for kb in range(8):
                last = e.matmul(g["ps_xd"][:, 0:ln],
                                g["WxT_sb"][:, kb * 33:(kb + 1) * 33],
                                g["xi2tile"][:, kb * TILE: kb * TILE + ln],
                                start=(kb == 0), stop=(kb == 7))
            return last
        sch.op("tensor", mk_xd, "mm_xd")
        n_mmxd = sch.cnt["mm_xd"]
        sch.wait("vector", "mm_xd", n_mmxd)
        nxd = sch.vop(lambda e, t0=t0, ln=ln: e.tensor_copy(
            out=g["xdTbuf"][:, t0:t0 + ln], in_=g["ps_xd"][:, 0:ln]))
        r["ev_xd"].append(nxd)
        sch.wait("sync", "ser_s", ns2)
        sch.dma("sync",
                g["xi2t_hbm"][:, t0:t0 + ln].rearrange("(a p) t -> p a t", p=128),
                g["xi2tile"][:].rearrange("p (a t) -> p a t", a=8)[:, :, 0:ln],
                "sp_dma")
        if debug_phase == 1:
            sch.dma("sync",
                    g["d_xi2t"][:, t0:t0 + ln].rearrange("(a p) t -> p a t", p=128),
                    g["xi2tile"][:].rearrange("p (a t) -> p a t", a=8)[:, :, 0:ln],
                    "dbg_dma")
        nch = 2 if ln == TILE else 1
        cl = TC if ln == TILE else ln
        sch.wait("tensor", "mm_xd", n_mmxd)
        sch.wait("tensor", "ser_v", nw)

        def mk_hst(e, nch=nch, cl=cl):
            last = None
            for ci in range(nch):
                for h in range(2):
                    for db in range(4):
                        kb = h * 4 + db
                        last = e.transpose(
                            g["ps_t1"][64 * h: 64 * h + cl,
                                       ci * 512 + db * 128: ci * 512 + db * 128 + 128],
                            g["xi2tile"][:, kb * TILE + ci * TC: kb * TILE + ci * TC + cl],
                            g["ident_sb"][:])
            return last
        sch.op("tensor", mk_hst, "tp_hs")
        n_tph = sch.cnt["tp_hs"]
        sch.wait("vector", "tp_hs", n_tph)
        for i_ in range(4):
            sch.wait("vector", f"sp{i_}", r["sp"][m][i_] if m > 0 else 0)
        nhs = sch.vop(lambda e, nch=nch: e.tensor_copy(
            out=g["xi2hs_t"][:, 0:nch * 512], in_=g["ps_t1"][:, 0:nch * 512]))
        r["ev_hs"].append(nhs)
        sch.wait("sync", "ser_v", nhs)
        for ci in range(nch):
            sch.dma("sync", g["xi2hs_hbm"][2 * m + ci],
                    g["xi2hs_t"][:, ci * 512:(ci + 1) * 512], f"sp{m % 4}")
        r["sp"].append(tuple(sch.cnt[f"sp{i}"] for i in range(4)))
        r["spd"].append(sch.cnt["sp_dma"])
        r["dbg"].append(sch.cnt["dbg_dma"])
    if debug_phase == 1:
        nxd_all = r["ev_xd"][-1]
        sch.wait("sync", "ser_v", nxd_all)
        sch.dma("sync", g["d_xdT"], g["xdTbuf"][:], "dbg_dma")


def emit_x1(sch, g, nc, RG):
    sems = sch.sems
    n_evxd = sch.cnt["ev_xd"]
    sch.wait("sync", "ev_xd", n_evxd)
    if sch.on("sync"):
        myslot = nc.values_load(g["slot_sb"][0:1, 0:1], engines=[mybir.EngineType.SP],
                                min_val=0, max_val=1, skip_runtime_bounds_check=True)
        g["myslot_sync"] = myslot
        nc.sync.dma_start(out=g["shr_xd"][0:33, :], in_=g["xdTbuf"][:],
                          cond=(myslot == 0)).then_inc(sems["x1_dma"], 16)
        nc.sync.dma_start(out=g["shr_xd"][33:66, :], in_=g["xdTbuf"][:],
                          cond=myslot).then_inc(sems["x1_dma"], 16)
    sch.inc("x1_dma", 32)
    sch.wait("gpsimd", "x1_dma", sch.cnt["x1_dma"])
    sch.op("gpsimd", lambda e: e.collective_compute(
        "AllReduce", ALU.add, ins=[g["cbar_in"][:]], outs=[g["cbar_out"][:]],
        replica_groups=RG), "bar")
    sch.wait("sync", "bar", sch.cnt["bar"])
    if sch.on("sync"):
        myslot = g["myslot_sync"]
        for half, dst in ((0, g["em_sb"]), (1, g["ep_sb"])):
            nc.sync.dma_start(out=dst[0:33, 0:4096],
                              in_=g["shr_xd"][33:66, half * 4096:(half + 1) * 4096],
                              cond=(myslot == 0)).then_inc(sems["x1_dma"], 16)
            nc.sync.dma_start(out=dst[0:33, 0:4096],
                              in_=g["shr_xd"][0:33, half * 4096:(half + 1) * 4096],
                              cond=myslot).then_inc(sems["x1_dma"], 16)
    sch.inc("x1_dma", 64)
    sch.wait("vector", "x1_dma", sch.cnt["x1_dma"])
    sch.vop(lambda e: e.tensor_tensor(out=g["xdTbuf"][:, 0:4096], in0=g["xdTbuf"][:, 0:4096],
                                      in1=g["em_sb"][0:33, 0:4096], op=ALU.add))
    nadd = sch.vop(lambda e: e.tensor_tensor(out=g["xdTbuf"][:, 4096:8192],
                                             in0=g["xdTbuf"][:, 4096:8192],
                                             in1=g["ep_sb"][0:33, 0:4096], op=ALU.add))
    sch.wait_all("ser_v", nadd)


def emit_p2(sch, g, nc, debug_phase):
    sems = sch.sems
    for i_ in range(4):
        sch.wait("gpsimd", f"sp{i_}", sch.cnt[f"sp{i_}"])
    he_cnt = [0] * S
    slot_rd = {}
    wslot_g = {}
    psg_rd = [0, 0]
    em_war = [0, 0]
    last_xdev = [0]
    g_cnt = [0] * S
    q_dtx = [0, 0]
    q_mmd = [0, 0]
    q_eact = [0, 0]
    q_evq = [0, 0]
    q_ydma_prev = [[0, 0] for _ in range(QUAD)]
    ld_cnt = [[0] * QUAD, [0] * QUAD]

    def emit_loads(qi2):
        rot2 = qi2 % 2
        sch.wait("gpsimd", "ser_v", q_dtx[rot2])
        for ci2, c2 in enumerate(QUADS[qi2]):
            sch.dma("gpsimd",
                    g["xi2hsq"][:, (rot2 * QUAD + ci2) * 512:(rot2 * QUAD + ci2 + 1) * 512],
                    g["xi2hs_hbm"][c2], f"p2l{ci2}")
            ld_cnt[rot2][ci2] = sch.cnt[f"p2l{ci2}"]

    emit_loads(0)
    for qi, quad in enumerate(QUADS):
        rot = qi % 2
        qn = len(quad)
        for ci, c in enumerate(quad):
            t0, cl = CHUNKS[c]
            sch.wait("tensor", "ser_v", last_xdev[0])

            def mk_xdt(e, t0=t0, cl=cl):
                e.transpose(g["ps_t1"][0:cl, 0:33], g["xdTbuf"][:, t0:t0 + cl],
                            g["ident_sb"][0:33, 0:33])
                return e.transpose(g["ps_t1"][64:64 + cl, 0:33],
                                   g["xdTbuf"][:, t0:t0 + cl],
                                   g["ident_sb"][0:33, 0:33])
            sch.op("tensor", mk_xdt, "tp_c")
            sch.wait("vector", "tp_c", sch.cnt["tp_c"])
            last_xdev[0] = sch.vop(lambda e, ci=ci, rot=rot: e.tensor_copy(
                out=g["xdhs"][:, (rot * QUAD + ci) * 33:(rot * QUAD + ci + 1) * 33],
                in_=g["ps_t1"][:, 0:33]))
        sch.wait("vector", "mm_d", q_mmd[rot])
        dtx_last = 0
        for ci, c in enumerate(quad):
            slot = rot * QUAD + ci
            dsl = g["dthsq"][:, slot * 512:(slot + 1) * 512]
            dtin_col = g["xdhs"][:, slot * 33: slot * 33 + 1]
            n1 = sch.vop(lambda e, dsl=dsl, dtin_col=dtin_col: e.scalar_tensor_tensor(
                out=dsl, in0=g["wdtrep_sb"][:], scalar=dtin_col, in1=g["bdtrep_sb"][:],
                op0=ALU.mult, op1=ALU.add))
            sch.wait("scalar", "ser_v", n1)
            sch.sop(lambda e, dsl=dsl: e.activation(out=dsl, in_=dsl, func=ACT.Exp))
            n2 = sch.sop(lambda e, dsl=dsl: e.activation(out=dsl, in_=dsl,
                                                         func=ACT.Ln, bias=1.0))
            sch.wait("vector", "ser_s", n2)
            sch.vop(lambda e, dsl=dsl: e.tensor_scalar(
                out=dsl, in0=dsl, scalar1=1e-4, scalar2=10.0, op0=ALU.max, op1=ALU.min))
            sch.wait("vector", f"p2l{ci}", ld_cnt[rot][ci])
            dtx_last = sch.vop(lambda e, slot=slot, dsl=dsl: e.tensor_tensor(
                out=g["dtxhsq"][:, slot * 512:(slot + 1) * 512], in0=dsl,
                in1=g["xi2hsq"][:, slot * 512:(slot + 1) * 512], op=ALU.mult))
        q_dtx[rot] = dtx_last
        if qi + 1 < len(QUADS):
            emit_loads(qi + 1)
        sch.wait("tensor", "ser_v", dtx_last)
        sch.wait("tensor", "ser_v", q_evq[rot])
        sch.wait("tensor", "ser_v", q_evq[1 - rot])

        def mk_dmm(e, quad=quad, rot=rot):
            last = None
            for ci, c in enumerate(quad):
                tail = (CHUNKS[c][1] != TC)
                lhsc = g["mtriD_sb"][:, 64:128] if tail else g["mtriD_sb"][:, 0:64]
                for h in range(2):
                    last = e.matmul(
                        g["ps_a"][64 * h:64 * h + 64, ci * 512:(ci + 1) * 512],
                        lhsc[64 * h:64 * h + 64, :],
                        g["dthsq"][64 * h:64 * h + 64,
                                   (rot * QUAD + ci) * 512:(rot * QUAD + ci + 1) * 512],
                        start=True, stop=True, tile_position=(64 * h, 64 * h))
            return last
        sch.op("tensor", mk_dmm, "mm_d")
        q_mmd[rot] = sch.cnt["mm_d"]
        sch.wait("vector", "mm_d", q_mmd[rot])
        sch.wait("vector", "ser_s", q_eact[rot])
        nev = sch.vop(lambda e, rot=rot, qn=qn: e.tensor_copy(
            out=g["dsbq"][:, rot * 2048: rot * 2048 + qn * 512],
            in_=g["ps_a"][:, 0:qn * 512]))
        q_evq[rot] = nev
        for sp in range(0, S, 2):
            lanes = (sp, sp + 1)
            sch.wait("scalar", "ser_v", nev)
            sch.wait("scalar", "ser_v", max(em_war))
            n_eact = 0
            for li, s_ in enumerate(lanes):
                sch.sop(lambda e, li=li, s_=s_, rot=rot, qn=qn: e.activation(
                    out=g["em_sb"][:, li * 2048: li * 2048 + qn * 512],
                    in_=g["dsbq"][:, rot * 2048: rot * 2048 + qn * 512],
                    func=ACT.Exp, scale=g["acol_sb"][:, S + s_:S + s_ + 1]))
                n_eact = sch.sop(lambda e, li=li, s_=s_, rot=rot, qn=qn: e.activation(
                    out=g["ep_sb"][:, li * 2048: li * 2048 + qn * 512],
                    in_=g["dsbq"][:, rot * 2048: rot * 2048 + qn * 512],
                    func=ACT.Exp, scale=g["acol_sb"][:, s_:s_ + 1]))
            w_pipe = {s2: {} for s2 in lanes}
            def emit_pw(ci2):
                c2 = quad[ci2]
                slot2 = rot * QUAD + ci2
                for li2, s2 in enumerate(lanes):
                    sch.wait("vector", "ser_s", n_eact)
                    wslot_pre = li2 * 2 + quad[ci2] % 2
                    sch.wait("vector", "mm_g", wslot_g.get(wslot_pre, 0))
                    Bcol = g["xdhs"][:, slot2 * 33 + 1 + s2: slot2 * 33 + 2 + s2]
                    wslot = li2 * 2 + c2 % 2
                    sch.vop(lambda e, slot2=slot2, Bcol=Bcol: e.tensor_scalar(
                        out=g["ptmp"][:], in0=g["dtxhsq"][:, slot2 * 512:(slot2 + 1) * 512],
                        scalar1=Bcol, scalar2=10.0, op0=ALU.mult, op1=ALU.min))
                    nw2 = sch.vop(lambda e, li2=li2, ci2=ci2, wslot=wslot: e.scalar_tensor_tensor(
                        out=g["wts_sb"][:, wslot * 512:(wslot + 1) * 512], in0=g["ptmp"][:],
                        scalar=-10.0,
                        in1=g["em_sb"][:, li2 * 2048 + ci2 * 512: li2 * 2048 + (ci2 + 1) * 512],
                        op0=ALU.max, op1=ALU.mult))
                    w_pipe[s2][ci2] = nw2

            emit_pw(0)
            if qn > 1:
                emit_pw(1)
            for ci, c in enumerate(quad):
                t0, cl = CHUNKS[c]
                tail = (cl != TC)
                slot = rot * QUAD + ci
                qstart = (ci == 0)
                for li, s_ in enumerate(lanes):
                    wslot = li * 2 + c % 2
                    crs = f"cr{sp // 2}"
                    sch.wait("sync", "ser_v", w_pipe[s_][ci])
                    if qstart:
                        srcs = [g["carrysave"][2 * s_:2 * s_ + 1, :],
                                g["carrysave"][2 * s_ + 1:2 * s_ + 2, :]]
                    else:
                        pslot = li * 2 + (c - 1) % 2
                        pcl = CHUNKS[c - 1][1]
                        sch.wait("sync", "ser_v", he_cnt[s_])
                        het_p = g["hets_sb"][:, pslot * 512:(pslot + 1) * 512]
                        srcs = [het_p[pcl - 1:pcl, :], het_p[63 + pcl:64 + pcl, :]]
                    css = f"cs{sp // 2}"
                    wt = g["wts_sb"][:, wslot * 512:(wslot + 1) * 512]
                    sch.dma("sync", wt[63:64, :], srcs[0], crs)
                    sch.dma("sync", wt[127:128, :], srcs[1], css)
                    n_cr = sch.cnt[crs]
                    n_cs = sch.cnt[css]
                    if not qstart:
                        slot_rd[pslot] = (crs, n_cr, css, n_cs)
                    sch.wait("tensor", crs, n_cr)
                    sch.wait("tensor", css, n_cs)
                    sch.wait("tensor", "ser_v", max(he_cnt[s_], psg_rd[li]))
                    lhsg = g["mtri_sb"][:, 64:128] if tail else g["mtri_sb"][:, 0:64]

                    def mk_g(e, li=li, lhsg=lhsg, wt=wt):
                        last = None
                        for h in range(2):
                            last = e.matmul(
                                g["ps_g"][64 * h:64 * h + 64, li * 512:(li + 1) * 512],
                                lhsg[64 * h:64 * h + 64, :], wt[64 * h:64 * h + 64, :],
                                start=True, stop=True, tile_position=(64 * h, 64 * h))
                        return last
                    sch.op("tensor", mk_g, "mm_g")
                    g_cnt[s_] = sch.cnt["mm_g"]
                    wslot_g[wslot] = sch.cnt["mm_g"]
                for li, s_ in enumerate(lanes):
                    hslot = li * 2 + c % 2
                    het = g["hets_sb"][:, hslot * 512:(hslot + 1) * 512]
                    sch.wait("vector", "mm_g", g_cnt[s_])
                    sch.wait("vector", f"cr{sp // 2}", sch.cnt[f"cr{sp // 2}"])
                    sch.wait("vector", f"cs{sp // 2}", sch.cnt[f"cs{sp // 2}"])
                    if hslot in slot_rd:
                        sr = slot_rd[hslot]
                        sch.wait("vector", sr[0], sr[1])
                        if len(sr) == 4:
                            sch.wait("vector", sr[2], sr[3])
                    nhe = sch.vop(lambda e, het=het, li=li, ci=ci: e.tensor_tensor(
                        out=het,
                        in0=g["ep_sb"][:, li * 2048 + ci * 512: li * 2048 + (ci + 1) * 512],
                        in1=g["ps_g"][:, li * 512:(li + 1) * 512], op=ALU.mult))
                    he_cnt[s_] = nhe
                    psg_rd[li] = nhe
                    Ccol = g["xdhs"][:, slot * 33 + 17 + s_: slot * 33 + 18 + s_]
                    adst = g["accq"][:, ci * 512:(ci + 1) * 512]
                    if s_ == 0:
                        sch.wait("vector", f"y{ci * 2}", q_ydma_prev[ci][0])
                        sch.wait("vector", f"y{ci * 2 + 1}", q_ydma_prev[ci][1])
                        if debug_phase == 2:
                            sch.wait("vector", "dbg2_dma", sch.cnt["dbg2_dma"])
                        sch.vop(lambda e, adst=adst, het=het, Ccol=Ccol: e.tensor_scalar(
                            out=adst, in0=het, scalar1=Ccol, scalar2=None, op0=ALU.mult))
                    else:
                        sch.vop(lambda e, adst=adst, het=het, Ccol=Ccol:
                                e.scalar_tensor_tensor(
                                    out=adst, in0=het, scalar=Ccol, in1=adst,
                                    op0=ALU.mult, op1=ALU.add))
                    if ci == qn - 1 and c != NCHUNK - 1:
                        crs = f"cr{sp // 2}"
                        sch.wait("sync", "ser_v", nhe)
                        sch.dma("sync", g["carrysave"][2 * s_:2 * s_ + 1, :],
                                het[cl - 1:cl, :], crs)
                        sch.dma("sync", g["carrysave"][2 * s_ + 1:2 * s_ + 2, :],
                                het[63 + cl:64 + cl, :], f"cs{sp // 2}")
                        slot_rd[hslot] = (crs, sch.cnt[crs],
                                          f"cs{sp // 2}", sch.cnt[f"cs{sp // 2}"])
                if ci + 2 < qn:
                    emit_pw(ci + 2)
            em_war[0] = he_cnt[sp]
            em_war[1] = he_cnt[sp + 1]
            q_eact[rot] = n_eact
        nacc_fin = sch.cnt["ser_v"]
        sch.wait("gpsimd", "ser_v", nacc_fin)
        for ci, c in enumerate(quad):
            t0, cl = CHUNKS[c]
            acc_fin = g["accq"][:, ci * 512:(ci + 1) * 512]
            sch.dma("gpsimd", g["y_hbm"][t0:t0 + cl, 0:512], acc_fin[0:cl, :],
                    f"y{ci * 2}")
            sch.dma("gpsimd", g["y_hbm"][t0:t0 + cl, 512:1024], acc_fin[64:64 + cl, :],
                    f"y{ci * 2 + 1}")
            q_ydma_prev[ci] = [sch.cnt[f"y{ci * 2}"], sch.cnt[f"y{ci * 2 + 1}"]]
            if debug_phase == 2:
                sch.dma("gpsimd", g["d_y"][t0:t0 + cl, 0:512], acc_fin[0:cl, :], "dbg2_dma")
                sch.dma("gpsimd", g["d_y"][t0:t0 + cl, 512:1024], acc_fin[64:64 + cl, :], "dbg2_dma")


def emit_p3(sch, g, nc, RG, debug_phase):
    sems = sch.sems
    for i_ in range(8):
        sch.wait("gpsimd", f"y{i_}", sch.cnt[f"y{i_}"])
    for e_ in ("gpsimd", "sync"):
        sch.wait(e_, "sp_dma", sch.cnt["sp_dma"])
        for i_ in range(4):
            sch.wait(e_, f"sp{i_}", sch.cnt[f"sp{i_}"])
    sch.wait("scalar", "mm_d", sch.cnt["mm_d"])
    st_tile = []
    for mt, (t0, ln) in enumerate(T128T):
        sch.wait("gpsimd", "ser_v", st_tile[-1] if st_tile else 0)
        sch.dma("gpsimd", g["y_ld"][0:ln, :], g["y_hbm"][t0:t0 + ln, :], "p3_ld")
        n_ld = sch.cnt["p3_ld"]
        sch.wait("vector", "p3_ld", n_ld)
        sch.vop(lambda e, mt=mt, ln=ln: e.reduce_sum(
            out=g["statsbuf"][0:ln, 2 * mt:2 * mt + 1], in_=g["y_ld"][0:ln, :],
            axis=mybir.AxisListType.X))
        sch.wait("scalar", "p3_ld", n_ld)
        nsq = sch.sop(lambda e, ln=ln: e.activation(
            out=g["x1_sb"][0:ln, :], in_=g["y_ld"][0:ln, :], func=ACT.Square))
        sch.wait("vector", "ser_s", nsq)
        nst = sch.vop(lambda e, mt=mt, ln=ln: e.reduce_sum(
            out=g["statsbuf"][0:ln, 2 * mt + 1:2 * mt + 2], in_=g["x1_sb"][0:ln, :],
            axis=mybir.AxisListType.X))
        st_tile.append(nst)
    sch.wait("sync", "ser_v", st_tile[-1])
    if sch.on("sync"):
        myslot = nc.values_load(g["slot_sb"][0:1, 0:1], engines=[mybir.EngineType.SP],
                                min_val=0, max_val=1, skip_runtime_bounds_check=True)
        g["myslot_sync"] = myslot
        nc.sync.dma_start(out=g["shr_st"][0:128, :], in_=g["statsbuf"][:, 0:128],
                          cond=(myslot == 0)).then_inc(sems["x2_dma"], 16)
        nc.sync.dma_start(out=g["shr_st"][128:256, :], in_=g["statsbuf"][:, 0:128],
                          cond=myslot).then_inc(sems["x2_dma"], 16)
    sch.inc("x2_dma", 32)
    sch.wait("gpsimd", "x2_dma", sch.cnt["x2_dma"])
    sch.op("gpsimd", lambda e: e.collective_compute(
        "AllReduce", ALU.add, ins=[g["cbar_in"][:]], outs=[g["cbar_out"][:]],
        replica_groups=RG), "bar")
    sch.wait("sync", "bar", sch.cnt["bar"])
    if sch.on("sync"):
        myslot = g["myslot_sync"]
        nc.sync.dma_start(out=g["statspeer"][:], in_=g["shr_st"][128:256, :],
                          cond=(myslot == 0)).then_inc(sems["x2_dma"], 16)
        nc.sync.dma_start(out=g["statspeer"][:], in_=g["shr_st"][0:128, :],
                          cond=myslot).then_inc(sems["x2_dma"], 16)
    sch.inc("x2_dma", 32)
    sch.wait("vector", "x2_dma", sch.cnt["x2_dma"])
    sv = g["statsbuf"][:, 0:128].rearrange("p (m two) -> p m two", two=2)
    sch.vop(lambda e: e.tensor_tensor(out=g["statsbuf"][:, 0:128],
                                      in0=g["statsbuf"][:, 0:128],
                                      in1=g["statspeer"][:, 0:128], op=ALU.add))
    sch.vop(lambda e: e.tensor_scalar_mul(
        out=g["statsbuf"][:, 128:192], in0=sv[:, :, 0], scalar1=1.0 / D_INNER))
    sch.vop(lambda e: e.tensor_scalar_mul(
        out=g["statspeer"][:, 0:64], in0=sv[:, :, 1], scalar1=1.0 / D_INNER))
    sch.vop(lambda e: e.tensor_tensor(out=g["statspeer"][:, 64:128],
                                      in0=g["statsbuf"][:, 128:192],
                                      in1=g["statsbuf"][:, 128:192], op=ALU.mult))
    nvar = sch.vop(lambda e: e.tensor_tensor(out=g["statspeer"][:, 0:64],
                                             in0=g["statspeer"][:, 0:64],
                                             in1=g["statspeer"][:, 64:128], op=ALU.subtract))
    sch.wait("scalar", "ser_v", nvar)
    nsq2 = sch.sop(lambda e: e.activation(out=g["statspeer"][:, 64:128],
                                          in_=g["statspeer"][:, 0:64],
                                          func=ACT.Sqrt, bias=g["epscol"][:]))
    sch.wait("vector", "ser_s", nsq2)
    nrst = sch.vop(lambda e: e.reciprocal(out=g["statsbuf"][:, 0:64],
                                          in_=g["statspeer"][:, 64:128]))
    gt_done = []
    ogT_prev = [0]
    for mt, (t0, ln) in enumerate(T128T):
        sch.wait("gpsimd", "ser_v", gt_done[-1] if gt_done else nrst)
        sch.dma("gpsimd", g["y_ld"][0:ln, :], g["y_hbm"][t0:t0 + ln, :], "p3_ld")
        sch.dma("gpsimd", g["sz_ld"][0:ln, :], g["sz_hbm"][t0:t0 + ln, :], "p3_ld")
        sch.wait("sync", "ser_v", gt_done[-1] if gt_done else nrst)
        sch.wait("sync", "p3t_ld", sch.cnt["p3t_ld"])
        if sch.on("sync"):
            for kb in range(8):
                sch.eng.dma_start_transpose(
                    g["xi2_ld"][0:ln, kb * 128:(kb + 1) * 128],
                    g["xi2t_hbm"][kb * 128:(kb + 1) * 128, t0:t0 + ln]).then_inc(sems["p3t_ld"], 16)
        sch.inc("p3t_ld", 8 * 16)
        n_ld = sch.cnt["p3_ld"]
        sch.wait("vector", "p3_ld", n_ld)
        sch.wait("vector", "p3t_ld", sch.cnt["p3t_ld"])
        sch.wait("vector", "ser_v", nrst)
        mu_c = g["statsbuf"][:, 128 + mt:129 + mt]
        rs_c = g["statsbuf"][:, mt:mt + 1]
        sch.vop(lambda e, ln=ln, mu_c=mu_c, rs_c=rs_c: e.tensor_scalar(
            out=g["gt1"][0:ln, :], in0=g["y_ld"][0:ln, :], scalar1=mu_c[0:ln, :],
            scalar2=rs_c[0:ln, :], op0=ALU.subtract, op1=ALU.mult))
        sch.vop(lambda e, ln=ln: e.tensor_tensor(
            out=g["gt1"][0:ln, :], in0=g["gt1"][0:ln, :], in1=g["lnwrep_sb"][0:ln, :],
            op=ALU.mult))
        sch.vop(lambda e, ln=ln: e.tensor_tensor(
            out=g["gt2"][0:ln, :], in0=g["xi2_ld"][0:ln, :], in1=g["drep_sb"][0:ln, :],
            op=ALU.mult))
        sch.vop(lambda e, ln=ln: e.tensor_tensor(
            out=g["gt1"][0:ln, :], in0=g["gt1"][0:ln, :], in1=g["gt2"][0:ln, :],
            op=ALU.add))
        sch.vop(lambda e, ln=ln: e.tensor_tensor(
            out=g["gt1"][0:ln, :], in0=g["gt1"][0:ln, :], in1=g["lnbrep_sb"][0:ln, :],
            op=ALU.add))
        nog = sch.vop(lambda e, ln=ln: e.tensor_tensor(
            out=g["og_sb"][0:ln, :], in0=g["gt1"][0:ln, :], in1=g["sz_ld"][0:ln, :],
            op=ALU.mult))
        sch.wait("tensor", "ser_v", nog)
        sch.wait("tensor", "ser_v", ogT_prev[0])

        def mk_ogt(e, ln=ln):
            last = None
            for kb in range(8):
                last = e.transpose(g["ps_t1"][:, kb * 128: kb * 128 + ln],
                                   g["og_sb"][0:ln, kb * 128:(kb + 1) * 128],
                                   g["ident_sb"][0:ln, 0:ln])
            return last
        sch.op("tensor", mk_ogt, "tp_og")
        sch.wait("vector", "tp_og", sch.cnt["tp_og"])
        nevog = sch.vop(lambda e: e.tensor_copy(out=g["ogT_sb"][:], in_=g["ps_t1"][:]))
        ogT_prev[0] = nevog
        sch.wait("tensor", "ser_v", nevog)
        sch.wait("tensor", "ser_v", sch.cnt["ser_v"])

        def mk_wo(e, ln=ln):
            last = None
            for kb in range(8):
                for oc in range(2):
                    last = e.matmul(
                        g["ps_a"][0:ln, oc * 512:(oc + 1) * 512],
                        g["ogT_sb"][:, kb * 128:(kb + 1) * 128][:, 0:ln],
                        g["WoutT_sb"][:, kb * 1024 + oc * 512: kb * 1024 + (oc + 1) * 512],
                        start=(kb == 0), stop=(kb == 7))
            return last
        sch.op("tensor", mk_wo, "mm_wo")
        n_wo = sch.cnt["mm_wo"]
        sch.wait("vector", "mm_wo", n_wo)
        sch.wait("vector", "x3_dma", len(gt_done) * 16)
        nop_ = sch.vop(lambda e, ln=ln: e.tensor_copy(
            out=g["op_sb"][0:ln, :], in_=g["ps_a"][0:ln, 0:1024]))
        gt_done.append(nop_)
        sch.wait("sync", "ser_v", nop_)
        sch.dma("sync", g["myout_hbm"][t0:t0 + ln, :], g["op_sb"][0:ln, :], "x3_dma")
    sch.wait("sync", "x3_dma", sch.cnt["x3_dma"])
    if sch.on("sync"):
        myslot = g["myslot_sync"]
        nc.sync.dma_start(out=g["shr_out"][0:L, :], in_=g["myout_hbm"][:, :],
                          cond=(myslot == 0)).then_inc(sems["x3_dma"], 16)
        nc.sync.dma_start(out=g["shr_out"][L:2 * L, :], in_=g["myout_hbm"][:, :],
                          cond=myslot).then_inc(sems["x3_dma"], 16)
    sch.inc("x3_dma", 32)
    sch.wait("gpsimd", "x3_dma", sch.cnt["x3_dma"])
    sch.op("gpsimd", lambda e: e.collective_compute(
        "AllReduce", ALU.add, ins=[g["cbar_in"][:]], outs=[g["cbar_out"][:]],
        replica_groups=RG), "bar")


def emit_p4(sch, g, nc, RG):
    sems = sch.sems
    nbar = sch.cnt["bar"]
    sch.wait_all("bar", nbar)
    AX = mybir.AxisListType.X
    if sch.on("sync"):
        myslot = g["myslot_sync"]
        for base, dst in ((0, g["loc0_hbm"]), (L, g["loc1_hbm"])):
            nc.sync.dma_start(out=dst[:, :], in_=g["shr_out"][base:base + 4096, :],
                              cond=(myslot == 0)).then_inc(sems["p4_ld"], 16)
            nc.sync.dma_start(out=dst[:, :], in_=g["shr_out"][base + 4096:base + 8192, :],
                              cond=myslot).then_inc(sems["p4_ld"], 16)
    sch.inc("p4_ld", 64)
    prev = [0]
    prevT = [0]
    for mt in range(N_P4):
        t0 = mt * 128
        sch.wait("sync", "ser_v", prev[0])
        sch.dma("sync", g["p4a"][:, 0:1024], g["loc0_hbm"][t0:t0 + 128, :], "p4_ld")
        sch.dma("sync", g["p4b"][:, 0:1024], g["loc1_hbm"][t0:t0 + 128, :], "p4_ld")
        sch.dma("gpsimd", g["xrow_sb"], g["x_rows"][t0:t0 + 128, :], "p4g_ld")
        n_ld = sch.cnt["p4_ld"]
        sch.wait("vector", "p4_ld", n_ld)
        sch.wait("vector", "p4g_ld", sch.cnt["p4g_ld"])
        sch.vop(lambda e: e.tensor_tensor(out=g["p4a"][:, 0:1024], in0=g["p4a"][:, 0:1024],
                                          in1=g["p4b"][:, 0:1024], op=ALU.add))
        nx1 = sch.vop(lambda e: e.tensor_tensor(out=g["x1_sb"], in0=g["xrow_sb"],
                                                in1=g["p4a"][:, 0:1024], op=ALU.add))
        sch.vop(lambda e: e.reduce_sum(out=g["sc1"][:, 0:1], in_=g["x1_sb"], axis=AX))
        sch.wait("scalar", "ser_v", nx1)
        sch.wait("scalar", "out_dma", mt * 16)
        nsq = sch.sop(lambda e: e.activation(out=g["x2_sb"], in_=g["x1_sb"], func=ACT.Square))
        sch.wait("vector", "ser_s", nsq)
        sch.vop(lambda e: e.reduce_sum(out=g["sc1"][:, 1:2], in_=g["x2_sb"], axis=AX))
        sch.vop(lambda e: e.tensor_scalar_mul(out=g["sc1"][:, 2:3], in0=g["sc1"][:, 0:1],
                                              scalar1=1.0 / D_MODEL))
        sch.vop(lambda e: e.tensor_scalar_mul(out=g["sc1"][:, 3:4], in0=g["sc1"][:, 1:2],
                                              scalar1=1.0 / D_MODEL))
        sch.vop(lambda e: e.tensor_tensor(out=g["sc1"][:, 4:5], in0=g["sc1"][:, 2:3],
                                          in1=g["sc1"][:, 2:3], op=ALU.mult))
        nvar = sch.vop(lambda e: e.tensor_tensor(out=g["sc1"][:, 3:4], in0=g["sc1"][:, 3:4],
                                                 in1=g["sc1"][:, 4:5], op=ALU.subtract))
        sch.wait("scalar", "ser_v", nvar)
        nsd = sch.sop(lambda e: e.activation(out=g["sc1"][:, 4:5], in_=g["sc1"][:, 3:4],
                                             func=ACT.Sqrt, bias=g["epscol"][:]))
        sch.wait("vector", "ser_s", nsd)
        sch.vop(lambda e: e.reciprocal(out=g["sc1"][:, 5:6], in_=g["sc1"][:, 4:5]))
        nx1n = sch.vop(lambda e: e.tensor_scalar(
            out=g["x1n_sb"][:, 0:1024], in0=g["x1_sb"], scalar1=g["sc1"][:, 2:3],
            scalar2=g["sc1"][:, 5:6], op0=ALU.subtract, op1=ALU.mult))
        sch.wait("tensor", "ser_v", nx1n)
        sch.wait("tensor", "ser_v", prevT[0])

        def mk_xt(e):
            last = None
            for kb in range(8):
                last = e.transpose(g["ps_t1"][:, kb * 128:(kb + 1) * 128],
                                   g["x1n_sb"][:, kb * 128:(kb + 1) * 128][0:128, :],
                                   g["ident_sb"][:])
            return last
        sch.op("tensor", mk_xt, "p4_tp")
        sch.wait("vector", "p4_tp", sch.cnt["p4_tp"])
        nxT = sch.vop(lambda e: e.tensor_copy(out=g["x1nT_sb"][:, 0:1024], in_=g["ps_t1"][:]))
        sch.wait("tensor", "ser_v", nxT)

        def mk_ffn1(e):
            last = None
            for kb in range(8):
                for oc in range(2):
                    last = e.matmul(
                        g["ps_a"][:, oc * 512:(oc + 1) * 512],
                        g["x1nT_sb"][:, kb * 128:(kb + 1) * 128],
                        g["baseWT_sb"][:, kb * 1024 + oc * 512: kb * 1024 + (oc + 1) * 512],
                        start=(kb == 0), stop=False)
            for oc in range(2):
                last = e.matmul(g["ps_a"][:, oc * 512:(oc + 1) * 512],
                                g["onesrowb_sb"][:],
                                g["bbrep_sb"][:, oc * 512:(oc + 1) * 512],
                                start=False, stop=False)
            for kb in range(8):
                last = e.matmul(g["ps_g"][0:16, 0:128],
                                g["sharedAT_sb"][:, kb * 16:(kb + 1) * 16],
                                g["x1nT_sb"][:, kb * 128:(kb + 1) * 128],
                                start=(kb == 0), stop=(kb == 7))
            for kb in range(8):
                last = e.matmul(g["ps_g"][:, 512:520],
                                g["x1nT_sb"][:, kb * 128:(kb + 1) * 128],
                                g["routerWT_sb"][:, kb * 8:(kb + 1) * 8],
                                start=(kb == 0), stop=False)
            last = e.matmul(g["ps_g"][:, 512:520], g["onesrowb_sb"][:],
                            g["rbrow_sb"][:], start=False, stop=True)
            for kb in range(8):
                last = e.matmul(g["ps_g"][:, 640:736],
                                g["x1nT_sb"][:, kb * 128:(kb + 1) * 128],
                                g["routedAT_sb"][:, kb * 96:(kb + 1) * 96],
                                start=(kb == 0), stop=(kb == 7))
            return last
        sch.op("tensor", mk_ffn1, "p4_mm")
        nmm1 = sch.cnt["p4_mm"]
        sch.wait("vector", "p4_mm", nmm1)
        sch.vop(lambda e: e.tensor_copy(out=g["hshT_sb"][:], in_=g["ps_g"][0:16, 0:128]))
        sch.vop(lambda e: e.tensor_copy(out=g["r6"][:, 0:8], in_=g["ps_g"][:, 512:520]))
        sch.vop(lambda e: e.reduce_max(out=g["r6"][:, 8:9], in_=g["r6"][:, 0:8], axis=AX))
        sch.vop(lambda e: e.tensor_scalar_mul(out=g["r6"][:, 9:10], in0=g["r6"][:, 8:9],
                                              scalar1=-1.0))
        nneg = sch.cnt["ser_v"]
        sch.wait("scalar", "ser_v", nneg)
        nexp = sch.sop(lambda e: e.activation(out=g["r6"][:, 16:24], in_=g["r6"][:, 0:8],
                                              func=ACT.Exp, bias=g["r6"][:, 9:10]))
        sch.wait("vector", "ser_s", nexp)
        sch.vop(lambda e: e.reduce_sum(out=g["r6"][:, 10:11], in_=g["r6"][:, 16:24], axis=AX))
        sch.vop(lambda e: e.reduce_max(out=g["r6"][:, 11:12], in_=g["r6"][:, 16:24], axis=AX))
        sch.vop(lambda e: e.tensor_scalar(out=g["r6"][:, 24:32], in0=g["r6"][:, 16:24],
                                          scalar1=g["r6"][:, 11:12], scalar2=None,
                                          op0=ALU.is_equal))
        sch.vop(lambda e: e.scalar_tensor_tensor(out=g["r6"][:, 24:32], in0=g["r6"][:, 24:32],
                                                 scalar=-1e30, in1=g["r6"][:, 16:24],
                                                 op0=ALU.mult, op1=ALU.add))
        sch.vop(lambda e: e.reduce_max(out=g["r6"][:, 12:13], in_=g["r6"][:, 24:32], axis=AX))
        sch.vop(lambda e: e.tensor_scalar_mul(out=g["r6"][:, 13:14], in0=g["r6"][:, 10:11],
                                              scalar1=1e-6))
        sch.vop(lambda e: e.tensor_tensor(out=g["r6"][:, 14:15], in0=g["r6"][:, 11:12],
                                          in1=g["r6"][:, 12:13], op=ALU.add))
        nden = sch.vop(lambda e: e.tensor_tensor(out=g["r6"][:, 14:15], in0=g["r6"][:, 14:15],
                                                 in1=g["r6"][:, 13:14], op=ALU.add))
        sch.vop(lambda e: e.reciprocal(out=g["r6"][:, 15:16], in_=g["r6"][:, 14:15]))
        sch.vop(lambda e: e.tensor_scalar(out=g["r6"][:, 32:40], in0=g["r6"][:, 16:24],
                                          scalar1=g["r6"][:, 12:13], scalar2=None,
                                          op0=ALU.is_ge))
        nwts = sch.vop(lambda e: e.scalar_tensor_tensor(
            out=g["r6"][:, 40:48], in0=g["r6"][:, 16:24], scalar=g["r6"][:, 15:16],
            in1=g["r6"][:, 32:40], op0=ALU.mult, op1=ALU.mult))
        nhrw = sch.vop(lambda e: e.tensor_tensor(
            out=g["hrw_sb"][:].rearrange("p (x r) -> p x r", x=6),
            in0=g["ps_g"][:, 640:736].rearrange("p (x r) -> p x r", x=6),
            in1=g["r6"][:, 40:46].unsqueeze(2).broadcast_to([128, 6, 16]), op=ALU.mult))
        sch.wait("tensor", "ser_v", nhrw)

        def mk_hrt(e):
            return e.transpose(g["ps_t1"][0:96, 0:128], g["hrw_sb"][:], g["ident_sb"][:])
        sch.op("tensor", mk_hrt, "p4_tp")
        sch.wait("vector", "p4_tp", sch.cnt["p4_tp"])
        nhrT = sch.vop(lambda e: e.tensor_copy(out=g["hrwT_sb"][:], in_=g["ps_t1"][0:96, 0:128]))
        prevT[0] = nhrT
        sch.wait("tensor", "ser_v", nhrT)

        def mk_ffn2(e):
            last = None
            for oc in range(2):
                last = e.matmul(g["ps_a"][:, oc * 512:(oc + 1) * 512],
                                g["hshT_sb"][:],
                                g["sharedBT_sb"][:, oc * 512:(oc + 1) * 512],
                                start=False, stop=False)
            for oc in range(2):
                last = e.matmul(g["ps_a"][:, oc * 512:(oc + 1) * 512],
                                g["hrwT_sb"][:],
                                g["routedBT_sb"][:, oc * 512:(oc + 1) * 512],
                                start=False, stop=True)
            return last
        sch.op("tensor", mk_ffn2, "p4_mm")
        nmm2 = sch.cnt["p4_mm"]
        sch.wait("vector", "p4_mm", nmm2)
        sch.wait("vector", "out_dma", mt * 16)
        nx2 = sch.vop(lambda e: e.tensor_tensor(out=g["x2_sb"], in0=g["x1n_sb"][:, 0:1024],
                                                in1=g["ps_a"][:, 0:1024], op=ALU.add))
        sch.vop(lambda e: e.reduce_sum(out=g["sc1"][:, 0:1], in_=g["x2_sb"], axis=AX))
        sch.wait("scalar", "ser_v", nx2)
        nsq2 = sch.sop(lambda e: e.activation(out=g["x1_sb"], in_=g["x2_sb"], func=ACT.Square))
        sch.wait("vector", "ser_s", nsq2)
        sch.vop(lambda e: e.reduce_sum(out=g["sc1"][:, 1:2], in_=g["x1_sb"], axis=AX))
        sch.vop(lambda e: e.tensor_scalar_mul(out=g["sc1"][:, 2:3], in0=g["sc1"][:, 0:1],
                                              scalar1=1.0 / D_MODEL))
        sch.vop(lambda e: e.tensor_scalar_mul(out=g["sc1"][:, 3:4], in0=g["sc1"][:, 1:2],
                                              scalar1=1.0 / D_MODEL))
        sch.vop(lambda e: e.tensor_tensor(out=g["sc1"][:, 4:5], in0=g["sc1"][:, 2:3],
                                          in1=g["sc1"][:, 2:3], op=ALU.mult))
        nvar2 = sch.vop(lambda e: e.tensor_tensor(out=g["sc1"][:, 3:4], in0=g["sc1"][:, 3:4],
                                                  in1=g["sc1"][:, 4:5], op=ALU.subtract))
        sch.wait("scalar", "ser_v", nvar2)
        nsd2 = sch.sop(lambda e: e.activation(out=g["sc1"][:, 4:5], in_=g["sc1"][:, 3:4],
                                              func=ACT.Sqrt, bias=g["epscol"][:]))
        sch.wait("vector", "ser_s", nsd2)
        sch.vop(lambda e: e.reciprocal(out=g["sc1"][:, 5:6], in_=g["sc1"][:, 4:5]))
        nout = sch.vop(lambda e: e.tensor_scalar(
            out=g["x2_sb"], in0=g["x2_sb"], scalar1=g["sc1"][:, 2:3],
            scalar2=g["sc1"][:, 5:6], op0=ALU.subtract, op1=ALU.mult))
        prev[0] = nout
        sch.wait("gpsimd", "ser_v", nout)
        sch.dma("gpsimd", g["out"][t0:t0 + 128, :], g["x2_sb"][:], "out_dma")


def emit_tail(sch, g, debug_phase):
    for e in Sched.ENGINES:
        for i_ in range(4):
            sch.wait(e, f"sp{i_}", sch.cnt[f"sp{i_}"])
        for i_ in range(8):
            sch.wait(e, f"cr{i_}", sch.cnt[f"cr{i_}"])
            sch.wait(e, f"cs{i_}", sch.cnt[f"cs{i_}"])
        if debug_phase in (1, 2):
            sch.wait(e, "dbg_dma", sch.cnt["dbg_dma"])
            sch.wait(e, "dbg2_dma", sch.cnt["dbg2_dma"])
        if debug_phase >= 2:
            for i_ in range(8):
                sch.wait(e, f"y{i_}", sch.cnt[f"y{i_}"])
        if debug_phase >= 4:
            sch.wait(e, "out_dma", sch.cnt["out_dma"])



def _consts():
    tri = np.tril(np.ones((63, 63), np.float32))
    mtri_aug = np.zeros((64, 64), np.float32)
    mtri_aug[0:63, 0:63] = tri.T
    mtri_aug[63, 0:63] = 1.0
    mtri_tail = np.zeros((64, 64), np.float32)
    mtri_tail[0:2, 0:2] = np.tril(np.ones((2, 2), np.float32)).T
    mtri_tail[63, 0:2] = 1.0
    mtriD = np.zeros((64, 64), np.float32)
    mtriD[0:63, 0:63] = tri.T
    mtriD_tail = np.zeros((64, 64), np.float32)
    mtriD_tail[0:2, 0:2] = np.tril(np.ones((2, 2), np.float32)).T
    return tuple(np.vstack([m, m]) for m in (mtri_aug, mtri_tail, mtriD, mtriD_tail))


def make_inputs(core, kw):
    b, j = core // 2, core % 2
    own = slice(j * DOWN, (j + 1) * DOWN)
    x = np.asarray(kw["L0_attn_x"])
    W_in = np.asarray(kw["L0_attn_W_in"])
    mtri_aug, mtri_tail, mtriD, mtriD_tail = _consts()
    rows = np.concatenate([W_in[j * DOWN:(j + 1) * DOWN],
                           W_in[D_INNER + j * DOWN: D_INNER + (j + 1) * DOWN]], 0)
    routedAT = np.ascontiguousarray(
        np.asarray(kw["L0_ffn_routed_A"]).transpose(2, 0, 1).reshape(D_MODEL, 96))
    routedBT = np.ascontiguousarray(
        np.asarray(kw["L0_ffn_routed_B"]).transpose(0, 2, 1).reshape(96, D_MODEL))
    routerWT = np.zeros((D_MODEL, 8), np.float32)
    routerWT[:, 0:6] = np.asarray(kw["L0_ffn_router_W"]).T
    router_b = np.full((1, 8), -1e30, np.float32)
    router_b[0, 0:6] = np.asarray(kw["L0_ffn_router_b"])
    f = np.float32

    def c(a):
        return np.ascontiguousarray(a).astype(f)
    return {
        "xT": c(x[b].T),
        "x_rows": c(x[b, j * (L // 2):(j + 1) * (L // 2)]),
        "W_inT": c(rows.T),
        "convw": c(np.asarray(kw["L0_attn_conv_w"])[own, 0, :]),
        "convb": c(np.asarray(kw["L0_attn_conv_b"])[own].reshape(8, 128).T),
        "W_xT": c(np.asarray(kw["L0_attn_W_x"])[:, own].T),
        "W_dt_own": c(np.asarray(kw["L0_attn_W_dt"])[None, own, 0]),
        "b_dt_own": c(np.asarray(kw["L0_attn_b_dt"])[None, own]),
        "A_log_in": c(np.asarray(kw["L0_attn_A_log"])[None, :]),
        "D_own": c(np.asarray(kw["L0_attn_D_param"])[None, own]),
        "lnw_own": c(np.asarray(kw["L0_attn_ln_w"])[None, own]),
        "lnb_own": c(np.asarray(kw["L0_attn_ln_b"])[None, own]),
        "W_outT": c(np.asarray(kw["L0_attn_W_out"])[:, own].T),
        "baseWT": c(np.asarray(kw["L0_ffn_base_W"]).T),
        "base_b": c(np.asarray(kw["L0_ffn_base_b"])[None, :]),
        "sharedAT": c(np.asarray(kw["L0_ffn_shared_A"]).T),
        "sharedBT": c(np.asarray(kw["L0_ffn_shared_B"]).T),
        "routerWT": routerWT,
        "router_b": router_b,
        "routedAT": c(routedAT),
        "routedBT": c(routedBT),
        "mtri_aug": mtri_aug.astype(BF),
        "mtri_tail": mtri_tail.astype(BF),
        "mtriD": mtriD,
        "mtriD_tail": mtriD_tail,
        "ident": np.eye(128, dtype=f).astype(BF),
        "onesrow": np.ones((1, 128), f),
        "onesrowb": np.ones((1, 128), f).astype(BF),
        "slot": np.array([[j]], np.int32),
    }


def kernel(**kw):
    nc = build_program()
    in_maps = [make_inputs(c, kw) for c in range(8)]
    res = run_bass_kernel_spmd(nc, in_maps, core_ids=list(range(8)))
    out = np.empty((NB, L, D_MODEL), np.float32)
    for c in range(8):
        b, j = c // 2, c % 2
        out[b, j * (L // 2):(j + 1) * (L // 2)] = res.results[c]["out"]
    return out


# version 12
# speedup vs baseline: 1.2783x; 1.0881x over previous
import contextlib
import numpy as np
import ml_dtypes
import concourse.bass as bass
import concourse.mybir as mybir
from concourse.bass import ds
from concourse.bass_utils import run_bass_kernel_spmd

F32 = mybir.dt.float32
BF16 = mybir.dt.bfloat16
I32 = mybir.dt.int32
ALU = mybir.AluOpType
ACT = mybir.ActivationFunctionType
BF = ml_dtypes.bfloat16

D_MODEL = 1024
D_INNER = 2048
DOWN = 1024
S = 16
L = 8192
NB = 4
TC = 63
TILE = 126
QUAD = 4

N_P4 = 32
DEBUG_PHASE = 4


def _tiles(n, sz):
    out = []
    t = 0
    while t < n:
        ln = min(sz, n - t)
        out.append((t, ln))
        t += ln
    return out


P1T = _tiles(L, TILE)
CHUNKS = _tiles(L, TC)
NCHUNK = len(CHUNKS)
QUADS = [list(range(i, min(i + QUAD, NCHUNK))) for i in range(0, NCHUNK, QUAD)]
T128T = _tiles(L, 128)

SBUF_ALLOCS = [
    ("WinT_sb", [128, 8 * 2048], BF16),
    ("WxT_sb", [128, 8 * 33], BF16),
    ("WoutT_sb", [128, 8 * 1024], BF16),
    ("baseWT_sb", [128, 8 * 1024], BF16),
    ("sharedAT_sb", [128, 8 * 16], BF16),
    ("sharedBT_sb", [16, 1024], BF16),
    ("routerWT_sb", [128, 8 * 8], BF16),
    ("routedAT_sb", [128, 8 * 96], BF16),
    ("routedBT_sb", [96, 1024], BF16),
    ("mtri_sb", [128, 2 * 64], BF16),
    ("mtriD_sb", [128, 2 * 64], F32),
    ("ident_sb", [128, 128], BF16),
    ("onesrow_sb", [1, 128], F32),
    ("onesrowb_sb", [1, 128], BF16),
    ("convw_sb", [128, 8 * 4], BF16),
    ("convb_sb", [128, 8], F32),
    ("acol_sb", [128, 2 * S], F32),
    ("arow_sb", [1, S], F32),
    ("wdtrep_sb", [128, 512], F32),
    ("bdtrep_sb", [128, 512], F32),
    ("drep_sb", [128, DOWN], BF16),
    ("lnwrep_sb", [128, DOWN], BF16),
    ("lnbrep_sb", [128, DOWN], BF16),
    ("bbrep_sb", [1, D_MODEL], BF16),
    ("rbrow_sb", [1, 8], BF16),
    ("slot_sb", [1, 1], I32),
    ("xiraw", [128, DOWN], BF16),
    ("szt", [128, DOWN], BF16),
    ("win", [128, 8 * (TILE + 3)], BF16),

    ("xi2tile", [128, 8 * TILE], BF16),
    ("xdTbuf", [33, L], BF16),
    ("xi2hs_t", [128, 2 * 512], BF16),
    ("xdhs", [128, 2 * QUAD * 33], F32),
    ("dthsq", [128, 2 * QUAD * 512], F32),
    ("dtxhsq", [128, 2 * QUAD * 512], BF16),
    ("xi2hsq", [128, 2 * QUAD * 512], BF16),
    ("dsbq", [128, 2 * QUAD * 512], F32),
    ("em_sb", [128, 2 * QUAD * 512], BF16),
    ("ep_sb", [128, 2 * QUAD * 512], BF16),
    ("ptmp", [128, 512], BF16),
    ("wts_sb", [128, 4 * 512], BF16),
    ("hets_sb", [128, 4 * 512], BF16),
    ("carrysave", [128, 512], BF16),
    ("accq", [128, QUAD * 512], BF16),
    ("y_ld", [128, DOWN], BF16),
    ("xi2_ld", [128, DOWN], BF16),
    ("sz_ld", [128, DOWN], BF16),
    ("statsbuf", [128, 192], F32),
    ("statspeer", [128, 128], F32),
    ("gt1", [128, DOWN], BF16),
    ("gt2", [128, DOWN], BF16),
    ("og_sb", [128, DOWN], BF16),
    ("ogT_sb", [128, DOWN], BF16),
    ("hshT_sb", [16, 128], BF16),
    ("hrw_sb", [128, 96], BF16),
    ("hrwT_sb", [96, 128], BF16),
    ("r6", [128, 48], F32),
    ("sc1", [128, 16], F32),
    ("epscol", [128, 1], F32),
]

SEM_NAMES = [
    "ser_v", "ser_s", "ser_g", "ld_w", "ld_x", "mm_xz", "ev_xz", "tp_xi", "ev_xi", "cv", "tp_hs", "ev_hs",
    "mm_xd", "ev_xd", "sp_dma", "x1_dma", "bar", "p2_ld", "tp_c", "ev_c",
    "p2l0", "p2l1", "p2l2", "p2l3", "y0", "y1", "y2", "y3", "y4", "y5", "y6", "y7",
    "dt_act", "dt_dve", "mm_d", "ev_dq", "e_act", "w_dve", "cr_dma", "mm_g",
    "he_dve", "acc_dve", "y_dma", "p3_ld", "st_dve", "x2_dma", "gt_dve",
    "tp_og", "ev_og", "mm_wo", "ev_wo", "x3_dma", "p4_ld", "p4_mm", "p4_dve",
    "p4_act", "p4_tp", "out_dma", "dbg_dma", "x1b_dma", "dbg2_dma", "p3t_ld", "p4g_ld",
] + [f"cr{i}" for i in range(8)] + [f"cs{i}" for i in range(8)] + [f"sp{i}" for i in range(4)]


class Sched:
    ENGINES = ("sync", "tensor", "vector", "scalar", "gpsimd")

    def __init__(self, nc, sems):
        self.nc = nc
        self.sems = sems
        self.cnt = {}
        self.cur = None
        self.eng = None

    def begin(self, eng_name, eng):
        self.cur = eng_name
        self.eng = eng
        self.cnt = {k: 0 for k in self.sems}

    def inc(self, sem, amt):
        self.cnt[sem] += amt
        return self.cnt[sem]

    def on(self, eng_name):
        return self.cur == eng_name

    SERIAL = {"vector": "ser_v", "scalar": "ser_s", "gpsimd": "ser_g"}

    def op(self, eng_name, fn, sem=None, amt=1):
        ser = Sched.SERIAL.get(eng_name)
        if ser is not None and sem is None:
            sem, amt = ser, 1
        if self.cur == eng_name:
            if ser is not None:
                self.eng.wait_ge(self.sems[ser], self.cnt[ser])
            inst = fn(self.eng)
            if sem is not None:
                inst.then_inc(self.sems[sem], amt)
        if sem is not None:
            self.inc(sem, amt)
        return self.cnt[sem] if sem is not None else None

    def vop(self, fn):
        return self.op("vector", fn)

    def sop(self, fn):
        return self.op("scalar", fn)

    def dma(self, eng_name, out, in_, sem, **kw):
        self.wait(eng_name, sem, self.cnt[sem])
        self.op(eng_name, lambda e: e.dma_start(out=out, in_=in_, **kw), sem, 16)

    def wait(self, eng_name, sem, val=None):
        v = self.cnt[sem] if val is None else val
        assert v <= self.cnt[sem], (sem, v, self.cnt[sem])
        if self.cur == eng_name:
            self.eng.wait_ge(self.sems[sem], v)

    def wait_all(self, sem, val=None):
        for e in Sched.ENGINES:
            self.wait(e, sem, val)


def build_program(debug_phase=DEBUG_PHASE):
    nc = bass.Bass("TRN2", target_bir_lowering=False)
    dtt = nc.dram_tensor
    g = {}

    def din(name, shape, dtype):
        g[name] = dtt(name, shape, dtype, kind="ExternalInput").ap()

    din("xT", [D_MODEL, L], F32)
    din("x_rows", [L // 2, D_MODEL], F32)
    din("W_inT", [D_MODEL, 2 * DOWN], F32)
    din("convw", [DOWN, 4], F32)
    din("convb", [128, 8], F32)
    din("W_xT", [DOWN, 33], F32)
    din("W_dt_own", [1, DOWN], F32)
    din("b_dt_own", [1, DOWN], F32)
    din("A_log_in", [1, S], F32)
    din("D_own", [1, DOWN], F32)
    din("lnw_own", [1, DOWN], F32)
    din("lnb_own", [1, DOWN], F32)
    din("W_outT", [DOWN, D_MODEL], F32)
    din("baseWT", [D_MODEL, D_MODEL], F32)
    din("base_b", [1, D_MODEL], F32)
    din("sharedAT", [D_MODEL, 16], F32)
    din("sharedBT", [16, D_MODEL], F32)
    din("routerWT", [D_MODEL, 8], F32)
    din("router_b", [1, 8], F32)
    din("routedAT", [D_MODEL, 96], F32)
    din("routedBT", [96, D_MODEL], F32)
    din("mtri_aug", [128, 64], BF16)
    din("mtri_tail", [128, 64], BF16)
    din("mtriD", [128, 64], F32)
    din("mtriD_tail", [128, 64], F32)
    din("ident", [128, 128], BF16)
    din("onesrow", [1, 128], F32)
    din("onesrowb", [1, 128], BF16)
    din("selg", [128, 64 + 1024], BF16)
    din("slot", [1, 1], I32)

    g["out"] = dtt("out", [L // 2, D_MODEL], F32, kind="ExternalOutput").ap()
    g["xi2hs_hbm"] = dtt("xi2hs", [NCHUNK, 128, 512], BF16, kind="Internal").ap()
    g["xi2t_hbm"] = dtt("xi2t", [DOWN, L], BF16, kind="Internal").ap()
    g["sz_hbm"] = dtt("szsp", [L, DOWN], BF16, kind="Internal").ap()
    g["y_hbm"] = dtt("ysp", [L, DOWN], BF16, kind="Internal").ap()
    g["shr_xd"] = dtt("shr_xd", [66, L], BF16, kind="Internal", addr_space="Shared").ap()
    g["shr_st"] = dtt("shr_st", [256, 128], F32, kind="Internal", addr_space="Shared").ap()
    g["shr_out"] = dtt("shr_out", [2 * L, D_MODEL], BF16, kind="Internal", addr_space="Shared").ap()
    g["myout_hbm"] = dtt("myout", [L, D_MODEL], BF16, kind="Internal").ap()
    g["loc0_hbm"] = dtt("loc0", [L // 2, D_MODEL], BF16, kind="Internal").ap()
    g["loc1_hbm"] = dtt("loc1", [L // 2, D_MODEL], BF16, kind="Internal").ap()
    g["cbar_in"] = dtt("cbar_in", [1, 16], F32, kind="Internal").ap()
    g["cbar_out"] = dtt("cbar_out", [1, 16], F32, kind="Internal").ap()
    RG = [[0, 1], [2, 3], [4, 5], [6, 7]]

    if debug_phase == 1:
        g["d_xi2t"] = dtt("d_xi2t", [DOWN, L], BF16, kind="ExternalOutput").ap()
        g["d_sz"] = dtt("d_sz", [L, DOWN], BF16, kind="ExternalOutput").ap()
        g["d_xdT"] = dtt("d_xdT", [33, L], BF16, kind="ExternalOutput").ap()
    if debug_phase == 2:
        g["d_y"] = dtt("d_y", [L, DOWN], BF16, kind="ExternalOutput").ap()

    stack = contextlib.ExitStack()
    with stack:
        for name, shape, dty in SBUF_ALLOCS:
            g[name] = stack.enter_context(nc.sbuf_tensor(name, shape, dty))
        g["ps_a"] = stack.enter_context(nc.psum_tensor("ps_a", [128, 2048], F32))
        g["ps_t1"] = stack.enter_context(nc.psum_tensor("ps_t1", [128, 1024], BF16))
        g["ps_xd"] = stack.enter_context(nc.psum_tensor("ps_xd", [33, TILE], F32))
        g["ps_g"] = stack.enter_context(nc.psum_tensor("ps_g", [128, 2 * 512], F32))
        g["op_sb"] = g["accq"][:, 0:1024]
        g["xTb"] = g["em_sb"][:, 0:2 * 8 * TILE]
        g["cvtmp"] = g["ep_sb"][:, 0:8 * TILE]
        g["cvacc"] = g["ep_sb"][:, 8 * TILE:2 * 8 * TILE]
        g["selq_sb"] = g["gt2"][:, 0:1024]
        g["sel62_sb"] = g["og_sb"][:, 0:64]
        g["p4a"] = g["y_ld"]
        g["p4b"] = g["xi2_ld"]
        g["x1n_sb"] = g["sz_ld"]
        g["x1nT_sb"] = g["gt1"]
        g["xrow_sb"] = g["dthsq"][:, 0:1024]
        g["x1_sb"] = g["dthsq"][:, 1024:2048]
        g["x2_sb"] = g["dsbq"][:, 0:1024]
        sems = {}
        for s_ in SEM_NAMES:
            sems[s_] = stack.enter_context(nc.semaphore(s_))
        sch = Sched(nc, sems)

        def walk():
            emit_setup(sch, g, nc, debug_phase)
            emit_p1(sch, g, nc, debug_phase)
            if debug_phase >= 2:
                emit_x1(sch, g, nc, RG)
                emit_p2(sch, g, nc, debug_phase)
            if debug_phase >= 3:
                emit_p3(sch, g, nc, RG, debug_phase)
            if debug_phase >= 4:
                emit_p4(sch, g, nc, RG)
            emit_tail(sch, g, debug_phase)

        with nc.Block() as block:
            @block.sync
            def _(eng):
                sch.begin("sync", eng)
                walk()

            @block.tensor
            def _(eng):
                sch.begin("tensor", eng)
                walk()

            @block.vector
            def _(eng):
                sch.begin("vector", eng)
                walk()

            @block.scalar
            def _(eng):
                sch.begin("scalar", eng)
                walk()

            @block.gpsimd
            def _(eng):
                sch.begin("gpsimd", eng)
                walk()
    return nc


def emit_setup(sch, g, nc, debug_phase):
    G = "gpsimd"

    def ld(dst, src):
        sch.dma(G, dst, src, "ld_w")

    ld(g["WinT_sb"][:].rearrange("p (a o) -> p a o", a=8),
       g["W_inT"].rearrange("(a p) o -> p a o", p=128))
    ld(g["WxT_sb"][:].rearrange("p (a o) -> p a o", a=8),
       g["W_xT"].rearrange("(a p) o -> p a o", p=128))
    ld(g["WoutT_sb"][:].rearrange("p (a o) -> p a o", a=8),
       g["W_outT"].rearrange("(a p) o -> p a o", p=128))
    ld(g["baseWT_sb"][:].rearrange("p (a o) -> p a o", a=8),
       g["baseWT"].rearrange("(a p) o -> p a o", p=128))
    ld(g["sharedAT_sb"][:].rearrange("p (a o) -> p a o", a=8),
       g["sharedAT"].rearrange("(a p) o -> p a o", p=128))
    ld(g["sharedBT_sb"][:], g["sharedBT"])
    ld(g["routerWT_sb"][:].rearrange("p (a o) -> p a o", a=8),
       g["routerWT"].rearrange("(a p) o -> p a o", p=128))
    ld(g["routedAT_sb"][:].rearrange("p (a o) -> p a o", a=8),
       g["routedAT"].rearrange("(a p) o -> p a o", p=128))
    ld(g["routedBT_sb"][:], g["routedBT"])
    ld(g["mtri_sb"][:, 0:64], g["mtri_aug"])
    ld(g["mtri_sb"][:, 64:128], g["mtri_tail"])
    ld(g["mtriD_sb"][:, 0:64], g["mtriD"])
    ld(g["mtriD_sb"][:, 64:128], g["mtriD_tail"])
    ld(g["ident_sb"][:], g["ident"])
    ld(g["onesrow_sb"][:], g["onesrow"])
    ld(g["onesrowb_sb"][:], g["onesrowb"])
    ld(g["sel62_sb"], g["selg"][:, 0:64])
    ld(g["selq_sb"], g["selg"][:, 64:1088])
    ld(g["convw_sb"][:].rearrange("p (a k) -> p a k", a=8),
       g["convw"].rearrange("(a p) k -> p a k", p=128))
    ld(g["convb_sb"][:], g["convb"])
    ld(g["wdtrep_sb"][0:64, :], g["W_dt_own"][:, 0:512].partition_broadcast(64))
    ld(g["wdtrep_sb"][64:128, :], g["W_dt_own"][:, 512:1024].partition_broadcast(64))
    ld(g["bdtrep_sb"][0:64, :], g["b_dt_own"][:, 0:512].partition_broadcast(64))
    ld(g["bdtrep_sb"][64:128, :], g["b_dt_own"][:, 512:1024].partition_broadcast(64))
    ld(g["drep_sb"][:], g["D_own"].partition_broadcast(128))
    ld(g["lnwrep_sb"][:], g["lnw_own"].partition_broadcast(128))
    ld(g["lnbrep_sb"][:], g["lnb_own"].partition_broadcast(128))
    ld(g["bbrep_sb"][:], g["base_b"])
    ld(g["rbrow_sb"][:], g["router_b"])
    ld(g["arow_sb"][:], g["A_log_in"])
    ld(g["slot_sb"][:], g["slot"])
    n_w = sch.cnt["ld_w"]

    sch.wait("scalar", "ld_w", n_w)
    n1 = sch.sop(lambda e: e.activation(
        out=g["acol_sb"][0:1, 0:S], in_=g["arow_sb"][:], func=ACT.Exp))
    n2 = sch.sop(lambda e: e.copy(
        out=g["arow_sb"][:], in_=g["acol_sb"][0:1, 0:S]))
    sch.wait("tensor", "ser_s", n2)
    sch.op("tensor", lambda e: e.matmul(
        g["ps_a"][:, 0:S], g["onesrow_sb"][:], g["arow_sb"][:],
        start=True, stop=True), "mm_d")
    sch.wait("vector", "mm_d", sch.cnt["mm_d"])
    sch.vop(lambda e: e.tensor_scalar_mul(
        out=g["acol_sb"][:, 0:S], in0=g["ps_a"][:, 0:S], scalar1=-1.0))
    sch.vop(lambda e: e.tensor_scalar_mul(
        out=g["acol_sb"][:, S:2 * S], in0=g["ps_a"][:, 0:S], scalar1=1.0))
    sch.vop(lambda e: e.memset(g["carrysave"][:], 0.0))
    sch.vop(lambda e: e.memset(g["epscol"][:], 1e-5))
    nz = sch.vop(lambda e: e.memset(g["xi2hsq"][:, 0:128], 0.0))
    sch.wait("tensor", "ser_v", nz)
    sch.wait("tensor", "ld_w", sch.cnt["ld_w"])

    def mk_zps(e):
        last = None
        for kb in range(8):
            last = e.transpose(g["ps_t1"][:, kb * 128:(kb + 1) * 128],
                               g["xi2hsq"][:, 0:128], g["ident_sb"][:])
        return last
    sch.op("tensor", mk_zps, "tp_xi")
    sch.vop(lambda e: e.memset(g["win"][:], 0.0))
    n_ms = sch.vop(lambda e: e.memset(g["statsbuf"][:], 0.0))
    sch.wait("gpsimd", "ser_v", n_ms)
    sch.dma("gpsimd", g["cbar_in"], g["statsbuf"][0:1, 0:16], "ld_w")
    sch.wait_all("ser_v", n_ms)
    sch.wait_all("ld_w", sch.cnt["ld_w"])


def emit_p1(sch, g, nc, debug_phase):
    sems = sch.sems
    r = {"mm_xz": [0], "silu_z": [0], "xi_cp": [0], "wincp": [0], "conv": [0],
         "silu2": [0], "ev_xd": [0], "ev_hs": [0], "sp": [(0, 0, 0, 0)], "dbg": [0],
         "spd": []}
    for m, (t0, ln) in enumerate(P1T):
        buf = m % 2
        sch.wait("gpsimd", "mm_xz", r["mm_xz"][max(0, m - 1)])
        sch.dma("gpsimd",
                g["xTb"][:, buf * 8 * TILE:(buf + 1) * 8 * TILE]
                .rearrange("p (a t) -> p a t", a=8)[:, :, 0:ln],
                g["xT"][:, t0:t0 + ln].rearrange("(a p) t -> p a t", p=128),
                "ld_x")
        n_ldx = sch.cnt["ld_x"]
        sch.wait("tensor", "ld_x", n_ldx)
        sch.wait("tensor", "ser_s", r["silu_z"][m])
        sch.wait("tensor", "ser_v", r["xi_cp"][m])

        def mk_xz(e, buf=buf, ln=ln):
            last = None
            for k in range(8):
                lhs = g["xTb"][:, buf * 8 * TILE + k * TILE: buf * 8 * TILE + k * TILE + ln]
                for oc in range(4):
                    last = e.matmul(
                        g["ps_a"][0:ln, oc * 512:(oc + 1) * 512], lhs,
                        g["WinT_sb"][:, k * 2048 + oc * 512: k * 2048 + (oc + 1) * 512],
                        start=(k == 0), stop=(k == 7))
            return last
        sch.op("tensor", mk_xz, "mm_xz")
        r["mm_xz"].append(sch.cnt["mm_xz"])
        n_mmxz = sch.cnt["mm_xz"]
        sch.wait("scalar", "mm_xz", n_mmxz)
        for i_ in range(4):
            sch.wait("scalar", f"sp{i_}", r["sp"][m][i_] if m > 0 else 0)
        if debug_phase == 1:
            sch.wait("scalar", "dbg_dma", r["dbg"][m])
        ns = sch.sop(lambda e, ln=ln: e.activation(
            out=g["szt"][0:ln, :], in_=g["ps_a"][0:ln, DOWN:2 * DOWN], func=ACT.Silu))
        r["silu_z"].append(ns)
        sch.wait("vector", "mm_xz", n_mmxz)
        nv = sch.vop(lambda e, ln=ln: e.tensor_copy(
            out=g["xiraw"][0:ln, :], in_=g["ps_a"][0:ln, 0:DOWN]))
        r["xi_cp"].append(nv)
        sch.wait("sync", "ser_s", ns)
        sch.dma("sync", g["sz_hbm"][t0:t0 + ln, :], g["szt"][0:ln, :], f"sp{m % 4}")
        if debug_phase == 1:
            sch.dma("sync", g["d_sz"][t0:t0 + ln, :], g["szt"][0:ln, :], "dbg_dma")
        sch.wait("tensor", "ser_v", nv)
        sch.wait("tensor", "ser_v", r["ev_hs"][m])

        def mk_tp(e, ln=ln):
            last = None
            for kb in range(8):
                last = e.transpose(g["ps_t1"][:, kb * 128: kb * 128 + ln],
                                   g["xiraw"][0:ln, kb * 128:(kb + 1) * 128],
                                   g["ident_sb"][0:ln, 0:ln])
            return last
        sch.op("tensor", mk_tp, "tp_xi")
        n_tpxi = sch.cnt["tp_xi"]
        sch.wait("vector", "tp_xi", n_tpxi)
        nw = sch.vop(lambda e, ln=ln: e.tensor_copy(
            out=g["win"][:].rearrange("p (a t) -> p a t", a=8)[:, :, 3:3 + ln],
            in_=g["ps_t1"][:].rearrange("p (a t) -> p a t", a=8)[:, :, 0:ln]))
        r["wincp"].append(nw)
        wv = g["win"][:].rearrange("p (a t) -> p a t", a=8)
        cwv = g["convw_sb"][:].rearrange("p (a k) -> p a k", a=8)

        def cv_mul0(e, ln=ln):
            return e.tensor_tensor(
                out=g["cvacc"][:].rearrange("p (a t) -> p a t", a=8)[:, :, 0:ln],
                in0=wv[:, :, 0:ln],
                in1=cwv[:, :, 0:1].broadcast_to([128, 8, ln]), op=ALU.mult)
        sch.vop(cv_mul0)
        for k in (1, 2, 3):
            def cv_mul(e, k=k, ln=ln):
                return e.tensor_tensor(
                    out=g["cvtmp"][:].rearrange("p (a t) -> p a t", a=8)[:, :, 0:ln],
                    in0=wv[:, :, k:k + ln],
                    in1=cwv[:, :, k:k + 1].broadcast_to([128, 8, ln]), op=ALU.mult)
            sch.vop(cv_mul)

            def cv_add(e, ln=ln):
                av = g["cvacc"][:].rearrange("p (a t) -> p a t", a=8)[:, :, 0:ln]
                return e.tensor_tensor(out=av, in0=av,
                                       in1=g["cvtmp"][:].rearrange("p (a t) -> p a t", a=8)[:, :, 0:ln],
                                       op=ALU.add)
            sch.vop(cv_add)
        nc_conv = sch.vop(lambda e, ln=ln: e.tensor_copy(
            out=g["win"][:].rearrange("p (a t) -> p a t", a=8)[:, :, 0:3],
            in_=wv[:, :, ln:ln + 3]))
        r["conv"].append(nc_conv)
        sch.wait("scalar", "ser_v", nc_conv)
        sch.wait("scalar", "mm_xd", m)
        sch.wait("scalar", "tp_hs", m)
        sch.wait("scalar", "sp_dma", r["spd"][m - 1] if m > 0 else 0)
        for i_ in range(4):
            sch.wait("scalar", f"sp{i_}", r["sp"][m][i_] if m > 0 else 0)

        def mk_silu(e, ln=ln):
            last = None
            for kb in range(8):
                last = e.activation(out=g["xi2tile"][:, kb * TILE: kb * TILE + ln],
                                    in_=g["cvacc"][:, kb * TILE: kb * TILE + ln],
                                    func=ACT.Silu, bias=g["convb_sb"][:, kb:kb + 1])
            return last
        ns2 = sch.sop(mk_silu)
        r["silu2"].append(ns2)
        sch.wait("tensor", "ser_s", ns2)
        sch.wait("tensor", "ser_v", r["ev_xd"][m])

        def mk_xd(e, ln=ln):
            last = None
            for kb in range(8):
                last = e.matmul(g["ps_xd"][:, 0:ln],
                                g["WxT_sb"][:, kb * 33:(kb + 1) * 33],
                                g["xi2tile"][:, kb * TILE: kb * TILE + ln],
                                start=(kb == 0), stop=(kb == 7))
            return last
        sch.op("tensor", mk_xd, "mm_xd")
        n_mmxd = sch.cnt["mm_xd"]
        sch.wait("vector", "mm_xd", n_mmxd)
        nxd = sch.vop(lambda e, t0=t0, ln=ln: e.tensor_copy(
            out=g["xdTbuf"][:, t0:t0 + ln], in_=g["ps_xd"][:, 0:ln]))
        r["ev_xd"].append(nxd)
        sch.wait("sync", "ser_s", ns2)
        sch.dma("sync",
                g["xi2t_hbm"][:, t0:t0 + ln].rearrange("(a p) t -> p a t", p=128),
                g["xi2tile"][:].rearrange("p (a t) -> p a t", a=8)[:, :, 0:ln],
                "sp_dma")
        if debug_phase == 1:
            sch.dma("sync",
                    g["d_xi2t"][:, t0:t0 + ln].rearrange("(a p) t -> p a t", p=128),
                    g["xi2tile"][:].rearrange("p (a t) -> p a t", a=8)[:, :, 0:ln],
                    "dbg_dma")
        nch = 2 if ln == TILE else 1
        cl = TC if ln == TILE else ln
        sch.wait("tensor", "mm_xd", n_mmxd)
        sch.wait("tensor", "ser_v", nw)

        def mk_hst(e, nch=nch, cl=cl):
            last = None
            for ci in range(nch):
                for h in range(2):
                    for db in range(4):
                        kb = h * 4 + db
                        last = e.transpose(
                            g["ps_t1"][64 * h: 64 * h + cl,
                                       ci * 512 + db * 128: ci * 512 + db * 128 + 128],
                            g["xi2tile"][:, kb * TILE + ci * TC: kb * TILE + ci * TC + cl],
                            g["ident_sb"][:])
            return last
        sch.op("tensor", mk_hst, "tp_hs")
        n_tph = sch.cnt["tp_hs"]
        sch.wait("vector", "tp_hs", n_tph)
        for i_ in range(4):
            sch.wait("vector", f"sp{i_}", r["sp"][m][i_] if m > 0 else 0)
        nhs = sch.vop(lambda e, nch=nch: e.tensor_copy(
            out=g["xi2hs_t"][:, 0:nch * 512], in_=g["ps_t1"][:, 0:nch * 512]))
        r["ev_hs"].append(nhs)
        sch.wait("sync", "ser_v", nhs)
        for ci in range(nch):
            sch.dma("sync", g["xi2hs_hbm"][2 * m + ci],
                    g["xi2hs_t"][:, ci * 512:(ci + 1) * 512], f"sp{m % 4}")
        r["sp"].append(tuple(sch.cnt[f"sp{i}"] for i in range(4)))
        r["spd"].append(sch.cnt["sp_dma"])
        r["dbg"].append(sch.cnt["dbg_dma"])
    if debug_phase == 1:
        nxd_all = r["ev_xd"][-1]
        sch.wait("sync", "ser_v", nxd_all)
        sch.dma("sync", g["d_xdT"], g["xdTbuf"][:], "dbg_dma")


def emit_x1(sch, g, nc, RG):
    sems = sch.sems
    n_evxd = sch.cnt["ev_xd"]
    sch.wait("sync", "ev_xd", n_evxd)
    if sch.on("sync"):
        myslot = nc.values_load(g["slot_sb"][0:1, 0:1], engines=[mybir.EngineType.SP],
                                min_val=0, max_val=1, skip_runtime_bounds_check=True)
        g["myslot_sync"] = myslot
        nc.sync.dma_start(out=g["shr_xd"][0:33, :], in_=g["xdTbuf"][:],
                          cond=(myslot == 0)).then_inc(sems["x1_dma"], 16)
        nc.sync.dma_start(out=g["shr_xd"][33:66, :], in_=g["xdTbuf"][:],
                          cond=myslot).then_inc(sems["x1_dma"], 16)
    sch.inc("x1_dma", 32)
    sch.wait("gpsimd", "x1_dma", sch.cnt["x1_dma"])
    sch.op("gpsimd", lambda e: e.collective_compute(
        "AllReduce", ALU.add, ins=[g["cbar_in"][:]], outs=[g["cbar_out"][:]],
        replica_groups=RG), "bar")
    sch.wait("sync", "bar", sch.cnt["bar"])
    if sch.on("sync"):
        myslot = g["myslot_sync"]
        for half, dst in ((0, g["em_sb"]), (1, g["ep_sb"])):
            nc.sync.dma_start(out=dst[0:33, 0:4096],
                              in_=g["shr_xd"][33:66, half * 4096:(half + 1) * 4096],
                              cond=(myslot == 0)).then_inc(sems["x1_dma"], 16)
            nc.sync.dma_start(out=dst[0:33, 0:4096],
                              in_=g["shr_xd"][0:33, half * 4096:(half + 1) * 4096],
                              cond=myslot).then_inc(sems["x1_dma"], 16)
    sch.inc("x1_dma", 64)
    sch.wait("vector", "x1_dma", sch.cnt["x1_dma"])
    sch.vop(lambda e: e.tensor_tensor(out=g["xdTbuf"][:, 0:4096], in0=g["xdTbuf"][:, 0:4096],
                                      in1=g["em_sb"][0:33, 0:4096], op=ALU.add))
    nadd = sch.vop(lambda e: e.tensor_tensor(out=g["xdTbuf"][:, 4096:8192],
                                             in0=g["xdTbuf"][:, 4096:8192],
                                             in1=g["ep_sb"][0:33, 0:4096], op=ALU.add))
    sch.wait_all("ser_v", nadd)


def emit_p2(sch, g, nc, debug_phase):
    sems = sch.sems
    for i_ in range(4):
        sch.wait("gpsimd", f"sp{i_}", sch.cnt[f"sp{i_}"])
    he_cnt = [0] * S
    slot_rd = {}
    save_cnt = {}
    wslot_g = {}
    psg_rd = [0, 0]
    em_war = [0, 0]
    last_xdev = [0]
    g_cnt = [0] * S
    q_dtx = [0, 0]
    q_mmd = [0, 0]
    q_eact = [0, 0]
    q_evq = [0, 0]
    q_ydma_prev = [[0, 0] for _ in range(QUAD)]
    ld_cnt = [[0] * QUAD, [0] * QUAD]

    def emit_loads(qi2):
        rot2 = qi2 % 2
        sch.wait("gpsimd", "ser_v", q_dtx[rot2])
        for ci2, c2 in enumerate(QUADS[qi2]):
            sch.dma("gpsimd",
                    g["xi2hsq"][:, (rot2 * QUAD + ci2) * 512:(rot2 * QUAD + ci2 + 1) * 512],
                    g["xi2hs_hbm"][c2], f"p2l{ci2}")
            ld_cnt[rot2][ci2] = sch.cnt[f"p2l{ci2}"]

    emit_loads(0)
    for qi, quad in enumerate(QUADS):
        rot = qi % 2
        qn = len(quad)
        for ci, c in enumerate(quad):
            t0, cl = CHUNKS[c]
            sch.wait("tensor", "ser_v", last_xdev[0])

            def mk_xdt(e, t0=t0, cl=cl):
                e.transpose(g["ps_t1"][0:cl, 0:33], g["xdTbuf"][:, t0:t0 + cl],
                            g["ident_sb"][0:33, 0:33])
                return e.transpose(g["ps_t1"][64:64 + cl, 0:33],
                                   g["xdTbuf"][:, t0:t0 + cl],
                                   g["ident_sb"][0:33, 0:33])
            sch.op("tensor", mk_xdt, "tp_c")
            sch.wait("vector", "tp_c", sch.cnt["tp_c"])
            last_xdev[0] = sch.vop(lambda e, ci=ci, rot=rot: e.tensor_copy(
                out=g["xdhs"][:, (rot * QUAD + ci) * 33:(rot * QUAD + ci + 1) * 33],
                in_=g["ps_t1"][:, 0:33]))
        sch.wait("vector", "mm_d", q_mmd[rot])
        dtx_last = 0
        for ci, c in enumerate(quad):
            slot = rot * QUAD + ci
            dsl = g["dthsq"][:, slot * 512:(slot + 1) * 512]
            dtin_col = g["xdhs"][:, slot * 33: slot * 33 + 1]
            n1 = sch.vop(lambda e, dsl=dsl, dtin_col=dtin_col: e.scalar_tensor_tensor(
                out=dsl, in0=g["wdtrep_sb"][:], scalar=dtin_col, in1=g["bdtrep_sb"][:],
                op0=ALU.mult, op1=ALU.add))
            sch.wait("scalar", "ser_v", n1)
            sch.sop(lambda e, dsl=dsl: e.activation(out=dsl, in_=dsl, func=ACT.Exp))
            n2 = sch.sop(lambda e, dsl=dsl: e.activation(out=dsl, in_=dsl,
                                                         func=ACT.Ln, bias=1.0))
            sch.wait("vector", "ser_s", n2)
            sch.vop(lambda e, dsl=dsl: e.tensor_scalar(
                out=dsl, in0=dsl, scalar1=1e-4, scalar2=10.0, op0=ALU.max, op1=ALU.min))
            sch.wait("vector", f"p2l{ci}", ld_cnt[rot][ci])
            dtx_last = sch.vop(lambda e, slot=slot, dsl=dsl: e.tensor_tensor(
                out=g["dtxhsq"][:, slot * 512:(slot + 1) * 512], in0=dsl,
                in1=g["xi2hsq"][:, slot * 512:(slot + 1) * 512], op=ALU.mult))
        q_dtx[rot] = dtx_last
        if qi + 1 < len(QUADS):
            emit_loads(qi + 1)
        sch.wait("tensor", "ser_v", dtx_last)
        sch.wait("tensor", "ser_v", q_evq[rot])
        sch.wait("tensor", "ser_v", q_evq[1 - rot])

        def mk_dmm(e, quad=quad, rot=rot):
            last = None
            for ci, c in enumerate(quad):
                tail = (CHUNKS[c][1] != TC)
                lhsc = g["mtriD_sb"][:, 64:128] if tail else g["mtriD_sb"][:, 0:64]
                for h in range(2):
                    last = e.matmul(
                        g["ps_a"][64 * h:64 * h + 64, ci * 512:(ci + 1) * 512],
                        lhsc[64 * h:64 * h + 64, :],
                        g["dthsq"][64 * h:64 * h + 64,
                                   (rot * QUAD + ci) * 512:(rot * QUAD + ci + 1) * 512],
                        start=True, stop=True, tile_position=(64 * h, 64 * h))
            return last
        sch.op("tensor", mk_dmm, "mm_d")
        q_mmd[rot] = sch.cnt["mm_d"]
        sch.wait("vector", "mm_d", q_mmd[rot])
        sch.wait("vector", "ser_s", q_eact[rot])
        nev = sch.vop(lambda e, rot=rot, qn=qn: e.tensor_copy(
            out=g["dsbq"][:, rot * 2048: rot * 2048 + qn * 512],
            in_=g["ps_a"][:, 0:qn * 512]))
        q_evq[rot] = nev
        for sp in range(0, S, 2):
            lanes = (sp, sp + 1)
            sch.wait("scalar", "ser_v", nev)
            sch.wait("scalar", "ser_v", max(em_war))
            n_eact = 0
            for li, s_ in enumerate(lanes):
                sch.sop(lambda e, li=li, s_=s_, rot=rot, qn=qn: e.activation(
                    out=g["em_sb"][:, li * 2048: li * 2048 + qn * 512],
                    in_=g["dsbq"][:, rot * 2048: rot * 2048 + qn * 512],
                    func=ACT.Exp, scale=g["acol_sb"][:, S + s_:S + s_ + 1]))
                n_eact = sch.sop(lambda e, li=li, s_=s_, rot=rot, qn=qn: e.activation(
                    out=g["ep_sb"][:, li * 2048: li * 2048 + qn * 512],
                    in_=g["dsbq"][:, rot * 2048: rot * 2048 + qn * 512],
                    func=ACT.Exp, scale=g["acol_sb"][:, s_:s_ + 1]))
            w_pipe = {s2: {} for s2 in lanes}
            def emit_pw(ci2):
                c2 = quad[ci2]
                slot2 = rot * QUAD + ci2
                for li2, s2 in enumerate(lanes):
                    sch.wait("vector", "ser_s", n_eact)
                    wslot_pre = li2 * 2 + quad[ci2] % 2
                    sch.wait("vector", "mm_g", wslot_g.get(wslot_pre, 0))
                    Bcol = g["xdhs"][:, slot2 * 33 + 1 + s2: slot2 * 33 + 2 + s2]
                    wslot = li2 * 2 + c2 % 2
                    sch.vop(lambda e, slot2=slot2, Bcol=Bcol: e.tensor_scalar(
                        out=g["ptmp"][:], in0=g["dtxhsq"][:, slot2 * 512:(slot2 + 1) * 512],
                        scalar1=Bcol, scalar2=10.0, op0=ALU.mult, op1=ALU.min))
                    nw2 = sch.vop(lambda e, li2=li2, ci2=ci2, wslot=wslot: e.scalar_tensor_tensor(
                        out=g["wts_sb"][:, wslot * 512:(wslot + 1) * 512], in0=g["ptmp"][:],
                        scalar=-10.0,
                        in1=g["em_sb"][:, li2 * 2048 + ci2 * 512: li2 * 2048 + (ci2 + 1) * 512],
                        op0=ALU.max, op1=ALU.mult))
                    w_pipe[s2][ci2] = nw2

            emit_pw(0)
            if qn > 1:
                emit_pw(1)
            for ci, c in enumerate(quad):
                t0, cl = CHUNKS[c]
                tail = (cl != TC)
                slot = rot * QUAD + ci
                qstart = (ci == 0)
                crs = f"cr{sp // 2}"
                ceng = "sync" if (sp // 2) % 2 == 0 else "gpsimd"
                wslot0 = c % 2
                n_cr = sch.cnt[crs]
                for li, s_ in enumerate(lanes):
                    wslot = li * 2 + c % 2
                    wt = g["wts_sb"][:, wslot * 512:(wslot + 1) * 512]
                    if qstart:
                        sel = g["selq_sb"][:, s_ * 64:(s_ + 1) * 64]
                        csrc = g["carrysave"][:]
                        sv = save_cnt.get(sp)
                        if sv is not None:
                            sch.wait("tensor", sv[0], sv[1])
                    else:
                        pslot = li * 2 + (c - 1) % 2
                        sel = g["sel62_sb"]
                        csrc = g["hets_sb"][:, pslot * 512:(pslot + 1) * 512]
                    sch.wait("tensor", "ser_v", w_pipe[s_][ci])
                    sch.wait("tensor", "ser_v", max(he_cnt[s_], psg_rd[li]))
                    lhsg = g["mtri_sb"][:, 64:128] if tail else g["mtri_sb"][:, 0:64]

                    def mk_g(e, li=li, lhsg=lhsg, wt=wt, sel=sel, csrc=csrc):
                        last = None
                        for h in range(2):
                            pg = g["ps_g"][64 * h:64 * h + 64,
                                           li * 512:(li + 1) * 512]
                            e.matmul(pg, lhsg[64 * h:64 * h + 64, :],
                                     wt[64 * h:64 * h + 64, :],
                                     start=True, stop=False,
                                     tile_position=(64 * h, 64 * h))
                            last = e.matmul(
                                pg, sel[64 * h:64 * h + 64, :],
                                csrc[64 * h:64 * h + 64, :],
                                start=False, stop=True,
                                tile_position=(64 * h, 64 * h))
                        return last
                    sch.op("tensor", mk_g, "mm_g")
                    g_cnt[s_] = sch.cnt["mm_g"]
                    wslot_g[wslot] = sch.cnt["mm_g"]
                    if not qstart:
                        slot_rd[pslot] = ("mm_g", sch.cnt["mm_g"])
                for li, s_ in enumerate(lanes):
                    hslot = li * 2 + c % 2
                    het = g["hets_sb"][:, hslot * 512:(hslot + 1) * 512]
                    sch.wait("vector", "mm_g", g_cnt[s_])
                    sch.wait("vector", crs, n_cr)
                    if hslot in slot_rd:
                        sr = slot_rd[hslot]
                        sch.wait("vector", sr[0], sr[1])
                    nhe = sch.vop(lambda e, het=het, li=li, ci=ci: e.tensor_tensor(
                        out=het,
                        in0=g["ep_sb"][:, li * 2048 + ci * 512: li * 2048 + (ci + 1) * 512],
                        in1=g["ps_g"][:, li * 512:(li + 1) * 512], op=ALU.mult))
                    he_cnt[s_] = nhe
                    psg_rd[li] = nhe
                    Ccol = g["xdhs"][:, slot * 33 + 17 + s_: slot * 33 + 18 + s_]
                    adst = g["accq"][:, ci * 512:(ci + 1) * 512]
                    if s_ == 0:
                        sch.wait("vector", f"y{ci * 2}", q_ydma_prev[ci][0])
                        sch.wait("vector", f"y{ci * 2 + 1}", q_ydma_prev[ci][1])
                        if debug_phase == 2:
                            sch.wait("vector", "dbg2_dma", sch.cnt["dbg2_dma"])
                        sch.vop(lambda e, adst=adst, het=het, Ccol=Ccol: e.tensor_scalar(
                            out=adst, in0=het, scalar1=Ccol, scalar2=None, op0=ALU.mult))
                    else:
                        sch.vop(lambda e, adst=adst, het=het, Ccol=Ccol:
                                e.scalar_tensor_tensor(
                                    out=adst, in0=het, scalar=Ccol, in1=adst,
                                    op0=ALU.mult, op1=ALU.add))
                if ci == qn - 1 and c != NCHUNK - 1:
                    sch.wait(ceng, "ser_v", max(he_cnt[sp], he_cnt[sp + 1]))
                    hsv2 = g["hets_sb"][:].rearrange("p (sl t) -> p sl t", sl=4)
                    for h in range(2):
                        sch.dma(ceng,
                                g["carrysave"][64 * h + sp: 64 * h + sp + 2, :]
                                .rearrange("p (a t) -> p a t", a=1),
                                hsv2[64 * h + cl - 1: 64 * h + cl, wslot0::2, :],
                                crs)
                    slot_rd[wslot0] = (crs, sch.cnt[crs])
                    slot_rd[wslot0 + 2] = (crs, sch.cnt[crs])
                    save_cnt[sp] = (crs, sch.cnt[crs])
                if ci + 2 < qn:
                    emit_pw(ci + 2)
            em_war[0] = he_cnt[sp]
            em_war[1] = he_cnt[sp + 1]
            q_eact[rot] = n_eact
        nacc_fin = sch.cnt["ser_v"]
        sch.wait("gpsimd", "ser_v", nacc_fin)
        for ci, c in enumerate(quad):
            t0, cl = CHUNKS[c]
            acc_fin = g["accq"][:, ci * 512:(ci + 1) * 512]
            sch.dma("gpsimd", g["y_hbm"][t0:t0 + cl, 0:512], acc_fin[0:cl, :],
                    f"y{ci * 2}")
            sch.dma("gpsimd", g["y_hbm"][t0:t0 + cl, 512:1024], acc_fin[64:64 + cl, :],
                    f"y{ci * 2 + 1}")
            q_ydma_prev[ci] = [sch.cnt[f"y{ci * 2}"], sch.cnt[f"y{ci * 2 + 1}"]]
            if debug_phase == 2:
                sch.dma("gpsimd", g["d_y"][t0:t0 + cl, 0:512], acc_fin[0:cl, :], "dbg2_dma")
                sch.dma("gpsimd", g["d_y"][t0:t0 + cl, 512:1024], acc_fin[64:64 + cl, :], "dbg2_dma")


def emit_p3(sch, g, nc, RG, debug_phase):
    sems = sch.sems
    for i_ in range(8):
        sch.wait("gpsimd", f"y{i_}", sch.cnt[f"y{i_}"])
    for e_ in ("gpsimd", "sync"):
        sch.wait(e_, "sp_dma", sch.cnt["sp_dma"])
        for i_ in range(4):
            sch.wait(e_, f"sp{i_}", sch.cnt[f"sp{i_}"])
    sch.wait("scalar", "mm_d", sch.cnt["mm_d"])
    st_tile = []
    for mt, (t0, ln) in enumerate(T128T):
        sch.wait("gpsimd", "ser_v", st_tile[-1] if st_tile else 0)
        sch.dma("gpsimd", g["y_ld"][0:ln, :], g["y_hbm"][t0:t0 + ln, :], "p3_ld")
        n_ld = sch.cnt["p3_ld"]
        sch.wait("vector", "p3_ld", n_ld)
        sch.vop(lambda e, mt=mt, ln=ln: e.reduce_sum(
            out=g["statsbuf"][0:ln, 2 * mt:2 * mt + 1], in_=g["y_ld"][0:ln, :],
            axis=mybir.AxisListType.X))
        sch.wait("scalar", "p3_ld", n_ld)
        nsq = sch.sop(lambda e, ln=ln: e.activation(
            out=g["x1_sb"][0:ln, :], in_=g["y_ld"][0:ln, :], func=ACT.Square))
        sch.wait("vector", "ser_s", nsq)
        nst = sch.vop(lambda e, mt=mt, ln=ln: e.reduce_sum(
            out=g["statsbuf"][0:ln, 2 * mt + 1:2 * mt + 2], in_=g["x1_sb"][0:ln, :],
            axis=mybir.AxisListType.X))
        st_tile.append(nst)
    sch.wait("sync", "ser_v", st_tile[-1])
    if sch.on("sync"):
        myslot = nc.values_load(g["slot_sb"][0:1, 0:1], engines=[mybir.EngineType.SP],
                                min_val=0, max_val=1, skip_runtime_bounds_check=True)
        g["myslot_sync"] = myslot
        nc.sync.dma_start(out=g["shr_st"][0:128, :], in_=g["statsbuf"][:, 0:128],
                          cond=(myslot == 0)).then_inc(sems["x2_dma"], 16)
        nc.sync.dma_start(out=g["shr_st"][128:256, :], in_=g["statsbuf"][:, 0:128],
                          cond=myslot).then_inc(sems["x2_dma"], 16)
    sch.inc("x2_dma", 32)
    sch.wait("gpsimd", "x2_dma", sch.cnt["x2_dma"])
    sch.op("gpsimd", lambda e: e.collective_compute(
        "AllReduce", ALU.add, ins=[g["cbar_in"][:]], outs=[g["cbar_out"][:]],
        replica_groups=RG), "bar")
    sch.wait("sync", "bar", sch.cnt["bar"])
    if sch.on("sync"):
        myslot = g["myslot_sync"]
        nc.sync.dma_start(out=g["statspeer"][:], in_=g["shr_st"][128:256, :],
                          cond=(myslot == 0)).then_inc(sems["x2_dma"], 16)
        nc.sync.dma_start(out=g["statspeer"][:], in_=g["shr_st"][0:128, :],
                          cond=myslot).then_inc(sems["x2_dma"], 16)
    sch.inc("x2_dma", 32)
    sch.wait("vector", "x2_dma", sch.cnt["x2_dma"])
    sv = g["statsbuf"][:, 0:128].rearrange("p (m two) -> p m two", two=2)
    sch.vop(lambda e: e.tensor_tensor(out=g["statsbuf"][:, 0:128],
                                      in0=g["statsbuf"][:, 0:128],
                                      in1=g["statspeer"][:, 0:128], op=ALU.add))
    sch.vop(lambda e: e.tensor_scalar_mul(
        out=g["statsbuf"][:, 128:192], in0=sv[:, :, 0], scalar1=1.0 / D_INNER))
    sch.vop(lambda e: e.tensor_scalar_mul(
        out=g["statspeer"][:, 0:64], in0=sv[:, :, 1], scalar1=1.0 / D_INNER))
    sch.vop(lambda e: e.tensor_tensor(out=g["statspeer"][:, 64:128],
                                      in0=g["statsbuf"][:, 128:192],
                                      in1=g["statsbuf"][:, 128:192], op=ALU.mult))
    nvar = sch.vop(lambda e: e.tensor_tensor(out=g["statspeer"][:, 0:64],
                                             in0=g["statspeer"][:, 0:64],
                                             in1=g["statspeer"][:, 64:128], op=ALU.subtract))
    sch.wait("scalar", "ser_v", nvar)
    nsq2 = sch.sop(lambda e: e.activation(out=g["statspeer"][:, 64:128],
                                          in_=g["statspeer"][:, 0:64],
                                          func=ACT.Sqrt, bias=g["epscol"][:]))
    sch.wait("vector", "ser_s", nsq2)
    nrst = sch.vop(lambda e: e.reciprocal(out=g["statsbuf"][:, 0:64],
                                          in_=g["statspeer"][:, 64:128]))
    gt_done = []
    ogT_prev = [0]
    for mt, (t0, ln) in enumerate(T128T):
        sch.wait("gpsimd", "ser_v", gt_done[-1] if gt_done else nrst)
        sch.dma("gpsimd", g["y_ld"][0:ln, :], g["y_hbm"][t0:t0 + ln, :], "p3_ld")
        sch.dma("gpsimd", g["sz_ld"][0:ln, :], g["sz_hbm"][t0:t0 + ln, :], "p3_ld")
        sch.wait("sync", "ser_v", gt_done[-1] if gt_done else nrst)
        sch.wait("sync", "p3t_ld", sch.cnt["p3t_ld"])
        if sch.on("sync"):
            for kb in range(8):
                sch.eng.dma_start_transpose(
                    g["xi2_ld"][0:ln, kb * 128:(kb + 1) * 128],
                    g["xi2t_hbm"][kb * 128:(kb + 1) * 128, t0:t0 + ln]).then_inc(sems["p3t_ld"], 16)
        sch.inc("p3t_ld", 8 * 16)
        n_ld = sch.cnt["p3_ld"]
        sch.wait("vector", "p3_ld", n_ld)
        sch.wait("vector", "p3t_ld", sch.cnt["p3t_ld"])
        sch.wait("vector", "ser_v", nrst)
        mu_c = g["statsbuf"][:, 128 + mt:129 + mt]
        rs_c = g["statsbuf"][:, mt:mt + 1]
        sch.vop(lambda e, ln=ln, mu_c=mu_c, rs_c=rs_c: e.tensor_scalar(
            out=g["gt1"][0:ln, :], in0=g["y_ld"][0:ln, :], scalar1=mu_c[0:ln, :],
            scalar2=rs_c[0:ln, :], op0=ALU.subtract, op1=ALU.mult))
        sch.vop(lambda e, ln=ln: e.tensor_tensor(
            out=g["gt1"][0:ln, :], in0=g["gt1"][0:ln, :], in1=g["lnwrep_sb"][0:ln, :],
            op=ALU.mult))
        sch.vop(lambda e, ln=ln: e.tensor_tensor(
            out=g["gt2"][0:ln, :], in0=g["xi2_ld"][0:ln, :], in1=g["drep_sb"][0:ln, :],
            op=ALU.mult))
        sch.vop(lambda e, ln=ln: e.tensor_tensor(
            out=g["gt1"][0:ln, :], in0=g["gt1"][0:ln, :], in1=g["gt2"][0:ln, :],
            op=ALU.add))
        sch.vop(lambda e, ln=ln: e.tensor_tensor(
            out=g["gt1"][0:ln, :], in0=g["gt1"][0:ln, :], in1=g["lnbrep_sb"][0:ln, :],
            op=ALU.add))
        nog = sch.vop(lambda e, ln=ln: e.tensor_tensor(
            out=g["og_sb"][0:ln, :], in0=g["gt1"][0:ln, :], in1=g["sz_ld"][0:ln, :],
            op=ALU.mult))
        sch.wait("tensor", "ser_v", nog)
        sch.wait("tensor", "ser_v", ogT_prev[0])

        def mk_ogt(e, ln=ln):
            last = None
            for kb in range(8):
                last = e.transpose(g["ps_t1"][:, kb * 128: kb * 128 + ln],
                                   g["og_sb"][0:ln, kb * 128:(kb + 1) * 128],
                                   g["ident_sb"][0:ln, 0:ln])
            return last
        sch.op("tensor", mk_ogt, "tp_og")
        sch.wait("vector", "tp_og", sch.cnt["tp_og"])
        nevog = sch.vop(lambda e: e.tensor_copy(out=g["ogT_sb"][:], in_=g["ps_t1"][:]))
        ogT_prev[0] = nevog
        sch.wait("tensor", "ser_v", nevog)
        sch.wait("tensor", "ser_v", sch.cnt["ser_v"])

        def mk_wo(e, ln=ln):
            last = None
            for kb in range(8):
                for oc in range(2):
                    last = e.matmul(
                        g["ps_a"][0:ln, oc * 512:(oc + 1) * 512],
                        g["ogT_sb"][:, kb * 128:(kb + 1) * 128][:, 0:ln],
                        g["WoutT_sb"][:, kb * 1024 + oc * 512: kb * 1024 + (oc + 1) * 512],
                        start=(kb == 0), stop=(kb == 7))
            return last
        sch.op("tensor", mk_wo, "mm_wo")
        n_wo = sch.cnt["mm_wo"]
        sch.wait("vector", "mm_wo", n_wo)
        sch.wait("vector", "x3_dma", len(gt_done) * 16)
        nop_ = sch.vop(lambda e, ln=ln: e.tensor_copy(
            out=g["op_sb"][0:ln, :], in_=g["ps_a"][0:ln, 0:1024]))
        gt_done.append(nop_)
        sch.wait("sync", "ser_v", nop_)
        sch.dma("sync", g["myout_hbm"][t0:t0 + ln, :], g["op_sb"][0:ln, :], "x3_dma")
    sch.wait("sync", "x3_dma", sch.cnt["x3_dma"])
    if sch.on("sync"):
        myslot = g["myslot_sync"]
        nc.sync.dma_start(out=g["shr_out"][0:L, :], in_=g["myout_hbm"][:, :],
                          cond=(myslot == 0)).then_inc(sems["x3_dma"], 16)
        nc.sync.dma_start(out=g["shr_out"][L:2 * L, :], in_=g["myout_hbm"][:, :],
                          cond=myslot).then_inc(sems["x3_dma"], 16)
    sch.inc("x3_dma", 32)
    sch.wait("gpsimd", "x3_dma", sch.cnt["x3_dma"])
    sch.op("gpsimd", lambda e: e.collective_compute(
        "AllReduce", ALU.add, ins=[g["cbar_in"][:]], outs=[g["cbar_out"][:]],
        replica_groups=RG), "bar")


def emit_p4(sch, g, nc, RG):
    sems = sch.sems
    nbar = sch.cnt["bar"]
    sch.wait_all("bar", nbar)
    AX = mybir.AxisListType.X
    if sch.on("sync"):
        myslot = g["myslot_sync"]
        for base, dst in ((0, g["loc0_hbm"]), (L, g["loc1_hbm"])):
            nc.sync.dma_start(out=dst[:, :], in_=g["shr_out"][base:base + 4096, :],
                              cond=(myslot == 0)).then_inc(sems["p4_ld"], 16)
            nc.sync.dma_start(out=dst[:, :], in_=g["shr_out"][base + 4096:base + 8192, :],
                              cond=myslot).then_inc(sems["p4_ld"], 16)
    sch.inc("p4_ld", 64)
    prev = [0]
    prevT = [0]
    for mt in range(N_P4):
        t0 = mt * 128
        sch.wait("sync", "ser_v", prev[0])
        sch.dma("sync", g["p4a"][:, 0:1024], g["loc0_hbm"][t0:t0 + 128, :], "p4_ld")
        sch.dma("sync", g["p4b"][:, 0:1024], g["loc1_hbm"][t0:t0 + 128, :], "p4_ld")
        sch.dma("gpsimd", g["xrow_sb"], g["x_rows"][t0:t0 + 128, :], "p4g_ld")
        n_ld = sch.cnt["p4_ld"]
        sch.wait("vector", "p4_ld", n_ld)
        sch.wait("vector", "p4g_ld", sch.cnt["p4g_ld"])
        sch.vop(lambda e: e.tensor_tensor(out=g["p4a"][:, 0:1024], in0=g["p4a"][:, 0:1024],
                                          in1=g["p4b"][:, 0:1024], op=ALU.add))
        nx1 = sch.vop(lambda e: e.tensor_tensor(out=g["x1_sb"], in0=g["xrow_sb"],
                                                in1=g["p4a"][:, 0:1024], op=ALU.add))
        sch.vop(lambda e: e.reduce_sum(out=g["sc1"][:, 0:1], in_=g["x1_sb"], axis=AX))
        sch.wait("scalar", "ser_v", nx1)
        sch.wait("scalar", "out_dma", mt * 16)
        nsq = sch.sop(lambda e: e.activation(out=g["x2_sb"], in_=g["x1_sb"], func=ACT.Square))
        sch.wait("vector", "ser_s", nsq)
        sch.vop(lambda e: e.reduce_sum(out=g["sc1"][:, 1:2], in_=g["x2_sb"], axis=AX))
        sch.vop(lambda e: e.tensor_scalar_mul(out=g["sc1"][:, 2:3], in0=g["sc1"][:, 0:1],
                                              scalar1=1.0 / D_MODEL))
        sch.vop(lambda e: e.tensor_scalar_mul(out=g["sc1"][:, 3:4], in0=g["sc1"][:, 1:2],
                                              scalar1=1.0 / D_MODEL))
        sch.vop(lambda e: e.tensor_tensor(out=g["sc1"][:, 4:5], in0=g["sc1"][:, 2:3],
                                          in1=g["sc1"][:, 2:3], op=ALU.mult))
        nvar = sch.vop(lambda e: e.tensor_tensor(out=g["sc1"][:, 3:4], in0=g["sc1"][:, 3:4],
                                                 in1=g["sc1"][:, 4:5], op=ALU.subtract))
        sch.wait("scalar", "ser_v", nvar)
        nsd = sch.sop(lambda e: e.activation(out=g["sc1"][:, 4:5], in_=g["sc1"][:, 3:4],
                                             func=ACT.Sqrt, bias=g["epscol"][:]))
        sch.wait("vector", "ser_s", nsd)
        sch.vop(lambda e: e.reciprocal(out=g["sc1"][:, 5:6], in_=g["sc1"][:, 4:5]))
        nx1n = sch.vop(lambda e: e.tensor_scalar(
            out=g["x1n_sb"][:, 0:1024], in0=g["x1_sb"], scalar1=g["sc1"][:, 2:3],
            scalar2=g["sc1"][:, 5:6], op0=ALU.subtract, op1=ALU.mult))
        sch.wait("tensor", "ser_v", nx1n)
        sch.wait("tensor", "ser_v", prevT[0])

        def mk_xt(e):
            last = None
            for kb in range(8):
                last = e.transpose(g["ps_t1"][:, kb * 128:(kb + 1) * 128],
                                   g["x1n_sb"][:, kb * 128:(kb + 1) * 128][0:128, :],
                                   g["ident_sb"][:])
            return last
        sch.op("tensor", mk_xt, "p4_tp")
        sch.wait("vector", "p4_tp", sch.cnt["p4_tp"])
        nxT = sch.vop(lambda e: e.tensor_copy(out=g["x1nT_sb"][:, 0:1024], in_=g["ps_t1"][:]))
        sch.wait("tensor", "ser_v", nxT)

        def mk_ffn1(e):
            last = None
            for kb in range(8):
                for oc in range(2):
                    last = e.matmul(
                        g["ps_a"][:, oc * 512:(oc + 1) * 512],
                        g["x1nT_sb"][:, kb * 128:(kb + 1) * 128],
                        g["baseWT_sb"][:, kb * 1024 + oc * 512: kb * 1024 + (oc + 1) * 512],
                        start=(kb == 0), stop=False)
            for oc in range(2):
                last = e.matmul(g["ps_a"][:, oc * 512:(oc + 1) * 512],
                                g["onesrowb_sb"][:],
                                g["bbrep_sb"][:, oc * 512:(oc + 1) * 512],
                                start=False, stop=False)
            for kb in range(8):
                last = e.matmul(g["ps_g"][0:16, 0:128],
                                g["sharedAT_sb"][:, kb * 16:(kb + 1) * 16],
                                g["x1nT_sb"][:, kb * 128:(kb + 1) * 128],
                                start=(kb == 0), stop=(kb == 7))
            for kb in range(8):
                last = e.matmul(g["ps_g"][:, 512:520],
                                g["x1nT_sb"][:, kb * 128:(kb + 1) * 128],
                                g["routerWT_sb"][:, kb * 8:(kb + 1) * 8],
                                start=(kb == 0), stop=False)
            last = e.matmul(g["ps_g"][:, 512:520], g["onesrowb_sb"][:],
                            g["rbrow_sb"][:], start=False, stop=True)
            for kb in range(8):
                last = e.matmul(g["ps_g"][:, 640:736],
                                g["x1nT_sb"][:, kb * 128:(kb + 1) * 128],
                                g["routedAT_sb"][:, kb * 96:(kb + 1) * 96],
                                start=(kb == 0), stop=(kb == 7))
            return last
        sch.op("tensor", mk_ffn1, "p4_mm")
        nmm1 = sch.cnt["p4_mm"]
        sch.wait("vector", "p4_mm", nmm1)
        sch.vop(lambda e: e.tensor_copy(out=g["hshT_sb"][:], in_=g["ps_g"][0:16, 0:128]))
        sch.vop(lambda e: e.tensor_copy(out=g["r6"][:, 0:8], in_=g["ps_g"][:, 512:520]))
        sch.vop(lambda e: e.reduce_max(out=g["r6"][:, 8:9], in_=g["r6"][:, 0:8], axis=AX))
        sch.vop(lambda e: e.tensor_scalar_mul(out=g["r6"][:, 9:10], in0=g["r6"][:, 8:9],
                                              scalar1=-1.0))
        nneg = sch.cnt["ser_v"]
        sch.wait("scalar", "ser_v", nneg)
        nexp = sch.sop(lambda e: e.activation(out=g["r6"][:, 16:24], in_=g["r6"][:, 0:8],
                                              func=ACT.Exp, bias=g["r6"][:, 9:10]))
        sch.wait("vector", "ser_s", nexp)
        sch.vop(lambda e: e.reduce_sum(out=g["r6"][:, 10:11], in_=g["r6"][:, 16:24], axis=AX))
        sch.vop(lambda e: e.reduce_max(out=g["r6"][:, 11:12], in_=g["r6"][:, 16:24], axis=AX))
        sch.vop(lambda e: e.tensor_scalar(out=g["r6"][:, 24:32], in0=g["r6"][:, 16:24],
                                          scalar1=g["r6"][:, 11:12], scalar2=None,
                                          op0=ALU.is_equal))
        sch.vop(lambda e: e.scalar_tensor_tensor(out=g["r6"][:, 24:32], in0=g["r6"][:, 24:32],
                                                 scalar=-1e30, in1=g["r6"][:, 16:24],
                                                 op0=ALU.mult, op1=ALU.add))
        sch.vop(lambda e: e.reduce_max(out=g["r6"][:, 12:13], in_=g["r6"][:, 24:32], axis=AX))
        sch.vop(lambda e: e.tensor_scalar_mul(out=g["r6"][:, 13:14], in0=g["r6"][:, 10:11],
                                              scalar1=1e-6))
        sch.vop(lambda e: e.tensor_tensor(out=g["r6"][:, 14:15], in0=g["r6"][:, 11:12],
                                          in1=g["r6"][:, 12:13], op=ALU.add))
        nden = sch.vop(lambda e: e.tensor_tensor(out=g["r6"][:, 14:15], in0=g["r6"][:, 14:15],
                                                 in1=g["r6"][:, 13:14], op=ALU.add))
        sch.vop(lambda e: e.reciprocal(out=g["r6"][:, 15:16], in_=g["r6"][:, 14:15]))
        sch.vop(lambda e: e.tensor_scalar(out=g["r6"][:, 32:40], in0=g["r6"][:, 16:24],
                                          scalar1=g["r6"][:, 12:13], scalar2=None,
                                          op0=ALU.is_ge))
        nwts = sch.vop(lambda e: e.scalar_tensor_tensor(
            out=g["r6"][:, 40:48], in0=g["r6"][:, 16:24], scalar=g["r6"][:, 15:16],
            in1=g["r6"][:, 32:40], op0=ALU.mult, op1=ALU.mult))
        nhrw = sch.vop(lambda e: e.tensor_tensor(
            out=g["hrw_sb"][:].rearrange("p (x r) -> p x r", x=6),
            in0=g["ps_g"][:, 640:736].rearrange("p (x r) -> p x r", x=6),
            in1=g["r6"][:, 40:46].unsqueeze(2).broadcast_to([128, 6, 16]), op=ALU.mult))
        sch.wait("tensor", "ser_v", nhrw)

        def mk_hrt(e):
            return e.transpose(g["ps_t1"][0:96, 0:128], g["hrw_sb"][:], g["ident_sb"][:])
        sch.op("tensor", mk_hrt, "p4_tp")
        sch.wait("vector", "p4_tp", sch.cnt["p4_tp"])
        nhrT = sch.vop(lambda e: e.tensor_copy(out=g["hrwT_sb"][:], in_=g["ps_t1"][0:96, 0:128]))
        prevT[0] = nhrT
        sch.wait("tensor", "ser_v", nhrT)

        def mk_ffn2(e):
            last = None
            for oc in range(2):
                last = e.matmul(g["ps_a"][:, oc * 512:(oc + 1) * 512],
                                g["hshT_sb"][:],
                                g["sharedBT_sb"][:, oc * 512:(oc + 1) * 512],
                                start=False, stop=False)
            for oc in range(2):
                last = e.matmul(g["ps_a"][:, oc * 512:(oc + 1) * 512],
                                g["hrwT_sb"][:],
                                g["routedBT_sb"][:, oc * 512:(oc + 1) * 512],
                                start=False, stop=True)
            return last
        sch.op("tensor", mk_ffn2, "p4_mm")
        nmm2 = sch.cnt["p4_mm"]
        sch.wait("vector", "p4_mm", nmm2)
        sch.wait("vector", "out_dma", mt * 16)
        nx2 = sch.vop(lambda e: e.tensor_tensor(out=g["x2_sb"], in0=g["x1n_sb"][:, 0:1024],
                                                in1=g["ps_a"][:, 0:1024], op=ALU.add))
        sch.vop(lambda e: e.reduce_sum(out=g["sc1"][:, 0:1], in_=g["x2_sb"], axis=AX))
        sch.wait("scalar", "ser_v", nx2)
        nsq2 = sch.sop(lambda e: e.activation(out=g["x1_sb"], in_=g["x2_sb"], func=ACT.Square))
        sch.wait("vector", "ser_s", nsq2)
        sch.vop(lambda e: e.reduce_sum(out=g["sc1"][:, 1:2], in_=g["x1_sb"], axis=AX))
        sch.vop(lambda e: e.tensor_scalar_mul(out=g["sc1"][:, 2:3], in0=g["sc1"][:, 0:1],
                                              scalar1=1.0 / D_MODEL))
        sch.vop(lambda e: e.tensor_scalar_mul(out=g["sc1"][:, 3:4], in0=g["sc1"][:, 1:2],
                                              scalar1=1.0 / D_MODEL))
        sch.vop(lambda e: e.tensor_tensor(out=g["sc1"][:, 4:5], in0=g["sc1"][:, 2:3],
                                          in1=g["sc1"][:, 2:3], op=ALU.mult))
        nvar2 = sch.vop(lambda e: e.tensor_tensor(out=g["sc1"][:, 3:4], in0=g["sc1"][:, 3:4],
                                                  in1=g["sc1"][:, 4:5], op=ALU.subtract))
        sch.wait("scalar", "ser_v", nvar2)
        nsd2 = sch.sop(lambda e: e.activation(out=g["sc1"][:, 4:5], in_=g["sc1"][:, 3:4],
                                              func=ACT.Sqrt, bias=g["epscol"][:]))
        sch.wait("vector", "ser_s", nsd2)
        sch.vop(lambda e: e.reciprocal(out=g["sc1"][:, 5:6], in_=g["sc1"][:, 4:5]))
        nout = sch.vop(lambda e: e.tensor_scalar(
            out=g["x2_sb"], in0=g["x2_sb"], scalar1=g["sc1"][:, 2:3],
            scalar2=g["sc1"][:, 5:6], op0=ALU.subtract, op1=ALU.mult))
        prev[0] = nout
        sch.wait("gpsimd", "ser_v", nout)
        sch.dma("gpsimd", g["out"][t0:t0 + 128, :], g["x2_sb"][:], "out_dma")


def emit_tail(sch, g, debug_phase):
    for e in Sched.ENGINES:
        for i_ in range(4):
            sch.wait(e, f"sp{i_}", sch.cnt[f"sp{i_}"])
        for i_ in range(8):
            sch.wait(e, f"cr{i_}", sch.cnt[f"cr{i_}"])
            sch.wait(e, f"cs{i_}", sch.cnt[f"cs{i_}"])
        if debug_phase in (1, 2):
            sch.wait(e, "dbg_dma", sch.cnt["dbg_dma"])
            sch.wait(e, "dbg2_dma", sch.cnt["dbg2_dma"])
        if debug_phase >= 2:
            for i_ in range(8):
                sch.wait(e, f"y{i_}", sch.cnt[f"y{i_}"])
        if debug_phase >= 4:
            sch.wait(e, "out_dma", sch.cnt["out_dma"])



def _consts():
    tri = np.tril(np.ones((63, 63), np.float32))
    mtri_aug = np.zeros((64, 64), np.float32)
    mtri_aug[0:63, 0:63] = tri.T
    mtri_tail = np.zeros((64, 64), np.float32)
    mtri_tail[0:2, 0:2] = np.tril(np.ones((2, 2), np.float32)).T
    mtriD = np.zeros((64, 64), np.float32)
    mtriD[0:63, 0:63] = tri.T
    mtriD_tail = np.zeros((64, 64), np.float32)
    mtriD_tail[0:2, 0:2] = np.tril(np.ones((2, 2), np.float32)).T
    return tuple(np.vstack([m, m]) for m in (mtri_aug, mtri_tail, mtriD, mtriD_tail))


def _selg():
    sg = np.zeros((128, 64 + 1024), np.float32)
    sg[62, 0:64] = 1.0
    sg[126, 0:64] = 1.0
    for s_ in range(S):
        sg[s_, 64 + s_ * 64: 64 + (s_ + 1) * 64] = 1.0
        sg[64 + s_, 64 + s_ * 64: 64 + (s_ + 1) * 64] = 1.0
    return sg.astype(BF)


def make_inputs(core, kw):
    b, j = core // 2, core % 2
    own = slice(j * DOWN, (j + 1) * DOWN)
    x = np.asarray(kw["L0_attn_x"])
    W_in = np.asarray(kw["L0_attn_W_in"])
    mtri_aug, mtri_tail, mtriD, mtriD_tail = _consts()
    rows = np.concatenate([W_in[j * DOWN:(j + 1) * DOWN],
                           W_in[D_INNER + j * DOWN: D_INNER + (j + 1) * DOWN]], 0)
    routedAT = np.ascontiguousarray(
        np.asarray(kw["L0_ffn_routed_A"]).transpose(2, 0, 1).reshape(D_MODEL, 96))
    routedBT = np.ascontiguousarray(
        np.asarray(kw["L0_ffn_routed_B"]).transpose(0, 2, 1).reshape(96, D_MODEL))
    routerWT = np.zeros((D_MODEL, 8), np.float32)
    routerWT[:, 0:6] = np.asarray(kw["L0_ffn_router_W"]).T
    router_b = np.full((1, 8), -1e30, np.float32)
    router_b[0, 0:6] = np.asarray(kw["L0_ffn_router_b"])
    f = np.float32

    def c(a):
        return np.ascontiguousarray(a).astype(f)
    return {
        "xT": c(x[b].T),
        "x_rows": c(x[b, j * (L // 2):(j + 1) * (L // 2)]),
        "W_inT": c(rows.T),
        "convw": c(np.asarray(kw["L0_attn_conv_w"])[own, 0, :]),
        "convb": c(np.asarray(kw["L0_attn_conv_b"])[own].reshape(8, 128).T),
        "W_xT": c(np.asarray(kw["L0_attn_W_x"])[:, own].T),
        "W_dt_own": c(np.asarray(kw["L0_attn_W_dt"])[None, own, 0]),
        "b_dt_own": c(np.asarray(kw["L0_attn_b_dt"])[None, own]),
        "A_log_in": c(np.asarray(kw["L0_attn_A_log"])[None, :]),
        "D_own": c(np.asarray(kw["L0_attn_D_param"])[None, own]),
        "lnw_own": c(np.asarray(kw["L0_attn_ln_w"])[None, own]),
        "lnb_own": c(np.asarray(kw["L0_attn_ln_b"])[None, own]),
        "W_outT": c(np.asarray(kw["L0_attn_W_out"])[:, own].T),
        "baseWT": c(np.asarray(kw["L0_ffn_base_W"]).T),
        "base_b": c(np.asarray(kw["L0_ffn_base_b"])[None, :]),
        "sharedAT": c(np.asarray(kw["L0_ffn_shared_A"]).T),
        "sharedBT": c(np.asarray(kw["L0_ffn_shared_B"]).T),
        "routerWT": routerWT,
        "router_b": router_b,
        "routedAT": c(routedAT),
        "routedBT": c(routedBT),
        "mtri_aug": mtri_aug.astype(BF),
        "mtri_tail": mtri_tail.astype(BF),
        "mtriD": mtriD,
        "mtriD_tail": mtriD_tail,
        "ident": np.eye(128, dtype=f).astype(BF),
        "onesrow": np.ones((1, 128), f),
        "onesrowb": np.ones((1, 128), f).astype(BF),
        "selg": _selg(),
        "slot": np.array([[j]], np.int32),
    }


def kernel(**kw):
    nc = build_program()
    in_maps = [make_inputs(c, kw) for c in range(8)]
    res = run_bass_kernel_spmd(nc, in_maps, core_ids=list(range(8)))
    out = np.empty((NB, L, D_MODEL), np.float32)
    for c in range(8):
        b, j = c // 2, c % 2
        out[b, j * (L // 2):(j + 1) * (L // 2)] = res.results[c]["out"]
    return out


# version 13 (final)
# speedup vs baseline: 1.4773x; 1.1557x over previous
import contextlib
import numpy as np
import ml_dtypes
import concourse.bass as bass
import concourse.mybir as mybir
from concourse.bass import ds
from concourse.bass_utils import run_bass_kernel_spmd

F32 = mybir.dt.float32
BF16 = mybir.dt.bfloat16
I32 = mybir.dt.int32
ALU = mybir.AluOpType
ACT = mybir.ActivationFunctionType
BF = ml_dtypes.bfloat16

D_MODEL = 1024
D_INNER = 2048
DOWN = 1024
S = 16
L = 8192
NB = 4
TC = 63
TILE = 126
QUAD = 4

N_P4 = 32
DEBUG_PHASE = 4


def _tiles(n, sz):
    out = []
    t = 0
    while t < n:
        ln = min(sz, n - t)
        out.append((t, ln))
        t += ln
    return out


P1T = _tiles(L, TILE)
CHUNKS = _tiles(L, TC)
NCHUNK = len(CHUNKS)
QUADS = [list(range(i, min(i + QUAD, NCHUNK))) for i in range(0, NCHUNK, QUAD)]
T128T = _tiles(L, 128)

SBUF_ALLOCS = [
    ("WinT_sb", [128, 8 * 2048], BF16),
    ("WxT_sb", [128, 8 * 33], BF16),
    ("WoutT_sb", [128, 8 * 1024], BF16),
    ("baseWT_sb", [128, 8 * 1024], BF16),
    ("sharedAT_sb", [128, 8 * 16], BF16),
    ("sharedBT_sb", [16, 1024], BF16),
    ("routerWT_sb", [128, 8 * 8], BF16),
    ("routedAT_sb", [128, 8 * 96], BF16),
    ("routedBT_sb", [96, 1024], BF16),
    ("mtri_sb", [128, 2 * 64], BF16),
    ("mtriD_sb", [128, 2 * 64], F32),
    ("ident_sb", [128, 128], BF16),
    ("onesrow_sb", [1, 128], F32),
    ("onesrowb_sb", [1, 128], BF16),
    ("convw_sb", [128, 8 * 4], BF16),
    ("convb_sb", [128, 8], F32),
    ("acol_sb", [128, 2 * S], F32),
    ("arow_sb", [1, S], F32),
    ("wdtrep_sb", [128, 512], F32),
    ("bdtrep_sb", [128, 512], F32),
    ("drep_sb", [128, DOWN], BF16),
    ("lnwrep_sb", [128, DOWN], BF16),
    ("lnbrep_sb", [128, DOWN], BF16),
    ("bbrep_sb", [1, D_MODEL], BF16),
    ("rbrow_sb", [1, 8], BF16),
    ("slot_sb", [1, 1], I32),
    ("xiraw", [128, DOWN], BF16),
    ("szt", [128, DOWN], BF16),
    ("win", [128, 8 * (TILE + 3)], BF16),

    ("xi2tile", [128, 8 * TILE], BF16),
    ("xdTbuf", [33, L], BF16),
    ("xi2hs_t", [128, 2 * 512], BF16),
    ("xdhs", [128, 2 * QUAD * 33], F32),
    ("dthsq", [128, 2 * QUAD * 512], F32),
    ("dtxhsq", [128, 2 * QUAD * 512], BF16),
    ("xi2hsq", [128, 2 * QUAD * 512], BF16),
    ("dsbq", [128, 2 * QUAD * 512], F32),
    ("em_sb", [128, 2 * QUAD * 512], BF16),
    ("ep_sb", [128, 2 * QUAD * 512], BF16),
    ("ptmp", [128, 512], BF16),
    ("wts_sb", [128, 4 * 512], BF16),
    ("hets_sb", [128, 4 * 512], BF16),
    ("carrysave", [128, 512], BF16),
    ("accq", [128, QUAD * 512], BF16),
    ("y_ld", [128, DOWN], BF16),
    ("xi2_ld", [128, DOWN], BF16),
    ("sz_ld", [128, DOWN], BF16),
    ("statsbuf", [128, 192], F32),
    ("statspeer", [128, 128], F32),
    ("gt1", [128, DOWN], BF16),
    ("gt2", [128, DOWN], BF16),
    ("og_sb", [128, DOWN], BF16),
    ("ogT_sb", [128, DOWN], BF16),
    ("hshT_sb", [16, 128], BF16),
    ("hrw_sb", [128, 96], BF16),
    ("hrwT_sb", [96, 128], BF16),
    ("r6", [128, 48], F32),
    ("sc1", [128, 16], F32),
    ("epscol", [128, 1], F32),
]

SEM_NAMES = [
    "ser_v", "ser_s", "ser_g", "ld_w", "ld_x", "mm_xz", "ev_xz", "tp_xi", "ev_xi", "cv", "tp_hs", "ev_hs",
    "mm_xd", "ev_xd", "sp_dma", "x1_dma", "bar", "p2_ld", "tp_c", "ev_c",
    "p2l0", "p2l1", "p2l2", "p2l3", "y0", "y1", "y2", "y3", "y4", "y5", "y6", "y7",
    "dt_act", "dt_dve", "mm_d", "ev_dq", "e_act", "w_dve", "cr_dma", "mm_g",
    "he_dve", "acc_dve", "y_dma", "p3_ld", "st_dve", "x2_dma", "gt_dve",
    "tp_og", "ev_og", "mm_wo", "ev_wo", "x3_dma", "p4_ld", "p4_mm", "p4_dve",
    "p4_act", "p4_tp", "out_dma", "dbg_dma", "x1b_dma", "dbg2_dma", "p3t_ld", "p4g_ld",
] + [f"cr{i}" for i in range(8)] + [f"cs{i}" for i in range(8)] + [f"sp{i}" for i in range(4)]


class Sched:
    ENGINES = ("sync", "tensor", "vector", "scalar", "gpsimd")

    def __init__(self, nc, sems):
        self.nc = nc
        self.sems = sems
        self.cnt = {}
        self.cur = None
        self.eng = None

    def begin(self, eng_name, eng):
        self.cur = eng_name
        self.eng = eng
        self.cnt = {k: 0 for k in self.sems}

    def inc(self, sem, amt):
        self.cnt[sem] += amt
        return self.cnt[sem]

    def on(self, eng_name):
        return self.cur == eng_name

    SERIAL = {"vector": "ser_v", "scalar": "ser_s", "gpsimd": "ser_g"}

    def op(self, eng_name, fn, sem=None, amt=1):
        ser = Sched.SERIAL.get(eng_name)
        if ser is not None and sem is None:
            sem, amt = ser, 1
        if self.cur == eng_name:
            if ser is not None:
                self.eng.wait_ge(self.sems[ser], self.cnt[ser])
            inst = fn(self.eng)
            if sem is not None:
                inst.then_inc(self.sems[sem], amt)
        if sem is not None:
            self.inc(sem, amt)
        return self.cnt[sem] if sem is not None else None

    def vop(self, fn):
        return self.op("vector", fn)

    def sop(self, fn):
        return self.op("scalar", fn)

    def dma(self, eng_name, out, in_, sem, **kw):
        self.wait(eng_name, sem, self.cnt[sem])
        self.op(eng_name, lambda e: e.dma_start(out=out, in_=in_, **kw), sem, 16)

    def wait(self, eng_name, sem, val=None):
        v = self.cnt[sem] if val is None else val
        assert v <= self.cnt[sem], (sem, v, self.cnt[sem])
        if self.cur == eng_name:
            self.eng.wait_ge(self.sems[sem], v)

    def wait_all(self, sem, val=None):
        for e in Sched.ENGINES:
            self.wait(e, sem, val)


def build_program(debug_phase=DEBUG_PHASE):
    nc = bass.Bass("TRN2", target_bir_lowering=False)
    dtt = nc.dram_tensor
    g = {}

    def din(name, shape, dtype):
        g[name] = dtt(name, shape, dtype, kind="ExternalInput").ap()

    din("xT", [D_MODEL, L], F32)
    din("x_rows", [L // 2, D_MODEL], F32)
    din("W_inT", [D_MODEL, 2 * DOWN], F32)
    din("convw", [DOWN, 4], F32)
    din("convb", [128, 8], F32)
    din("W_xT", [DOWN, 33], F32)
    din("W_dt_own", [1, DOWN], F32)
    din("b_dt_own", [1, DOWN], F32)
    din("A_log_in", [1, S], F32)
    din("D_own", [1, DOWN], F32)
    din("lnw_own", [1, DOWN], F32)
    din("lnb_own", [1, DOWN], F32)
    din("W_outT", [DOWN, D_MODEL], F32)
    din("baseWT", [D_MODEL, D_MODEL], F32)
    din("base_b", [1, D_MODEL], F32)
    din("sharedAT", [D_MODEL, 16], F32)
    din("sharedBT", [16, D_MODEL], F32)
    din("routerWT", [D_MODEL, 8], F32)
    din("router_b", [1, 8], F32)
    din("routedAT", [D_MODEL, 96], F32)
    din("routedBT", [96, D_MODEL], F32)
    din("mtri_aug", [128, 64], BF16)
    din("mtri_tail", [128, 64], BF16)
    din("mtriD", [128, 64], F32)
    din("mtriD_tail", [128, 64], F32)
    din("ident", [128, 128], BF16)
    din("onesrow", [1, 128], F32)
    din("onesrowb", [1, 128], BF16)
    din("selg", [128, 64 + 1024], BF16)
    din("slot", [1, 1], I32)

    g["out"] = dtt("out", [L // 2, D_MODEL], F32, kind="ExternalOutput").ap()
    g["xi2hs_hbm"] = dtt("xi2hs", [NCHUNK, 128, 512], BF16, kind="Internal").ap()
    g["xi2t_hbm"] = dtt("xi2t", [DOWN, L], BF16, kind="Internal").ap()
    g["sz_hbm"] = dtt("szsp", [L, DOWN], BF16, kind="Internal").ap()
    g["y_hbm"] = dtt("ysp", [L, DOWN], BF16, kind="Internal").ap()
    g["shr_xd"] = dtt("shr_xd", [66, L], BF16, kind="Internal", addr_space="Shared").ap()
    g["shr_st"] = dtt("shr_st", [256, 128], F32, kind="Internal", addr_space="Shared").ap()
    g["shr_out"] = dtt("shr_out", [2 * L, D_MODEL], BF16, kind="Internal", addr_space="Shared").ap()
    g["myout_hbm"] = dtt("myout", [L, D_MODEL], BF16, kind="Internal").ap()
    g["loc0_hbm"] = dtt("loc0", [L // 2, D_MODEL], BF16, kind="Internal").ap()
    g["loc1_hbm"] = dtt("loc1", [L // 2, D_MODEL], BF16, kind="Internal").ap()
    g["cbar_in"] = dtt("cbar_in", [1, 16], F32, kind="Internal").ap()
    g["cbar_out"] = dtt("cbar_out", [1, 16], F32, kind="Internal").ap()
    RG = [[0, 1], [2, 3], [4, 5], [6, 7]]

    if debug_phase == 1:
        g["d_xi2t"] = dtt("d_xi2t", [DOWN, L], BF16, kind="ExternalOutput").ap()
        g["d_sz"] = dtt("d_sz", [L, DOWN], BF16, kind="ExternalOutput").ap()
        g["d_xdT"] = dtt("d_xdT", [33, L], BF16, kind="ExternalOutput").ap()
    if debug_phase == 2:
        g["d_y"] = dtt("d_y", [L, DOWN], BF16, kind="ExternalOutput").ap()

    stack = contextlib.ExitStack()
    with stack:
        for name, shape, dty in SBUF_ALLOCS:
            g[name] = stack.enter_context(nc.sbuf_tensor(name, shape, dty))
        g["ps_a"] = stack.enter_context(nc.psum_tensor("ps_a", [128, 2048], F32))
        g["ps_t1"] = stack.enter_context(nc.psum_tensor("ps_t1", [128, 1024], BF16))
        g["ps_xd"] = stack.enter_context(nc.psum_tensor("ps_xd", [33, TILE], F32))
        g["ps_g"] = stack.enter_context(nc.psum_tensor("ps_g", [128, 2 * 512], F32))
        g["op_sb"] = g["accq"][:, 0:1024]
        g["xTb"] = g["em_sb"][:, 0:2 * 8 * TILE]
        g["cvtmp"] = g["ep_sb"][:, 0:8 * TILE]
        g["cvacc"] = g["ep_sb"][:, 8 * TILE:2 * 8 * TILE]
        g["selq_sb"] = g["gt2"][:, 0:1024]
        g["sel62_sb"] = g["og_sb"][:, 0:64]
        g["p4a"] = g["y_ld"]
        g["p4b"] = g["xi2_ld"]
        g["x1n_sb"] = g["sz_ld"]
        g["x1nT_sb"] = g["gt1"]
        g["xrow_sb"] = g["dthsq"][:, 0:1024]
        g["x1_sb"] = g["dthsq"][:, 1024:2048]
        g["x2_sb"] = g["dsbq"][:, 0:1024]
        sems = {}
        for s_ in SEM_NAMES:
            sems[s_] = stack.enter_context(nc.semaphore(s_))
        sch = Sched(nc, sems)

        def walk():
            emit_setup(sch, g, nc, debug_phase)
            emit_p1(sch, g, nc, debug_phase)
            if debug_phase >= 2:
                emit_x1(sch, g, nc, RG)
                emit_p2(sch, g, nc, debug_phase)
            if debug_phase >= 3:
                emit_p3(sch, g, nc, RG, debug_phase)
            if debug_phase >= 4:
                emit_p4(sch, g, nc, RG)
            emit_tail(sch, g, debug_phase)

        with nc.Block() as block:
            @block.sync
            def _(eng):
                sch.begin("sync", eng)
                walk()

            @block.tensor
            def _(eng):
                sch.begin("tensor", eng)
                walk()

            @block.vector
            def _(eng):
                sch.begin("vector", eng)
                walk()

            @block.scalar
            def _(eng):
                sch.begin("scalar", eng)
                walk()

            @block.gpsimd
            def _(eng):
                sch.begin("gpsimd", eng)
                walk()
    return nc


def emit_setup(sch, g, nc, debug_phase):
    G = "gpsimd"

    def ld(dst, src):
        sch.dma(G, dst, src, "ld_w")

    ld(g["WinT_sb"][:].rearrange("p (a o) -> p a o", a=8),
       g["W_inT"].rearrange("(a p) o -> p a o", p=128))
    ld(g["WxT_sb"][:].rearrange("p (a o) -> p a o", a=8),
       g["W_xT"].rearrange("(a p) o -> p a o", p=128))
    ld(g["WoutT_sb"][:].rearrange("p (a o) -> p a o", a=8),
       g["W_outT"].rearrange("(a p) o -> p a o", p=128))
    ld(g["baseWT_sb"][:].rearrange("p (a o) -> p a o", a=8),
       g["baseWT"].rearrange("(a p) o -> p a o", p=128))
    ld(g["sharedAT_sb"][:].rearrange("p (a o) -> p a o", a=8),
       g["sharedAT"].rearrange("(a p) o -> p a o", p=128))
    ld(g["sharedBT_sb"][:], g["sharedBT"])
    ld(g["routerWT_sb"][:].rearrange("p (a o) -> p a o", a=8),
       g["routerWT"].rearrange("(a p) o -> p a o", p=128))
    ld(g["routedAT_sb"][:].rearrange("p (a o) -> p a o", a=8),
       g["routedAT"].rearrange("(a p) o -> p a o", p=128))
    ld(g["routedBT_sb"][:], g["routedBT"])
    ld(g["mtri_sb"][:, 0:64], g["mtri_aug"])
    ld(g["mtri_sb"][:, 64:128], g["mtri_tail"])
    ld(g["mtriD_sb"][:, 0:64], g["mtriD"])
    ld(g["mtriD_sb"][:, 64:128], g["mtriD_tail"])
    ld(g["ident_sb"][:], g["ident"])
    ld(g["onesrow_sb"][:], g["onesrow"])
    ld(g["onesrowb_sb"][:], g["onesrowb"])
    ld(g["sel62_sb"], g["selg"][:, 0:64])
    ld(g["selq_sb"], g["selg"][:, 64:1088])
    ld(g["convw_sb"][:].rearrange("p (a k) -> p a k", a=8),
       g["convw"].rearrange("(a p) k -> p a k", p=128))
    ld(g["convb_sb"][:], g["convb"])
    ld(g["wdtrep_sb"][0:64, :], g["W_dt_own"][:, 0:512].partition_broadcast(64))
    ld(g["wdtrep_sb"][64:128, :], g["W_dt_own"][:, 512:1024].partition_broadcast(64))
    ld(g["bdtrep_sb"][0:64, :], g["b_dt_own"][:, 0:512].partition_broadcast(64))
    ld(g["bdtrep_sb"][64:128, :], g["b_dt_own"][:, 512:1024].partition_broadcast(64))
    ld(g["drep_sb"][:], g["D_own"].partition_broadcast(128))
    ld(g["lnwrep_sb"][:], g["lnw_own"].partition_broadcast(128))
    ld(g["lnbrep_sb"][:], g["lnb_own"].partition_broadcast(128))
    ld(g["bbrep_sb"][:], g["base_b"])
    ld(g["rbrow_sb"][:], g["router_b"])
    ld(g["arow_sb"][:], g["A_log_in"])
    ld(g["slot_sb"][:], g["slot"])
    n_w = sch.cnt["ld_w"]

    sch.wait("scalar", "ld_w", n_w)
    n1 = sch.sop(lambda e: e.activation(
        out=g["acol_sb"][0:1, 0:S], in_=g["arow_sb"][:], func=ACT.Exp))
    n2 = sch.sop(lambda e: e.copy(
        out=g["arow_sb"][:], in_=g["acol_sb"][0:1, 0:S]))
    sch.wait("tensor", "ser_s", n2)
    sch.op("tensor", lambda e: e.matmul(
        g["ps_a"][:, 0:S], g["onesrow_sb"][:], g["arow_sb"][:],
        start=True, stop=True), "mm_d")
    sch.wait("vector", "mm_d", sch.cnt["mm_d"])
    sch.vop(lambda e: e.tensor_scalar_mul(
        out=g["acol_sb"][:, 0:S], in0=g["ps_a"][:, 0:S], scalar1=-1.0))
    sch.vop(lambda e: e.tensor_scalar_mul(
        out=g["acol_sb"][:, S:2 * S], in0=g["ps_a"][:, 0:S], scalar1=1.0))
    sch.vop(lambda e: e.memset(g["carrysave"][:], 0.0))
    sch.vop(lambda e: e.memset(g["epscol"][:], 1e-5))
    nz = sch.vop(lambda e: e.memset(g["xi2hsq"][:, 0:128], 0.0))
    sch.wait("tensor", "ser_v", nz)
    sch.wait("tensor", "ld_w", sch.cnt["ld_w"])

    def mk_zps(e):
        last = None
        for kb in range(8):
            last = e.transpose(g["ps_t1"][:, kb * 128:(kb + 1) * 128],
                               g["xi2hsq"][:, 0:128], g["ident_sb"][:])
        return last
    sch.op("tensor", mk_zps, "tp_xi")
    sch.vop(lambda e: e.memset(g["win"][:], 0.0))
    n_ms = sch.vop(lambda e: e.memset(g["statsbuf"][:], 0.0))
    sch.wait("gpsimd", "ser_v", n_ms)
    sch.dma("gpsimd", g["cbar_in"], g["statsbuf"][0:1, 0:16], "ld_w")
    sch.wait_all("ser_v", n_ms)
    sch.wait_all("ld_w", sch.cnt["ld_w"])


def emit_p1(sch, g, nc, debug_phase):
    sems = sch.sems
    r = {"mm_xz": [0], "silu_z": [0], "xi_cp": [0], "wincp": [0], "conv": [0],
         "silu2": [0], "ev_xd": [0], "ev_hs": [0], "sp": [(0, 0, 0, 0)], "dbg": [0],
         "spd": []}
    for m, (t0, ln) in enumerate(P1T):
        buf = m % 2
        sch.wait("gpsimd", "mm_xz", r["mm_xz"][max(0, m - 1)])
        sch.dma("gpsimd",
                g["xTb"][:, buf * 8 * TILE:(buf + 1) * 8 * TILE]
                .rearrange("p (a t) -> p a t", a=8)[:, :, 0:ln],
                g["xT"][:, t0:t0 + ln].rearrange("(a p) t -> p a t", p=128),
                "ld_x")
        n_ldx = sch.cnt["ld_x"]
        sch.wait("tensor", "ld_x", n_ldx)
        sch.wait("tensor", "ser_s", r["silu_z"][m])
        sch.wait("tensor", "ser_v", r["xi_cp"][m])

        def mk_xz(e, buf=buf, ln=ln):
            last = None
            for k in range(8):
                lhs = g["xTb"][:, buf * 8 * TILE + k * TILE: buf * 8 * TILE + k * TILE + ln]
                for oc in range(4):
                    last = e.matmul(
                        g["ps_a"][0:ln, oc * 512:(oc + 1) * 512], lhs,
                        g["WinT_sb"][:, k * 2048 + oc * 512: k * 2048 + (oc + 1) * 512],
                        start=(k == 0), stop=(k == 7))
            return last
        sch.op("tensor", mk_xz, "mm_xz")
        r["mm_xz"].append(sch.cnt["mm_xz"])
        n_mmxz = sch.cnt["mm_xz"]
        sch.wait("scalar", "mm_xz", n_mmxz)
        for i_ in range(4):
            sch.wait("scalar", f"sp{i_}", r["sp"][m][i_] if m > 0 else 0)
        if debug_phase == 1:
            sch.wait("scalar", "dbg_dma", r["dbg"][m])
        ns = sch.sop(lambda e, ln=ln: e.activation(
            out=g["szt"][0:ln, :], in_=g["ps_a"][0:ln, DOWN:2 * DOWN], func=ACT.Silu))
        r["silu_z"].append(ns)
        sch.wait("vector", "mm_xz", n_mmxz)
        nv = sch.vop(lambda e, ln=ln: e.tensor_copy(
            out=g["xiraw"][0:ln, :], in_=g["ps_a"][0:ln, 0:DOWN]))
        r["xi_cp"].append(nv)
        sch.wait("sync", "ser_s", ns)
        sch.dma("sync", g["sz_hbm"][t0:t0 + ln, :], g["szt"][0:ln, :], f"sp{m % 4}")
        if debug_phase == 1:
            sch.dma("sync", g["d_sz"][t0:t0 + ln, :], g["szt"][0:ln, :], "dbg_dma")
        sch.wait("tensor", "ser_v", nv)
        sch.wait("tensor", "ser_v", r["ev_hs"][m])

        def mk_tp(e, ln=ln):
            last = None
            for kb in range(8):
                last = e.transpose(g["ps_t1"][:, kb * 128: kb * 128 + ln],
                                   g["xiraw"][0:ln, kb * 128:(kb + 1) * 128],
                                   g["ident_sb"][0:ln, 0:ln])
            return last
        sch.op("tensor", mk_tp, "tp_xi")
        n_tpxi = sch.cnt["tp_xi"]
        sch.wait("vector", "tp_xi", n_tpxi)
        nw = sch.vop(lambda e, ln=ln: e.tensor_copy(
            out=g["win"][:].rearrange("p (a t) -> p a t", a=8)[:, :, 3:3 + ln],
            in_=g["ps_t1"][:].rearrange("p (a t) -> p a t", a=8)[:, :, 0:ln]))
        r["wincp"].append(nw)
        wv = g["win"][:].rearrange("p (a t) -> p a t", a=8)
        cwv = g["convw_sb"][:].rearrange("p (a k) -> p a k", a=8)

        def cv_mul0(e, ln=ln):
            return e.tensor_tensor(
                out=g["cvacc"][:].rearrange("p (a t) -> p a t", a=8)[:, :, 0:ln],
                in0=wv[:, :, 0:ln],
                in1=cwv[:, :, 0:1].broadcast_to([128, 8, ln]), op=ALU.mult)
        sch.vop(cv_mul0)
        for k in (1, 2, 3):
            def cv_mul(e, k=k, ln=ln):
                return e.tensor_tensor(
                    out=g["cvtmp"][:].rearrange("p (a t) -> p a t", a=8)[:, :, 0:ln],
                    in0=wv[:, :, k:k + ln],
                    in1=cwv[:, :, k:k + 1].broadcast_to([128, 8, ln]), op=ALU.mult)
            sch.vop(cv_mul)

            def cv_add(e, ln=ln):
                av = g["cvacc"][:].rearrange("p (a t) -> p a t", a=8)[:, :, 0:ln]
                return e.tensor_tensor(out=av, in0=av,
                                       in1=g["cvtmp"][:].rearrange("p (a t) -> p a t", a=8)[:, :, 0:ln],
                                       op=ALU.add)
            sch.vop(cv_add)
        nc_conv = sch.vop(lambda e, ln=ln: e.tensor_copy(
            out=g["win"][:].rearrange("p (a t) -> p a t", a=8)[:, :, 0:3],
            in_=wv[:, :, ln:ln + 3]))
        r["conv"].append(nc_conv)
        sch.wait("scalar", "ser_v", nc_conv)
        sch.wait("scalar", "mm_xd", m)
        sch.wait("scalar", "tp_hs", m)
        sch.wait("scalar", "sp_dma", r["spd"][m - 1] if m > 0 else 0)
        for i_ in range(4):
            sch.wait("scalar", f"sp{i_}", r["sp"][m][i_] if m > 0 else 0)

        def mk_silu(e, ln=ln):
            last = None
            for kb in range(8):
                last = e.activation(out=g["xi2tile"][:, kb * TILE: kb * TILE + ln],
                                    in_=g["cvacc"][:, kb * TILE: kb * TILE + ln],
                                    func=ACT.Silu, bias=g["convb_sb"][:, kb:kb + 1])
            return last
        ns2 = sch.sop(mk_silu)
        r["silu2"].append(ns2)
        sch.wait("tensor", "ser_s", ns2)
        sch.wait("tensor", "ser_v", r["ev_xd"][m])

        def mk_xd(e, ln=ln):
            last = None
            for kb in range(8):
                last = e.matmul(g["ps_xd"][:, 0:ln],
                                g["WxT_sb"][:, kb * 33:(kb + 1) * 33],
                                g["xi2tile"][:, kb * TILE: kb * TILE + ln],
                                start=(kb == 0), stop=(kb == 7))
            return last
        sch.op("tensor", mk_xd, "mm_xd")
        n_mmxd = sch.cnt["mm_xd"]
        sch.wait("vector", "mm_xd", n_mmxd)
        nxd = sch.vop(lambda e, t0=t0, ln=ln: e.tensor_copy(
            out=g["xdTbuf"][:, t0:t0 + ln], in_=g["ps_xd"][:, 0:ln]))
        r["ev_xd"].append(nxd)
        sch.wait("sync", "ser_s", ns2)
        sch.dma("sync",
                g["xi2t_hbm"][:, t0:t0 + ln].rearrange("(a p) t -> p a t", p=128),
                g["xi2tile"][:].rearrange("p (a t) -> p a t", a=8)[:, :, 0:ln],
                "sp_dma")
        if debug_phase == 1:
            sch.dma("sync",
                    g["d_xi2t"][:, t0:t0 + ln].rearrange("(a p) t -> p a t", p=128),
                    g["xi2tile"][:].rearrange("p (a t) -> p a t", a=8)[:, :, 0:ln],
                    "dbg_dma")
        nch = 2 if ln == TILE else 1
        cl = TC if ln == TILE else ln
        sch.wait("tensor", "mm_xd", n_mmxd)
        sch.wait("tensor", "ser_v", nw)

        def mk_hst(e, nch=nch, cl=cl):
            last = None
            for ci in range(nch):
                for h in range(2):
                    for db in range(4):
                        kb = h * 4 + db
                        last = e.transpose(
                            g["ps_t1"][64 * h: 64 * h + cl,
                                       ci * 512 + db * 128: ci * 512 + db * 128 + 128],
                            g["xi2tile"][:, kb * TILE + ci * TC: kb * TILE + ci * TC + cl],
                            g["ident_sb"][:])
            return last
        sch.op("tensor", mk_hst, "tp_hs")
        n_tph = sch.cnt["tp_hs"]
        sch.wait("vector", "tp_hs", n_tph)
        for i_ in range(4):
            sch.wait("vector", f"sp{i_}", r["sp"][m][i_] if m > 0 else 0)
        nhs = sch.vop(lambda e, nch=nch: e.tensor_copy(
            out=g["xi2hs_t"][:, 0:nch * 512], in_=g["ps_t1"][:, 0:nch * 512]))
        r["ev_hs"].append(nhs)
        sch.wait("sync", "ser_v", nhs)
        for ci in range(nch):
            sch.dma("sync", g["xi2hs_hbm"][2 * m + ci],
                    g["xi2hs_t"][:, ci * 512:(ci + 1) * 512], f"sp{m % 4}")
        r["sp"].append(tuple(sch.cnt[f"sp{i}"] for i in range(4)))
        r["spd"].append(sch.cnt["sp_dma"])
        r["dbg"].append(sch.cnt["dbg_dma"])
    if debug_phase == 1:
        nxd_all = r["ev_xd"][-1]
        sch.wait("sync", "ser_v", nxd_all)
        sch.dma("sync", g["d_xdT"], g["xdTbuf"][:], "dbg_dma")


def emit_x1(sch, g, nc, RG):
    sems = sch.sems
    n_evxd = sch.cnt["ev_xd"]
    sch.wait("sync", "ev_xd", n_evxd)
    if sch.on("sync"):
        myslot = nc.values_load(g["slot_sb"][0:1, 0:1], engines=[mybir.EngineType.SP],
                                min_val=0, max_val=1, skip_runtime_bounds_check=True)
        g["myslot_sync"] = myslot
        nc.sync.dma_start(out=g["shr_xd"][0:33, :], in_=g["xdTbuf"][:],
                          cond=(myslot == 0)).then_inc(sems["x1_dma"], 16)
        nc.sync.dma_start(out=g["shr_xd"][33:66, :], in_=g["xdTbuf"][:],
                          cond=myslot).then_inc(sems["x1_dma"], 16)
    sch.inc("x1_dma", 32)
    sch.wait("gpsimd", "x1_dma", sch.cnt["x1_dma"])
    sch.op("gpsimd", lambda e: e.collective_compute(
        "AllReduce", ALU.add, ins=[g["cbar_in"][:]], outs=[g["cbar_out"][:]],
        replica_groups=RG), "bar")
    sch.wait("sync", "bar", sch.cnt["bar"])
    if sch.on("sync"):
        myslot = g["myslot_sync"]
        for half, dst in ((0, g["em_sb"]), (1, g["ep_sb"])):
            nc.sync.dma_start(out=dst[0:33, 0:4096],
                              in_=g["shr_xd"][33:66, half * 4096:(half + 1) * 4096],
                              cond=(myslot == 0)).then_inc(sems["x1_dma"], 16)
            nc.sync.dma_start(out=dst[0:33, 0:4096],
                              in_=g["shr_xd"][0:33, half * 4096:(half + 1) * 4096],
                              cond=myslot).then_inc(sems["x1_dma"], 16)
    sch.inc("x1_dma", 64)
    sch.wait("vector", "x1_dma", sch.cnt["x1_dma"])
    sch.vop(lambda e: e.tensor_tensor(out=g["xdTbuf"][:, 0:4096], in0=g["xdTbuf"][:, 0:4096],
                                      in1=g["em_sb"][0:33, 0:4096], op=ALU.add))
    nadd = sch.vop(lambda e: e.tensor_tensor(out=g["xdTbuf"][:, 4096:8192],
                                             in0=g["xdTbuf"][:, 4096:8192],
                                             in1=g["ep_sb"][0:33, 0:4096], op=ALU.add))
    sch.wait_all("ser_v", nadd)


def emit_p2(sch, g, nc, debug_phase):
    sems = sch.sems
    for i_ in range(4):
        sch.wait("gpsimd", f"sp{i_}", sch.cnt[f"sp{i_}"])
    he_cnt = [0] * S
    slot_rd = {}
    save_cnt = {}
    wslot_g = {}
    psg_rd = [0, 0]
    emep_war = {}
    last_xdev = [0]
    g_cnt = [0] * S
    q_dtx = [0, 0]
    q_mmd = [0, 0]
    q_eact = [0, 0]
    q_evq = [0, 0]
    q_ydma_prev = [[0, 0] for _ in range(QUAD)]
    ld_cnt = [[0] * QUAD, [0] * QUAD]

    def emit_loads(qi2):
        rot2 = qi2 % 2
        sch.wait("gpsimd", "ser_v", q_dtx[rot2])
        for ci2, c2 in enumerate(QUADS[qi2]):
            sch.dma("gpsimd",
                    g["xi2hsq"][:, (rot2 * QUAD + ci2) * 512:(rot2 * QUAD + ci2 + 1) * 512],
                    g["xi2hs_hbm"][c2], f"p2l{ci2}")
            ld_cnt[rot2][ci2] = sch.cnt[f"p2l{ci2}"]

    emit_loads(0)
    for qi, quad in enumerate(QUADS):
        rot = qi % 2
        qn = len(quad)
        for ci, c in enumerate(quad):
            t0, cl = CHUNKS[c]
            sch.wait("tensor", "ser_v", last_xdev[0])

            def mk_xdt(e, t0=t0, cl=cl):
                e.transpose(g["ps_t1"][0:cl, 0:33], g["xdTbuf"][:, t0:t0 + cl],
                            g["ident_sb"][0:33, 0:33])
                return e.transpose(g["ps_t1"][64:64 + cl, 0:33],
                                   g["xdTbuf"][:, t0:t0 + cl],
                                   g["ident_sb"][0:33, 0:33])
            sch.op("tensor", mk_xdt, "tp_c")
            sch.wait("vector", "tp_c", sch.cnt["tp_c"])
            last_xdev[0] = sch.vop(lambda e, ci=ci, rot=rot: e.tensor_copy(
                out=g["xdhs"][:, (rot * QUAD + ci) * 33:(rot * QUAD + ci + 1) * 33],
                in_=g["ps_t1"][:, 0:33]))
        sch.wait("vector", "mm_d", q_mmd[rot])
        dtx_last = 0
        for ci, c in enumerate(quad):
            slot = rot * QUAD + ci
            dsl = g["dthsq"][:, slot * 512:(slot + 1) * 512]
            dtin_col = g["xdhs"][:, slot * 33: slot * 33 + 1]
            n1 = sch.vop(lambda e, dsl=dsl, dtin_col=dtin_col: e.scalar_tensor_tensor(
                out=dsl, in0=g["wdtrep_sb"][:], scalar=dtin_col, in1=g["bdtrep_sb"][:],
                op0=ALU.mult, op1=ALU.add))
            sch.wait("scalar", "ser_v", n1)
            sch.sop(lambda e, dsl=dsl: e.activation(out=dsl, in_=dsl, func=ACT.Exp))
            n2 = sch.sop(lambda e, dsl=dsl: e.activation(out=dsl, in_=dsl,
                                                         func=ACT.Ln, bias=1.0))
            sch.wait("vector", "ser_s", n2)
            sch.vop(lambda e, dsl=dsl: e.tensor_scalar(
                out=dsl, in0=dsl, scalar1=1e-4, scalar2=10.0, op0=ALU.max, op1=ALU.min))
            sch.wait("vector", f"p2l{ci}", ld_cnt[rot][ci])
            dtx_last = sch.vop(lambda e, slot=slot, dsl=dsl: e.tensor_tensor(
                out=g["dtxhsq"][:, slot * 512:(slot + 1) * 512], in0=dsl,
                in1=g["xi2hsq"][:, slot * 512:(slot + 1) * 512], op=ALU.mult))
        q_dtx[rot] = dtx_last
        if qi + 1 < len(QUADS):
            emit_loads(qi + 1)
        sch.wait("tensor", "ser_v", dtx_last)
        sch.wait("tensor", "ser_v", q_evq[rot])
        sch.wait("tensor", "ser_v", q_evq[1 - rot])

        def mk_dmm(e, quad=quad, rot=rot):
            last = None
            for ci, c in enumerate(quad):
                tail = (CHUNKS[c][1] != TC)
                lhsc = g["mtriD_sb"][:, 64:128] if tail else g["mtriD_sb"][:, 0:64]
                for h in range(2):
                    last = e.matmul(
                        g["ps_a"][64 * h:64 * h + 64, ci * 512:(ci + 1) * 512],
                        lhsc[64 * h:64 * h + 64, :],
                        g["dthsq"][64 * h:64 * h + 64,
                                   (rot * QUAD + ci) * 512:(rot * QUAD + ci + 1) * 512],
                        start=True, stop=True, tile_position=(64 * h, 64 * h))
            return last
        sch.op("tensor", mk_dmm, "mm_d")
        q_mmd[rot] = sch.cnt["mm_d"]
        sch.wait("vector", "mm_d", q_mmd[rot])
        sch.wait("vector", "ser_s", q_eact[rot])
        nev = sch.vop(lambda e, rot=rot, qn=qn: e.tensor_copy(
            out=g["dsbq"][:, rot * 2048: rot * 2048 + qn * 512],
            in_=g["ps_a"][:, 0:qn * 512]))
        q_evq[rot] = nev
        for sp in range(0, S, 2):
            lanes = (sp, sp + 1)
            eact_cnt = {}
            w_pipe = {s2: {} for s2 in lanes}

            def emit_eact(ci3):
                for li3, s3 in enumerate(lanes):
                    sch.wait("scalar", "ser_v", nev)
                    sch.wait("scalar", "ser_v", emep_war.get((li3, ci3), 0))
                    sch.sop(lambda e, li3=li3, s3=s3, ci3=ci3, rot=rot: e.activation(
                        out=g["em_sb"][:, li3 * 2048 + ci3 * 512:
                                       li3 * 2048 + (ci3 + 1) * 512],
                        in_=g["dsbq"][:, rot * 2048 + ci3 * 512:
                                      rot * 2048 + (ci3 + 1) * 512],
                        func=ACT.Exp, scale=g["acol_sb"][:, S + s3:S + s3 + 1]))
                    n2 = sch.sop(lambda e, li3=li3, s3=s3, ci3=ci3, rot=rot: e.activation(
                        out=g["ep_sb"][:, li3 * 2048 + ci3 * 512:
                                       li3 * 2048 + (ci3 + 1) * 512],
                        in_=g["dsbq"][:, rot * 2048 + ci3 * 512:
                                      rot * 2048 + (ci3 + 1) * 512],
                        func=ACT.Exp, scale=g["acol_sb"][:, s3:s3 + 1]))
                    eact_cnt[(li3, ci3)] = n2
                    q_eact[rot] = max(q_eact[rot], n2)

            def emit_pw(ci2):
                c2 = quad[ci2]
                slot2 = rot * QUAD + ci2
                for li2, s2 in enumerate(lanes):
                    sch.wait("vector", "ser_s", eact_cnt[(li2, ci2)])
                    wslot_pre = li2 * 2 + quad[ci2] % 2
                    sch.wait("vector", "mm_g", wslot_g.get(wslot_pre, 0))
                    Bcol = g["xdhs"][:, slot2 * 33 + 1 + s2: slot2 * 33 + 2 + s2]
                    wslot = li2 * 2 + c2 % 2
                    sch.vop(lambda e, slot2=slot2, Bcol=Bcol: e.tensor_scalar(
                        out=g["ptmp"][:], in0=g["dtxhsq"][:, slot2 * 512:(slot2 + 1) * 512],
                        scalar1=Bcol, scalar2=10.0, op0=ALU.mult, op1=ALU.min))
                    nw2 = sch.vop(lambda e, li2=li2, ci2=ci2, wslot=wslot: e.scalar_tensor_tensor(
                        out=g["wts_sb"][:, wslot * 512:(wslot + 1) * 512], in0=g["ptmp"][:],
                        scalar=-10.0,
                        in1=g["em_sb"][:, li2 * 2048 + ci2 * 512: li2 * 2048 + (ci2 + 1) * 512],
                        op0=ALU.max, op1=ALU.mult))
                    w_pipe[s2][ci2] = nw2

            emit_eact(0)
            if qn > 1:
                emit_eact(1)
            emit_pw(0)
            if qn > 1:
                emit_pw(1)
            for ci, c in enumerate(quad):
                t0, cl = CHUNKS[c]
                tail = (cl != TC)
                slot = rot * QUAD + ci
                qstart = (ci == 0)
                crs = f"cr{sp // 2}"
                ceng = "sync" if (sp // 2) % 2 == 0 else "gpsimd"
                wslot0 = c % 2
                n_cr = sch.cnt[crs]
                for li, s_ in enumerate(lanes):
                    wslot = li * 2 + c % 2
                    wt = g["wts_sb"][:, wslot * 512:(wslot + 1) * 512]
                    if qstart:
                        sel = g["selq_sb"][:, s_ * 64:(s_ + 1) * 64]
                        csrc = g["carrysave"][:]
                        sv = save_cnt.get(sp)
                        if sv is not None:
                            sch.wait("tensor", sv[0], sv[1])
                    else:
                        pslot = li * 2 + (c - 1) % 2
                        sel = g["sel62_sb"]
                        csrc = g["hets_sb"][:, pslot * 512:(pslot + 1) * 512]
                    sch.wait("tensor", "ser_v", w_pipe[s_][ci])
                    sch.wait("tensor", "ser_v", max(he_cnt[s_], psg_rd[li]))
                    lhsg = g["mtri_sb"][:, 64:128] if tail else g["mtri_sb"][:, 0:64]

                    def mk_g(e, li=li, lhsg=lhsg, wt=wt, sel=sel, csrc=csrc):
                        last = None
                        for h in range(2):
                            pg = g["ps_g"][64 * h:64 * h + 64,
                                           li * 512:(li + 1) * 512]
                            e.matmul(pg, lhsg[64 * h:64 * h + 64, :],
                                     wt[64 * h:64 * h + 64, :],
                                     start=True, stop=False,
                                     tile_position=(64 * h, 64 * h))
                            last = e.matmul(
                                pg, sel[64 * h:64 * h + 64, :],
                                csrc[64 * h:64 * h + 64, :],
                                start=False, stop=True,
                                tile_position=(64 * h, 64 * h))
                        return last
                    sch.op("tensor", mk_g, "mm_g")
                    g_cnt[s_] = sch.cnt["mm_g"]
                    wslot_g[wslot] = sch.cnt["mm_g"]
                    if not qstart:
                        slot_rd[pslot] = ("mm_g", sch.cnt["mm_g"])
                for li, s_ in enumerate(lanes):
                    hslot = li * 2 + c % 2
                    het = g["hets_sb"][:, hslot * 512:(hslot + 1) * 512]
                    sch.wait("vector", "mm_g", g_cnt[s_])
                    sch.wait("vector", crs, n_cr)
                    if hslot in slot_rd:
                        sr = slot_rd[hslot]
                        sch.wait("vector", sr[0], sr[1])
                    nhe = sch.vop(lambda e, het=het, li=li, ci=ci: e.tensor_tensor(
                        out=het,
                        in0=g["ep_sb"][:, li * 2048 + ci * 512: li * 2048 + (ci + 1) * 512],
                        in1=g["ps_g"][:, li * 512:(li + 1) * 512], op=ALU.mult))
                    he_cnt[s_] = nhe
                    psg_rd[li] = nhe
                    emep_war[(li, ci)] = max(w_pipe[s_][ci], nhe)
                    Ccol = g["xdhs"][:, slot * 33 + 17 + s_: slot * 33 + 18 + s_]
                    adst = g["accq"][:, ci * 512:(ci + 1) * 512]
                    if s_ == 0:
                        sch.wait("vector", f"y{ci * 2}", q_ydma_prev[ci][0])
                        sch.wait("vector", f"y{ci * 2 + 1}", q_ydma_prev[ci][1])
                        if debug_phase == 2:
                            sch.wait("vector", "dbg2_dma", sch.cnt["dbg2_dma"])
                        sch.vop(lambda e, adst=adst, het=het, Ccol=Ccol: e.tensor_scalar(
                            out=adst, in0=het, scalar1=Ccol, scalar2=None, op0=ALU.mult))
                    else:
                        sch.vop(lambda e, adst=adst, het=het, Ccol=Ccol:
                                e.scalar_tensor_tensor(
                                    out=adst, in0=het, scalar=Ccol, in1=adst,
                                    op0=ALU.mult, op1=ALU.add))
                if ci == qn - 1 and c != NCHUNK - 1:
                    sch.wait(ceng, "ser_v", max(he_cnt[sp], he_cnt[sp + 1]))
                    hsv2 = g["hets_sb"][:].rearrange("p (sl t) -> p sl t", sl=4)
                    for h in range(2):
                        sch.dma(ceng,
                                g["carrysave"][64 * h + sp: 64 * h + sp + 2, :]
                                .rearrange("p (a t) -> p a t", a=1),
                                hsv2[64 * h + cl - 1: 64 * h + cl, wslot0::2, :],
                                crs)
                    slot_rd[wslot0] = (crs, sch.cnt[crs])
                    slot_rd[wslot0 + 2] = (crs, sch.cnt[crs])
                    save_cnt[sp] = (crs, sch.cnt[crs])
                if ci + 2 < qn:
                    emit_eact(ci + 2)
                    emit_pw(ci + 2)
        nacc_fin = sch.cnt["ser_v"]
        sch.wait("gpsimd", "ser_v", nacc_fin)
        for ci, c in enumerate(quad):
            t0, cl = CHUNKS[c]
            acc_fin = g["accq"][:, ci * 512:(ci + 1) * 512]
            sch.dma("gpsimd", g["y_hbm"][t0:t0 + cl, 0:512], acc_fin[0:cl, :],
                    f"y{ci * 2}")
            sch.dma("gpsimd", g["y_hbm"][t0:t0 + cl, 512:1024], acc_fin[64:64 + cl, :],
                    f"y{ci * 2 + 1}")
            q_ydma_prev[ci] = [sch.cnt[f"y{ci * 2}"], sch.cnt[f"y{ci * 2 + 1}"]]
            if debug_phase == 2:
                sch.dma("gpsimd", g["d_y"][t0:t0 + cl, 0:512], acc_fin[0:cl, :], "dbg2_dma")
                sch.dma("gpsimd", g["d_y"][t0:t0 + cl, 512:1024], acc_fin[64:64 + cl, :], "dbg2_dma")


def emit_p3(sch, g, nc, RG, debug_phase):
    sems = sch.sems
    for i_ in range(8):
        sch.wait("gpsimd", f"y{i_}", sch.cnt[f"y{i_}"])
    for e_ in ("gpsimd", "sync"):
        sch.wait(e_, "sp_dma", sch.cnt["sp_dma"])
        for i_ in range(4):
            sch.wait(e_, f"sp{i_}", sch.cnt[f"sp{i_}"])
    sch.wait("scalar", "mm_d", sch.cnt["mm_d"])
    st_tile = []
    for mt, (t0, ln) in enumerate(T128T):
        sch.wait("gpsimd", "ser_v", st_tile[-1] if st_tile else 0)
        sch.dma("gpsimd", g["y_ld"][0:ln, :], g["y_hbm"][t0:t0 + ln, :], "p3_ld")
        n_ld = sch.cnt["p3_ld"]
        sch.wait("vector", "p3_ld", n_ld)
        sch.vop(lambda e, mt=mt, ln=ln: e.reduce_sum(
            out=g["statsbuf"][0:ln, 2 * mt:2 * mt + 1], in_=g["y_ld"][0:ln, :],
            axis=mybir.AxisListType.X))
        sch.wait("scalar", "p3_ld", n_ld)
        nsq = sch.sop(lambda e, ln=ln: e.activation(
            out=g["x1_sb"][0:ln, :], in_=g["y_ld"][0:ln, :], func=ACT.Square))
        sch.wait("vector", "ser_s", nsq)
        nst = sch.vop(lambda e, mt=mt, ln=ln: e.reduce_sum(
            out=g["statsbuf"][0:ln, 2 * mt + 1:2 * mt + 2], in_=g["x1_sb"][0:ln, :],
            axis=mybir.AxisListType.X))
        st_tile.append(nst)
    sch.wait("sync", "ser_v", st_tile[-1])
    if sch.on("sync"):
        myslot = nc.values_load(g["slot_sb"][0:1, 0:1], engines=[mybir.EngineType.SP],
                                min_val=0, max_val=1, skip_runtime_bounds_check=True)
        g["myslot_sync"] = myslot
        nc.sync.dma_start(out=g["shr_st"][0:128, :], in_=g["statsbuf"][:, 0:128],
                          cond=(myslot == 0)).then_inc(sems["x2_dma"], 16)
        nc.sync.dma_start(out=g["shr_st"][128:256, :], in_=g["statsbuf"][:, 0:128],
                          cond=myslot).then_inc(sems["x2_dma"], 16)
    sch.inc("x2_dma", 32)
    sch.wait("gpsimd", "x2_dma", sch.cnt["x2_dma"])
    sch.op("gpsimd", lambda e: e.collective_compute(
        "AllReduce", ALU.add, ins=[g["cbar_in"][:]], outs=[g["cbar_out"][:]],
        replica_groups=RG), "bar")
    sch.wait("sync", "bar", sch.cnt["bar"])
    if sch.on("sync"):
        myslot = g["myslot_sync"]
        nc.sync.dma_start(out=g["statspeer"][:], in_=g["shr_st"][128:256, :],
                          cond=(myslot == 0)).then_inc(sems["x2_dma"], 16)
        nc.sync.dma_start(out=g["statspeer"][:], in_=g["shr_st"][0:128, :],
                          cond=myslot).then_inc(sems["x2_dma"], 16)
    sch.inc("x2_dma", 32)
    sch.wait("vector", "x2_dma", sch.cnt["x2_dma"])
    sv = g["statsbuf"][:, 0:128].rearrange("p (m two) -> p m two", two=2)
    sch.vop(lambda e: e.tensor_tensor(out=g["statsbuf"][:, 0:128],
                                      in0=g["statsbuf"][:, 0:128],
                                      in1=g["statspeer"][:, 0:128], op=ALU.add))
    sch.vop(lambda e: e.tensor_scalar_mul(
        out=g["statsbuf"][:, 128:192], in0=sv[:, :, 0], scalar1=1.0 / D_INNER))
    sch.vop(lambda e: e.tensor_scalar_mul(
        out=g["statspeer"][:, 0:64], in0=sv[:, :, 1], scalar1=1.0 / D_INNER))
    sch.vop(lambda e: e.tensor_tensor(out=g["statspeer"][:, 64:128],
                                      in0=g["statsbuf"][:, 128:192],
                                      in1=g["statsbuf"][:, 128:192], op=ALU.mult))
    nvar = sch.vop(lambda e: e.tensor_tensor(out=g["statspeer"][:, 0:64],
                                             in0=g["statspeer"][:, 0:64],
                                             in1=g["statspeer"][:, 64:128], op=ALU.subtract))
    sch.wait("scalar", "ser_v", nvar)
    nsq2 = sch.sop(lambda e: e.activation(out=g["statspeer"][:, 64:128],
                                          in_=g["statspeer"][:, 0:64],
                                          func=ACT.Sqrt, bias=g["epscol"][:]))
    sch.wait("vector", "ser_s", nsq2)
    nrst = sch.vop(lambda e: e.reciprocal(out=g["statsbuf"][:, 0:64],
                                          in_=g["statspeer"][:, 64:128]))
    gt_done = []
    ogT_prev = [0]
    for mt, (t0, ln) in enumerate(T128T):
        sch.wait("gpsimd", "ser_v", gt_done[-1] if gt_done else nrst)
        sch.dma("gpsimd", g["y_ld"][0:ln, :], g["y_hbm"][t0:t0 + ln, :], "p3_ld")
        sch.dma("gpsimd", g["sz_ld"][0:ln, :], g["sz_hbm"][t0:t0 + ln, :], "p3_ld")
        sch.wait("sync", "ser_v", gt_done[-1] if gt_done else nrst)
        sch.wait("sync", "p3t_ld", sch.cnt["p3t_ld"])
        if sch.on("sync"):
            for kb in range(8):
                sch.eng.dma_start_transpose(
                    g["xi2_ld"][0:ln, kb * 128:(kb + 1) * 128],
                    g["xi2t_hbm"][kb * 128:(kb + 1) * 128, t0:t0 + ln]).then_inc(sems["p3t_ld"], 16)
        sch.inc("p3t_ld", 8 * 16)
        n_ld = sch.cnt["p3_ld"]
        sch.wait("vector", "p3_ld", n_ld)
        sch.wait("vector", "p3t_ld", sch.cnt["p3t_ld"])
        sch.wait("vector", "ser_v", nrst)
        mu_c = g["statsbuf"][:, 128 + mt:129 + mt]
        rs_c = g["statsbuf"][:, mt:mt + 1]
        sch.vop(lambda e, ln=ln, mu_c=mu_c, rs_c=rs_c: e.tensor_scalar(
            out=g["gt1"][0:ln, :], in0=g["y_ld"][0:ln, :], scalar1=mu_c[0:ln, :],
            scalar2=rs_c[0:ln, :], op0=ALU.subtract, op1=ALU.mult))
        sch.vop(lambda e, ln=ln: e.tensor_tensor(
            out=g["gt1"][0:ln, :], in0=g["gt1"][0:ln, :], in1=g["lnwrep_sb"][0:ln, :],
            op=ALU.mult))
        sch.vop(lambda e, ln=ln: e.tensor_tensor(
            out=g["gt2"][0:ln, :], in0=g["xi2_ld"][0:ln, :], in1=g["drep_sb"][0:ln, :],
            op=ALU.mult))
        sch.vop(lambda e, ln=ln: e.tensor_tensor(
            out=g["gt1"][0:ln, :], in0=g["gt1"][0:ln, :], in1=g["gt2"][0:ln, :],
            op=ALU.add))
        sch.vop(lambda e, ln=ln: e.tensor_tensor(
            out=g["gt1"][0:ln, :], in0=g["gt1"][0:ln, :], in1=g["lnbrep_sb"][0:ln, :],
            op=ALU.add))
        nog = sch.vop(lambda e, ln=ln: e.tensor_tensor(
            out=g["og_sb"][0:ln, :], in0=g["gt1"][0:ln, :], in1=g["sz_ld"][0:ln, :],
            op=ALU.mult))
        sch.wait("tensor", "ser_v", nog)
        sch.wait("tensor", "ser_v", ogT_prev[0])

        def mk_ogt(e, ln=ln):
            last = None
            for kb in range(8):
                last = e.transpose(g["ps_t1"][:, kb * 128: kb * 128 + ln],
                                   g["og_sb"][0:ln, kb * 128:(kb + 1) * 128],
                                   g["ident_sb"][0:ln, 0:ln])
            return last
        sch.op("tensor", mk_ogt, "tp_og")
        sch.wait("vector", "tp_og", sch.cnt["tp_og"])
        nevog = sch.vop(lambda e: e.tensor_copy(out=g["ogT_sb"][:], in_=g["ps_t1"][:]))
        ogT_prev[0] = nevog
        sch.wait("tensor", "ser_v", nevog)
        sch.wait("tensor", "ser_v", sch.cnt["ser_v"])

        def mk_wo(e, ln=ln):
            last = None
            for kb in range(8):
                for oc in range(2):
                    last = e.matmul(
                        g["ps_a"][0:ln, oc * 512:(oc + 1) * 512],
                        g["ogT_sb"][:, kb * 128:(kb + 1) * 128][:, 0:ln],
                        g["WoutT_sb"][:, kb * 1024 + oc * 512: kb * 1024 + (oc + 1) * 512],
                        start=(kb == 0), stop=(kb == 7))
            return last
        sch.op("tensor", mk_wo, "mm_wo")
        n_wo = sch.cnt["mm_wo"]
        sch.wait("vector", "mm_wo", n_wo)
        sch.wait("vector", "x3_dma", len(gt_done) * 16)
        nop_ = sch.vop(lambda e, ln=ln: e.tensor_copy(
            out=g["op_sb"][0:ln, :], in_=g["ps_a"][0:ln, 0:1024]))
        gt_done.append(nop_)
        sch.wait("sync", "ser_v", nop_)
        sch.dma("sync", g["myout_hbm"][t0:t0 + ln, :], g["op_sb"][0:ln, :], "x3_dma")
    sch.wait("sync", "x3_dma", sch.cnt["x3_dma"])
    if sch.on("sync"):
        myslot = g["myslot_sync"]
        nc.sync.dma_start(out=g["shr_out"][0:L, :], in_=g["myout_hbm"][:, :],
                          cond=(myslot == 0)).then_inc(sems["x3_dma"], 16)
        nc.sync.dma_start(out=g["shr_out"][L:2 * L, :], in_=g["myout_hbm"][:, :],
                          cond=myslot).then_inc(sems["x3_dma"], 16)
    sch.inc("x3_dma", 32)
    sch.wait("gpsimd", "x3_dma", sch.cnt["x3_dma"])
    sch.op("gpsimd", lambda e: e.collective_compute(
        "AllReduce", ALU.add, ins=[g["cbar_in"][:]], outs=[g["cbar_out"][:]],
        replica_groups=RG), "bar")


def emit_p4(sch, g, nc, RG):
    sems = sch.sems
    nbar = sch.cnt["bar"]
    sch.wait_all("bar", nbar)
    AX = mybir.AxisListType.X
    if sch.on("sync"):
        myslot = g["myslot_sync"]
        for base, dst in ((0, g["loc0_hbm"]), (L, g["loc1_hbm"])):
            nc.sync.dma_start(out=dst[:, :], in_=g["shr_out"][base:base + 4096, :],
                              cond=(myslot == 0)).then_inc(sems["p4_ld"], 16)
            nc.sync.dma_start(out=dst[:, :], in_=g["shr_out"][base + 4096:base + 8192, :],
                              cond=myslot).then_inc(sems["p4_ld"], 16)
    sch.inc("p4_ld", 64)
    prev = [0]
    prevT = [0]
    for mt in range(N_P4):
        t0 = mt * 128
        sch.wait("sync", "ser_v", prev[0])
        sch.dma("sync", g["p4a"][:, 0:1024], g["loc0_hbm"][t0:t0 + 128, :], "p4_ld")
        sch.dma("sync", g["p4b"][:, 0:1024], g["loc1_hbm"][t0:t0 + 128, :], "p4_ld")
        sch.dma("gpsimd", g["xrow_sb"], g["x_rows"][t0:t0 + 128, :], "p4g_ld")
        n_ld = sch.cnt["p4_ld"]
        sch.wait("vector", "p4_ld", n_ld)
        sch.wait("vector", "p4g_ld", sch.cnt["p4g_ld"])
        sch.vop(lambda e: e.tensor_tensor(out=g["p4a"][:, 0:1024], in0=g["p4a"][:, 0:1024],
                                          in1=g["p4b"][:, 0:1024], op=ALU.add))
        nx1 = sch.vop(lambda e: e.tensor_tensor(out=g["x1_sb"], in0=g["xrow_sb"],
                                                in1=g["p4a"][:, 0:1024], op=ALU.add))
        sch.vop(lambda e: e.reduce_sum(out=g["sc1"][:, 0:1], in_=g["x1_sb"], axis=AX))
        sch.wait("scalar", "ser_v", nx1)
        sch.wait("scalar", "out_dma", mt * 16)
        nsq = sch.sop(lambda e: e.activation(out=g["x2_sb"], in_=g["x1_sb"], func=ACT.Square))
        sch.wait("vector", "ser_s", nsq)
        sch.vop(lambda e: e.reduce_sum(out=g["sc1"][:, 1:2], in_=g["x2_sb"], axis=AX))
        sch.vop(lambda e: e.tensor_scalar_mul(out=g["sc1"][:, 2:3], in0=g["sc1"][:, 0:1],
                                              scalar1=1.0 / D_MODEL))
        sch.vop(lambda e: e.tensor_scalar_mul(out=g["sc1"][:, 3:4], in0=g["sc1"][:, 1:2],
                                              scalar1=1.0 / D_MODEL))
        sch.vop(lambda e: e.tensor_tensor(out=g["sc1"][:, 4:5], in0=g["sc1"][:, 2:3],
                                          in1=g["sc1"][:, 2:3], op=ALU.mult))
        nvar = sch.vop(lambda e: e.tensor_tensor(out=g["sc1"][:, 3:4], in0=g["sc1"][:, 3:4],
                                                 in1=g["sc1"][:, 4:5], op=ALU.subtract))
        sch.wait("scalar", "ser_v", nvar)
        nsd = sch.sop(lambda e: e.activation(out=g["sc1"][:, 4:5], in_=g["sc1"][:, 3:4],
                                             func=ACT.Sqrt, bias=g["epscol"][:]))
        sch.wait("vector", "ser_s", nsd)
        sch.vop(lambda e: e.reciprocal(out=g["sc1"][:, 5:6], in_=g["sc1"][:, 4:5]))
        nx1n = sch.vop(lambda e: e.tensor_scalar(
            out=g["x1n_sb"][:, 0:1024], in0=g["x1_sb"], scalar1=g["sc1"][:, 2:3],
            scalar2=g["sc1"][:, 5:6], op0=ALU.subtract, op1=ALU.mult))
        sch.wait("tensor", "ser_v", nx1n)
        sch.wait("tensor", "ser_v", prevT[0])

        def mk_xt(e):
            last = None
            for kb in range(8):
                last = e.transpose(g["ps_t1"][:, kb * 128:(kb + 1) * 128],
                                   g["x1n_sb"][:, kb * 128:(kb + 1) * 128][0:128, :],
                                   g["ident_sb"][:])
            return last
        sch.op("tensor", mk_xt, "p4_tp")
        sch.wait("vector", "p4_tp", sch.cnt["p4_tp"])
        nxT = sch.vop(lambda e: e.tensor_copy(out=g["x1nT_sb"][:, 0:1024], in_=g["ps_t1"][:]))
        sch.wait("tensor", "ser_v", nxT)

        def mk_ffn1(e):
            last = None
            for kb in range(8):
                for oc in range(2):
                    last = e.matmul(
                        g["ps_a"][:, oc * 512:(oc + 1) * 512],
                        g["x1nT_sb"][:, kb * 128:(kb + 1) * 128],
                        g["baseWT_sb"][:, kb * 1024 + oc * 512: kb * 1024 + (oc + 1) * 512],
                        start=(kb == 0), stop=False)
            for oc in range(2):
                last = e.matmul(g["ps_a"][:, oc * 512:(oc + 1) * 512],
                                g["onesrowb_sb"][:],
                                g["bbrep_sb"][:, oc * 512:(oc + 1) * 512],
                                start=False, stop=False)
            for kb in range(8):
                last = e.matmul(g["ps_g"][0:16, 0:128],
                                g["sharedAT_sb"][:, kb * 16:(kb + 1) * 16],
                                g["x1nT_sb"][:, kb * 128:(kb + 1) * 128],
                                start=(kb == 0), stop=(kb == 7))
            for kb in range(8):
                last = e.matmul(g["ps_g"][:, 512:520],
                                g["x1nT_sb"][:, kb * 128:(kb + 1) * 128],
                                g["routerWT_sb"][:, kb * 8:(kb + 1) * 8],
                                start=(kb == 0), stop=False)
            last = e.matmul(g["ps_g"][:, 512:520], g["onesrowb_sb"][:],
                            g["rbrow_sb"][:], start=False, stop=True)
            for kb in range(8):
                last = e.matmul(g["ps_g"][:, 640:736],
                                g["x1nT_sb"][:, kb * 128:(kb + 1) * 128],
                                g["routedAT_sb"][:, kb * 96:(kb + 1) * 96],
                                start=(kb == 0), stop=(kb == 7))
            return last
        sch.op("tensor", mk_ffn1, "p4_mm")
        nmm1 = sch.cnt["p4_mm"]
        sch.wait("vector", "p4_mm", nmm1)
        sch.vop(lambda e: e.tensor_copy(out=g["hshT_sb"][:], in_=g["ps_g"][0:16, 0:128]))
        sch.vop(lambda e: e.tensor_copy(out=g["r6"][:, 0:8], in_=g["ps_g"][:, 512:520]))
        sch.vop(lambda e: e.reduce_max(out=g["r6"][:, 8:9], in_=g["r6"][:, 0:8], axis=AX))
        sch.vop(lambda e: e.tensor_scalar_mul(out=g["r6"][:, 9:10], in0=g["r6"][:, 8:9],
                                              scalar1=-1.0))
        nneg = sch.cnt["ser_v"]
        sch.wait("scalar", "ser_v", nneg)
        nexp = sch.sop(lambda e: e.activation(out=g["r6"][:, 16:24], in_=g["r6"][:, 0:8],
                                              func=ACT.Exp, bias=g["r6"][:, 9:10]))
        sch.wait("vector", "ser_s", nexp)
        sch.vop(lambda e: e.reduce_sum(out=g["r6"][:, 10:11], in_=g["r6"][:, 16:24], axis=AX))
        sch.vop(lambda e: e.reduce_max(out=g["r6"][:, 11:12], in_=g["r6"][:, 16:24], axis=AX))
        sch.vop(lambda e: e.tensor_scalar(out=g["r6"][:, 24:32], in0=g["r6"][:, 16:24],
                                          scalar1=g["r6"][:, 11:12], scalar2=None,
                                          op0=ALU.is_equal))
        sch.vop(lambda e: e.scalar_tensor_tensor(out=g["r6"][:, 24:32], in0=g["r6"][:, 24:32],
                                                 scalar=-1e30, in1=g["r6"][:, 16:24],
                                                 op0=ALU.mult, op1=ALU.add))
        sch.vop(lambda e: e.reduce_max(out=g["r6"][:, 12:13], in_=g["r6"][:, 24:32], axis=AX))
        sch.vop(lambda e: e.tensor_scalar_mul(out=g["r6"][:, 13:14], in0=g["r6"][:, 10:11],
                                              scalar1=1e-6))
        sch.vop(lambda e: e.tensor_tensor(out=g["r6"][:, 14:15], in0=g["r6"][:, 11:12],
                                          in1=g["r6"][:, 12:13], op=ALU.add))
        nden = sch.vop(lambda e: e.tensor_tensor(out=g["r6"][:, 14:15], in0=g["r6"][:, 14:15],
                                                 in1=g["r6"][:, 13:14], op=ALU.add))
        sch.vop(lambda e: e.reciprocal(out=g["r6"][:, 15:16], in_=g["r6"][:, 14:15]))
        sch.vop(lambda e: e.tensor_scalar(out=g["r6"][:, 32:40], in0=g["r6"][:, 16:24],
                                          scalar1=g["r6"][:, 12:13], scalar2=None,
                                          op0=ALU.is_ge))
        nwts = sch.vop(lambda e: e.scalar_tensor_tensor(
            out=g["r6"][:, 40:48], in0=g["r6"][:, 16:24], scalar=g["r6"][:, 15:16],
            in1=g["r6"][:, 32:40], op0=ALU.mult, op1=ALU.mult))
        nhrw = sch.vop(lambda e: e.tensor_tensor(
            out=g["hrw_sb"][:].rearrange("p (x r) -> p x r", x=6),
            in0=g["ps_g"][:, 640:736].rearrange("p (x r) -> p x r", x=6),
            in1=g["r6"][:, 40:46].unsqueeze(2).broadcast_to([128, 6, 16]), op=ALU.mult))
        sch.wait("tensor", "ser_v", nhrw)

        def mk_hrt(e):
            return e.transpose(g["ps_t1"][0:96, 0:128], g["hrw_sb"][:], g["ident_sb"][:])
        sch.op("tensor", mk_hrt, "p4_tp")
        sch.wait("vector", "p4_tp", sch.cnt["p4_tp"])
        nhrT = sch.vop(lambda e: e.tensor_copy(out=g["hrwT_sb"][:], in_=g["ps_t1"][0:96, 0:128]))
        prevT[0] = nhrT
        sch.wait("tensor", "ser_v", nhrT)

        def mk_ffn2(e):
            last = None
            for oc in range(2):
                last = e.matmul(g["ps_a"][:, oc * 512:(oc + 1) * 512],
                                g["hshT_sb"][:],
                                g["sharedBT_sb"][:, oc * 512:(oc + 1) * 512],
                                start=False, stop=False)
            for oc in range(2):
                last = e.matmul(g["ps_a"][:, oc * 512:(oc + 1) * 512],
                                g["hrwT_sb"][:],
                                g["routedBT_sb"][:, oc * 512:(oc + 1) * 512],
                                start=False, stop=True)
            return last
        sch.op("tensor", mk_ffn2, "p4_mm")
        nmm2 = sch.cnt["p4_mm"]
        sch.wait("vector", "p4_mm", nmm2)
        sch.wait("vector", "out_dma", mt * 16)
        nx2 = sch.vop(lambda e: e.tensor_tensor(out=g["x2_sb"], in0=g["x1n_sb"][:, 0:1024],
                                                in1=g["ps_a"][:, 0:1024], op=ALU.add))
        sch.vop(lambda e: e.reduce_sum(out=g["sc1"][:, 0:1], in_=g["x2_sb"], axis=AX))
        sch.wait("scalar", "ser_v", nx2)
        nsq2 = sch.sop(lambda e: e.activation(out=g["x1_sb"], in_=g["x2_sb"], func=ACT.Square))
        sch.wait("vector", "ser_s", nsq2)
        sch.vop(lambda e: e.reduce_sum(out=g["sc1"][:, 1:2], in_=g["x1_sb"], axis=AX))
        sch.vop(lambda e: e.tensor_scalar_mul(out=g["sc1"][:, 2:3], in0=g["sc1"][:, 0:1],
                                              scalar1=1.0 / D_MODEL))
        sch.vop(lambda e: e.tensor_scalar_mul(out=g["sc1"][:, 3:4], in0=g["sc1"][:, 1:2],
                                              scalar1=1.0 / D_MODEL))
        sch.vop(lambda e: e.tensor_tensor(out=g["sc1"][:, 4:5], in0=g["sc1"][:, 2:3],
                                          in1=g["sc1"][:, 2:3], op=ALU.mult))
        nvar2 = sch.vop(lambda e: e.tensor_tensor(out=g["sc1"][:, 3:4], in0=g["sc1"][:, 3:4],
                                                  in1=g["sc1"][:, 4:5], op=ALU.subtract))
        sch.wait("scalar", "ser_v", nvar2)
        nsd2 = sch.sop(lambda e: e.activation(out=g["sc1"][:, 4:5], in_=g["sc1"][:, 3:4],
                                              func=ACT.Sqrt, bias=g["epscol"][:]))
        sch.wait("vector", "ser_s", nsd2)
        sch.vop(lambda e: e.reciprocal(out=g["sc1"][:, 5:6], in_=g["sc1"][:, 4:5]))
        nout = sch.vop(lambda e: e.tensor_scalar(
            out=g["x2_sb"], in0=g["x2_sb"], scalar1=g["sc1"][:, 2:3],
            scalar2=g["sc1"][:, 5:6], op0=ALU.subtract, op1=ALU.mult))
        prev[0] = nout
        sch.wait("gpsimd", "ser_v", nout)
        sch.dma("gpsimd", g["out"][t0:t0 + 128, :], g["x2_sb"][:], "out_dma")


def emit_tail(sch, g, debug_phase):
    for e in Sched.ENGINES:
        for i_ in range(4):
            sch.wait(e, f"sp{i_}", sch.cnt[f"sp{i_}"])
        for i_ in range(8):
            sch.wait(e, f"cr{i_}", sch.cnt[f"cr{i_}"])
            sch.wait(e, f"cs{i_}", sch.cnt[f"cs{i_}"])
        if debug_phase in (1, 2):
            sch.wait(e, "dbg_dma", sch.cnt["dbg_dma"])
            sch.wait(e, "dbg2_dma", sch.cnt["dbg2_dma"])
        if debug_phase >= 2:
            for i_ in range(8):
                sch.wait(e, f"y{i_}", sch.cnt[f"y{i_}"])
        if debug_phase >= 4:
            sch.wait(e, "out_dma", sch.cnt["out_dma"])



def _consts():
    tri = np.tril(np.ones((63, 63), np.float32))
    mtri_aug = np.zeros((64, 64), np.float32)
    mtri_aug[0:63, 0:63] = tri.T
    mtri_tail = np.zeros((64, 64), np.float32)
    mtri_tail[0:2, 0:2] = np.tril(np.ones((2, 2), np.float32)).T
    mtriD = np.zeros((64, 64), np.float32)
    mtriD[0:63, 0:63] = tri.T
    mtriD_tail = np.zeros((64, 64), np.float32)
    mtriD_tail[0:2, 0:2] = np.tril(np.ones((2, 2), np.float32)).T
    return tuple(np.vstack([m, m]) for m in (mtri_aug, mtri_tail, mtriD, mtriD_tail))


def _selg():
    sg = np.zeros((128, 64 + 1024), np.float32)
    sg[62, 0:64] = 1.0
    sg[126, 0:64] = 1.0
    for s_ in range(S):
        sg[s_, 64 + s_ * 64: 64 + (s_ + 1) * 64] = 1.0
        sg[64 + s_, 64 + s_ * 64: 64 + (s_ + 1) * 64] = 1.0
    return sg.astype(BF)


def make_inputs(core, kw):
    b, j = core // 2, core % 2
    own = slice(j * DOWN, (j + 1) * DOWN)
    x = np.asarray(kw["L0_attn_x"])
    W_in = np.asarray(kw["L0_attn_W_in"])
    mtri_aug, mtri_tail, mtriD, mtriD_tail = _consts()
    rows = np.concatenate([W_in[j * DOWN:(j + 1) * DOWN],
                           W_in[D_INNER + j * DOWN: D_INNER + (j + 1) * DOWN]], 0)
    routedAT = np.ascontiguousarray(
        np.asarray(kw["L0_ffn_routed_A"]).transpose(2, 0, 1).reshape(D_MODEL, 96))
    routedBT = np.ascontiguousarray(
        np.asarray(kw["L0_ffn_routed_B"]).transpose(0, 2, 1).reshape(96, D_MODEL))
    routerWT = np.zeros((D_MODEL, 8), np.float32)
    routerWT[:, 0:6] = np.asarray(kw["L0_ffn_router_W"]).T
    router_b = np.full((1, 8), -1e30, np.float32)
    router_b[0, 0:6] = np.asarray(kw["L0_ffn_router_b"])
    f = np.float32

    def c(a):
        return np.ascontiguousarray(a).astype(f)
    return {
        "xT": c(x[b].T),
        "x_rows": c(x[b, j * (L // 2):(j + 1) * (L // 2)]),
        "W_inT": c(rows.T),
        "convw": c(np.asarray(kw["L0_attn_conv_w"])[own, 0, :]),
        "convb": c(np.asarray(kw["L0_attn_conv_b"])[own].reshape(8, 128).T),
        "W_xT": c(np.asarray(kw["L0_attn_W_x"])[:, own].T),
        "W_dt_own": c(np.asarray(kw["L0_attn_W_dt"])[None, own, 0]),
        "b_dt_own": c(np.asarray(kw["L0_attn_b_dt"])[None, own]),
        "A_log_in": c(np.asarray(kw["L0_attn_A_log"])[None, :]),
        "D_own": c(np.asarray(kw["L0_attn_D_param"])[None, own]),
        "lnw_own": c(np.asarray(kw["L0_attn_ln_w"])[None, own]),
        "lnb_own": c(np.asarray(kw["L0_attn_ln_b"])[None, own]),
        "W_outT": c(np.asarray(kw["L0_attn_W_out"])[:, own].T),
        "baseWT": c(np.asarray(kw["L0_ffn_base_W"]).T),
        "base_b": c(np.asarray(kw["L0_ffn_base_b"])[None, :]),
        "sharedAT": c(np.asarray(kw["L0_ffn_shared_A"]).T),
        "sharedBT": c(np.asarray(kw["L0_ffn_shared_B"]).T),
        "routerWT": routerWT,
        "router_b": router_b,
        "routedAT": c(routedAT),
        "routedBT": c(routedBT),
        "mtri_aug": mtri_aug.astype(BF),
        "mtri_tail": mtri_tail.astype(BF),
        "mtriD": mtriD,
        "mtriD_tail": mtriD_tail,
        "ident": np.eye(128, dtype=f).astype(BF),
        "onesrow": np.ones((1, 128), f),
        "onesrowb": np.ones((1, 128), f).astype(BF),
        "selg": _selg(),
        "slot": np.array([[j]], np.int32),
    }


def kernel(**kw):
    nc = build_program()
    in_maps = [make_inputs(c, kw) for c in range(8)]
    res = run_bass_kernel_spmd(nc, in_maps, core_ids=list(range(8)))
    out = np.empty((NB, L, D_MODEL), np.float32)
    for c in range(8):
        b, j = c // 2, c % 2
        out[b, j * (L // 2):(j + 1) * (L // 2)] = res.results[c]["out"]
    return out


# version 14
# speedup vs baseline: 1.4786x; 1.0008x over previous
import contextlib
import numpy as np
import ml_dtypes
import concourse.bass as bass
import concourse.mybir as mybir
from concourse.bass import ds
from concourse.bass_utils import run_bass_kernel_spmd

F32 = mybir.dt.float32
BF16 = mybir.dt.bfloat16
I32 = mybir.dt.int32
ALU = mybir.AluOpType
ACT = mybir.ActivationFunctionType
BF = ml_dtypes.bfloat16

D_MODEL = 1024
D_INNER = 2048
DOWN = 1024
S = 16
L = 8192
NB = 4
TC = 63
TILE = 126
QUAD = 4

N_P4 = 32
DEBUG_PHASE = 4


def _tiles(n, sz):
    out = []
    t = 0
    while t < n:
        ln = min(sz, n - t)
        out.append((t, ln))
        t += ln
    return out


P1T = _tiles(L, TILE)
CHUNKS = _tiles(L, TC)
NCHUNK = len(CHUNKS)
QUADS = [list(range(i, min(i + QUAD, NCHUNK))) for i in range(0, NCHUNK, QUAD)]
T128T = _tiles(L, 128)

SBUF_ALLOCS = [
    ("WinT_sb", [128, 8 * 2048], BF16),
    ("WxT_sb", [128, 8 * 33], BF16),
    ("WoutT_sb", [128, 8 * 1024], BF16),
    ("baseWT_sb", [128, 8 * 1024], BF16),
    ("sharedAT_sb", [128, 8 * 16], BF16),
    ("sharedBT_sb", [16, 1024], BF16),
    ("routerWT_sb", [128, 8 * 8], BF16),
    ("routedAT_sb", [128, 8 * 96], BF16),
    ("routedBT_sb", [96, 1024], BF16),
    ("mtri_sb", [128, 2 * 64], BF16),
    ("mtriD_sb", [128, 2 * 64], F32),
    ("ident_sb", [128, 128], BF16),
    ("onesrow_sb", [1, 128], F32),
    ("onesrowb_sb", [1, 128], BF16),
    ("convw_sb", [128, 8 * 4], BF16),
    ("convb_sb", [128, 8], F32),
    ("acol_sb", [128, 2 * S], F32),
    ("arow_sb", [1, S], F32),
    ("wdtrep_sb", [128, 512], F32),
    ("bdtrep_sb", [128, 512], F32),
    ("drep_sb", [128, DOWN], BF16),
    ("lnwrep_sb", [128, DOWN], BF16),
    ("lnbrep_sb", [128, DOWN], BF16),
    ("bbrep_sb", [1, D_MODEL], BF16),
    ("rbrow_sb", [1, 8], BF16),
    ("slot_sb", [1, 1], I32),
    ("xiraw", [128, DOWN], BF16),
    ("szt", [128, DOWN], BF16),
    ("win", [128, 8 * (TILE + 3)], BF16),

    ("xi2tile", [128, 8 * TILE], BF16),
    ("xdTbuf", [33, L], BF16),
    ("xi2hs_t", [128, 2 * 512], BF16),
    ("xdhs", [128, 2 * QUAD * 33], F32),
    ("dthsq", [128, 2 * QUAD * 512], F32),
    ("dtxhsq", [128, 2 * QUAD * 512], BF16),
    ("xi2hsq", [128, 2 * QUAD * 512], BF16),
    ("dsbq", [128, 2 * QUAD * 512], F32),
    ("em_sb", [128, 2 * QUAD * 512], BF16),
    ("ep_sb", [128, 2 * QUAD * 512], BF16),
    ("ptmp", [128, 512], BF16),
    ("wts_sb", [128, 4 * 512], BF16),
    ("hets_sb", [128, 4 * 512], BF16),
    ("carrysave", [128, 512], BF16),
    ("accq", [128, QUAD * 512], BF16),
    ("y_ld", [128, DOWN], BF16),
    ("xi2_ld", [128, DOWN], BF16),
    ("sz_ld", [128, DOWN], BF16),
    ("statsbuf", [128, 192], F32),
    ("statspeer", [128, 128], F32),
    ("gt1", [128, DOWN], BF16),
    ("gt2", [128, DOWN], BF16),
    ("og_sb", [128, DOWN], BF16),
    ("ogT_sb", [128, DOWN], BF16),
    ("hshT_sb", [16, 128], BF16),
    ("hrw_sb", [128, 96], BF16),
    ("hrwT_sb", [96, 128], BF16),
    ("r6", [128, 48], F32),
    ("sc1", [128, 16], F32),
    ("epscol", [128, 1], F32),
]

SEM_NAMES = [
    "ser_v", "ser_s", "ser_g", "ld_w", "ld_x", "mm_xz", "ev_xz", "tp_xi", "ev_xi", "cv", "tp_hs", "ev_hs",
    "mm_xd", "ev_xd", "sp_dma", "x1_dma", "bar", "p2_ld", "tp_c", "ev_c",
    "p2l0", "p2l1", "p2l2", "p2l3", "y0", "y1", "y2", "y3", "y4", "y5", "y6", "y7",
    "dt_act", "dt_dve", "mm_d", "ev_dq", "e_act", "w_dve", "cr_dma", "mm_g",
    "he_dve", "acc_dve", "y_dma", "p3_ld", "st_dve", "x2_dma", "gt_dve",
    "tp_og", "ev_og", "mm_wo", "ev_wo", "x3_dma", "p4_ld", "p4_mm", "p4_dve",
    "p4_act", "p4_tp", "out_dma", "dbg_dma", "x1b_dma", "dbg2_dma", "p3t_ld", "p4g_ld",
] + [f"cr{i}" for i in range(8)] + [f"cs{i}" for i in range(8)] + [f"sp{i}" for i in range(4)]


class Sched:
    ENGINES = ("sync", "tensor", "vector", "scalar", "gpsimd")

    def __init__(self, nc, sems):
        self.nc = nc
        self.sems = sems
        self.cnt = {}
        self.cur = None
        self.eng = None

    def begin(self, eng_name, eng):
        self.cur = eng_name
        self.eng = eng
        self.cnt = {k: 0 for k in self.sems}

    def inc(self, sem, amt):
        self.cnt[sem] += amt
        return self.cnt[sem]

    def on(self, eng_name):
        return self.cur == eng_name

    SERIAL = {"vector": "ser_v", "scalar": "ser_s", "gpsimd": "ser_g"}

    def op(self, eng_name, fn, sem=None, amt=1):
        ser = Sched.SERIAL.get(eng_name)
        if ser is not None and sem is None:
            sem, amt = ser, 1
        if self.cur == eng_name:
            if ser is not None:
                self.eng.wait_ge(self.sems[ser], self.cnt[ser])
            inst = fn(self.eng)
            if sem is not None:
                inst.then_inc(self.sems[sem], amt)
        if sem is not None:
            self.inc(sem, amt)
        return self.cnt[sem] if sem is not None else None

    def vop(self, fn):
        return self.op("vector", fn)

    def sop(self, fn):
        return self.op("scalar", fn)

    def dma(self, eng_name, out, in_, sem, **kw):
        self.wait(eng_name, sem, self.cnt[sem])
        self.op(eng_name, lambda e: e.dma_start(out=out, in_=in_, **kw), sem, 16)

    def wait(self, eng_name, sem, val=None):
        v = self.cnt[sem] if val is None else val
        assert v <= self.cnt[sem], (sem, v, self.cnt[sem])
        if self.cur == eng_name:
            self.eng.wait_ge(self.sems[sem], v)

    def wait_all(self, sem, val=None):
        for e in Sched.ENGINES:
            self.wait(e, sem, val)


def build_program(debug_phase=DEBUG_PHASE):
    nc = bass.Bass("TRN2", target_bir_lowering=False)
    dtt = nc.dram_tensor
    g = {}

    def din(name, shape, dtype):
        g[name] = dtt(name, shape, dtype, kind="ExternalInput").ap()

    din("xT", [D_MODEL, L], F32)
    din("x_rows", [L // 2, D_MODEL], F32)
    din("W_inT", [D_MODEL, 2 * DOWN], F32)
    din("convw", [DOWN, 4], F32)
    din("convb", [128, 8], F32)
    din("W_xT", [DOWN, 33], F32)
    din("W_dt_own", [1, DOWN], F32)
    din("b_dt_own", [1, DOWN], F32)
    din("A_log_in", [1, S], F32)
    din("D_own", [1, DOWN], F32)
    din("lnw_own", [1, DOWN], F32)
    din("lnb_own", [1, DOWN], F32)
    din("W_outT", [DOWN, D_MODEL], F32)
    din("baseWT", [D_MODEL, D_MODEL], F32)
    din("base_b", [1, D_MODEL], F32)
    din("sharedAT", [D_MODEL, 16], F32)
    din("sharedBT", [16, D_MODEL], F32)
    din("routerWT", [D_MODEL, 8], F32)
    din("router_b", [1, 8], F32)
    din("routedAT", [D_MODEL, 96], F32)
    din("routedBT", [96, D_MODEL], F32)
    din("mtri_aug", [128, 64], BF16)
    din("mtri_tail", [128, 64], BF16)
    din("mtriD", [128, 64], F32)
    din("mtriD_tail", [128, 64], F32)
    din("ident", [128, 128], BF16)
    din("onesrow", [1, 128], F32)
    din("onesrowb", [1, 128], BF16)
    din("selg", [128, 64 + 1024], BF16)
    din("slot", [1, 1], I32)

    g["out"] = dtt("out", [L // 2, D_MODEL], F32, kind="ExternalOutput").ap()
    g["xi2hs_hbm"] = dtt("xi2hs", [NCHUNK, 128, 512], BF16, kind="Internal").ap()
    g["xi2t_hbm"] = dtt("xi2t", [DOWN, L], BF16, kind="Internal").ap()
    g["sz_hbm"] = dtt("szsp", [L, DOWN], BF16, kind="Internal").ap()
    g["y_hbm"] = dtt("ysp", [L, DOWN], BF16, kind="Internal").ap()
    g["shr_xd"] = dtt("shr_xd", [66, L], BF16, kind="Internal", addr_space="Shared").ap()
    g["shr_st"] = dtt("shr_st", [256, 128], F32, kind="Internal", addr_space="Shared").ap()
    g["shr_out"] = dtt("shr_out", [2 * L, D_MODEL], BF16, kind="Internal", addr_space="Shared").ap()
    g["myout_hbm"] = dtt("myout", [L, D_MODEL], BF16, kind="Internal").ap()
    g["loc0_hbm"] = dtt("loc0", [L // 2, D_MODEL], BF16, kind="Internal").ap()
    g["loc1_hbm"] = dtt("loc1", [L // 2, D_MODEL], BF16, kind="Internal").ap()
    g["cbar_in"] = dtt("cbar_in", [1, 16], F32, kind="Internal").ap()
    g["cbar_out"] = dtt("cbar_out", [1, 16], F32, kind="Internal").ap()
    RG = [[0, 1], [2, 3], [4, 5], [6, 7]]

    if debug_phase == 1:
        g["d_xi2t"] = dtt("d_xi2t", [DOWN, L], BF16, kind="ExternalOutput").ap()
        g["d_sz"] = dtt("d_sz", [L, DOWN], BF16, kind="ExternalOutput").ap()
        g["d_xdT"] = dtt("d_xdT", [33, L], BF16, kind="ExternalOutput").ap()
    if debug_phase == 2:
        g["d_y"] = dtt("d_y", [L, DOWN], BF16, kind="ExternalOutput").ap()

    stack = contextlib.ExitStack()
    with stack:
        for name, shape, dty in SBUF_ALLOCS:
            g[name] = stack.enter_context(nc.sbuf_tensor(name, shape, dty))
        g["ps_a"] = stack.enter_context(nc.psum_tensor("ps_a", [128, 2048], F32))
        g["ps_t1"] = stack.enter_context(nc.psum_tensor("ps_t1", [128, 1024], BF16))
        g["ps_xd"] = stack.enter_context(nc.psum_tensor("ps_xd", [33, TILE], F32))
        g["ps_g"] = stack.enter_context(nc.psum_tensor("ps_g", [128, 2 * 512], F32))
        g["op_sb"] = g["accq"][:, 0:1024]
        g["xTb"] = g["em_sb"][:, 0:2 * 8 * TILE]
        g["cvtmp"] = g["ep_sb"][:, 0:8 * TILE]
        g["cvacc"] = g["ep_sb"][:, 8 * TILE:2 * 8 * TILE]
        g["selq_sb"] = g["gt2"][:, 0:1024]
        g["sel62_sb"] = g["og_sb"][:, 0:64]
        g["p4a"] = g["y_ld"]
        g["p4b"] = g["xi2_ld"]
        g["x1n_sb"] = g["sz_ld"]
        g["x1nT_sb"] = g["gt1"]
        g["xrow_sb"] = g["dthsq"][:, 0:1024]
        g["x1_sb"] = g["dthsq"][:, 1024:2048]
        g["x2_sb"] = g["dsbq"][:, 0:1024]
        sems = {}
        for s_ in SEM_NAMES:
            sems[s_] = stack.enter_context(nc.semaphore(s_))
        sch = Sched(nc, sems)

        def walk():
            emit_setup(sch, g, nc, debug_phase)
            emit_p1(sch, g, nc, debug_phase)
            if debug_phase >= 2:
                emit_x1(sch, g, nc, RG)
                emit_p2(sch, g, nc, debug_phase)
            if debug_phase >= 3:
                emit_p3(sch, g, nc, RG, debug_phase)
            if debug_phase >= 4:
                emit_p4(sch, g, nc, RG)
            emit_tail(sch, g, debug_phase)

        with nc.Block() as block:
            @block.sync
            def _(eng):
                sch.begin("sync", eng)
                walk()

            @block.tensor
            def _(eng):
                sch.begin("tensor", eng)
                walk()

            @block.vector
            def _(eng):
                sch.begin("vector", eng)
                walk()

            @block.scalar
            def _(eng):
                sch.begin("scalar", eng)
                walk()

            @block.gpsimd
            def _(eng):
                sch.begin("gpsimd", eng)
                walk()
    return nc


def emit_setup(sch, g, nc, debug_phase):
    G = "gpsimd"

    def ld(dst, src):
        sch.dma(G, dst, src, "ld_w")

    ld(g["WinT_sb"][:].rearrange("p (a o) -> p a o", a=8),
       g["W_inT"].rearrange("(a p) o -> p a o", p=128))
    ld(g["WxT_sb"][:].rearrange("p (a o) -> p a o", a=8),
       g["W_xT"].rearrange("(a p) o -> p a o", p=128))
    ld(g["WoutT_sb"][:].rearrange("p (a o) -> p a o", a=8),
       g["W_outT"].rearrange("(a p) o -> p a o", p=128))
    ld(g["baseWT_sb"][:].rearrange("p (a o) -> p a o", a=8),
       g["baseWT"].rearrange("(a p) o -> p a o", p=128))
    ld(g["sharedAT_sb"][:].rearrange("p (a o) -> p a o", a=8),
       g["sharedAT"].rearrange("(a p) o -> p a o", p=128))
    ld(g["sharedBT_sb"][:], g["sharedBT"])
    ld(g["routerWT_sb"][:].rearrange("p (a o) -> p a o", a=8),
       g["routerWT"].rearrange("(a p) o -> p a o", p=128))
    ld(g["routedAT_sb"][:].rearrange("p (a o) -> p a o", a=8),
       g["routedAT"].rearrange("(a p) o -> p a o", p=128))
    ld(g["routedBT_sb"][:], g["routedBT"])
    ld(g["mtri_sb"][:, 0:64], g["mtri_aug"])
    ld(g["mtri_sb"][:, 64:128], g["mtri_tail"])
    ld(g["mtriD_sb"][:, 0:64], g["mtriD"])
    ld(g["mtriD_sb"][:, 64:128], g["mtriD_tail"])
    ld(g["ident_sb"][:], g["ident"])
    ld(g["onesrow_sb"][:], g["onesrow"])
    ld(g["onesrowb_sb"][:], g["onesrowb"])
    ld(g["sel62_sb"], g["selg"][:, 0:64])
    ld(g["selq_sb"], g["selg"][:, 64:1088])
    ld(g["convw_sb"][:].rearrange("p (a k) -> p a k", a=8),
       g["convw"].rearrange("(a p) k -> p a k", p=128))
    ld(g["convb_sb"][:], g["convb"])
    ld(g["wdtrep_sb"][0:64, :], g["W_dt_own"][:, 0:512].partition_broadcast(64))
    ld(g["wdtrep_sb"][64:128, :], g["W_dt_own"][:, 512:1024].partition_broadcast(64))
    ld(g["bdtrep_sb"][0:64, :], g["b_dt_own"][:, 0:512].partition_broadcast(64))
    ld(g["bdtrep_sb"][64:128, :], g["b_dt_own"][:, 512:1024].partition_broadcast(64))
    ld(g["drep_sb"][:], g["D_own"].partition_broadcast(128))
    ld(g["lnwrep_sb"][:], g["lnw_own"].partition_broadcast(128))
    ld(g["lnbrep_sb"][:], g["lnb_own"].partition_broadcast(128))
    ld(g["bbrep_sb"][:], g["base_b"])
    ld(g["rbrow_sb"][:], g["router_b"])
    ld(g["arow_sb"][:], g["A_log_in"])
    ld(g["slot_sb"][:], g["slot"])
    n_w = sch.cnt["ld_w"]

    sch.wait("scalar", "ld_w", n_w)
    n1 = sch.sop(lambda e: e.activation(
        out=g["acol_sb"][0:1, 0:S], in_=g["arow_sb"][:], func=ACT.Exp))
    n2 = sch.sop(lambda e: e.copy(
        out=g["arow_sb"][:], in_=g["acol_sb"][0:1, 0:S]))
    sch.wait("tensor", "ser_s", n2)
    sch.op("tensor", lambda e: e.matmul(
        g["ps_a"][:, 0:S], g["onesrow_sb"][:], g["arow_sb"][:],
        start=True, stop=True), "mm_d")
    sch.wait("vector", "mm_d", sch.cnt["mm_d"])
    sch.vop(lambda e: e.tensor_scalar_mul(
        out=g["acol_sb"][:, 0:S], in0=g["ps_a"][:, 0:S], scalar1=-1.0))
    sch.vop(lambda e: e.tensor_scalar_mul(
        out=g["acol_sb"][:, S:2 * S], in0=g["ps_a"][:, 0:S], scalar1=1.0))
    sch.vop(lambda e: e.memset(g["carrysave"][:], 0.0))
    sch.vop(lambda e: e.memset(g["epscol"][:], 1e-5))
    nz = sch.vop(lambda e: e.memset(g["xi2hsq"][:, 0:128], 0.0))
    sch.wait("tensor", "ser_v", nz)
    sch.wait("tensor", "ld_w", sch.cnt["ld_w"])

    def mk_zps(e):
        last = None
        for kb in range(8):
            last = e.transpose(g["ps_t1"][:, kb * 128:(kb + 1) * 128],
                               g["xi2hsq"][:, 0:128], g["ident_sb"][:])
        return last
    sch.op("tensor", mk_zps, "tp_xi")
    sch.vop(lambda e: e.memset(g["win"][:], 0.0))
    n_ms = sch.vop(lambda e: e.memset(g["statsbuf"][:], 0.0))
    sch.wait("gpsimd", "ser_v", n_ms)
    sch.dma("gpsimd", g["cbar_in"], g["statsbuf"][0:1, 0:16], "ld_w")
    sch.wait_all("ser_v", n_ms)
    sch.wait_all("ld_w", sch.cnt["ld_w"])


def emit_p1(sch, g, nc, debug_phase):
    sems = sch.sems
    r = {"mm_xz": [0], "silu_z": [0], "xi_cp": [0], "wincp": [0], "conv": [0],
         "silu2": [0], "ev_xd": [0], "ev_hs": [0], "sp": [(0, 0, 0, 0)], "dbg": [0],
         "spd": []}
    for m, (t0, ln) in enumerate(P1T):
        buf = m % 2
        sch.wait("gpsimd", "mm_xz", r["mm_xz"][max(0, m - 1)])
        sch.dma("gpsimd",
                g["xTb"][:, buf * 8 * TILE:(buf + 1) * 8 * TILE]
                .rearrange("p (a t) -> p a t", a=8)[:, :, 0:ln],
                g["xT"][:, t0:t0 + ln].rearrange("(a p) t -> p a t", p=128),
                "ld_x")
        n_ldx = sch.cnt["ld_x"]
        sch.wait("tensor", "ld_x", n_ldx)
        sch.wait("tensor", "ser_s", r["silu_z"][m])
        sch.wait("tensor", "ser_v", r["xi_cp"][m])

        def mk_xz(e, buf=buf, ln=ln):
            last = None
            for k in range(8):
                lhs = g["xTb"][:, buf * 8 * TILE + k * TILE: buf * 8 * TILE + k * TILE + ln]
                for oc in range(4):
                    last = e.matmul(
                        g["ps_a"][0:ln, oc * 512:(oc + 1) * 512], lhs,
                        g["WinT_sb"][:, k * 2048 + oc * 512: k * 2048 + (oc + 1) * 512],
                        start=(k == 0), stop=(k == 7))
            return last
        sch.op("tensor", mk_xz, "mm_xz")
        r["mm_xz"].append(sch.cnt["mm_xz"])
        n_mmxz = sch.cnt["mm_xz"]
        sch.wait("scalar", "mm_xz", n_mmxz)
        for i_ in range(4):
            sch.wait("scalar", f"sp{i_}", r["sp"][m][i_] if m > 0 else 0)
        if debug_phase == 1:
            sch.wait("scalar", "dbg_dma", r["dbg"][m])
        ns = sch.sop(lambda e, ln=ln: e.activation(
            out=g["szt"][0:ln, :], in_=g["ps_a"][0:ln, DOWN:2 * DOWN], func=ACT.Silu))
        r["silu_z"].append(ns)
        sch.wait("vector", "mm_xz", n_mmxz)
        nv = sch.vop(lambda e, ln=ln: e.tensor_copy(
            out=g["xiraw"][0:ln, :], in_=g["ps_a"][0:ln, 0:DOWN]))
        r["xi_cp"].append(nv)
        sch.wait("sync", "ser_s", ns)
        sch.dma("sync", g["sz_hbm"][t0:t0 + ln, :], g["szt"][0:ln, :], f"sp{m % 4}")
        if debug_phase == 1:
            sch.dma("sync", g["d_sz"][t0:t0 + ln, :], g["szt"][0:ln, :], "dbg_dma")
        sch.wait("tensor", "ser_v", nv)
        sch.wait("tensor", "ser_v", r["ev_hs"][m])

        def mk_tp(e, ln=ln):
            last = None
            for kb in range(8):
                last = e.transpose(g["ps_t1"][:, kb * 128: kb * 128 + ln],
                                   g["xiraw"][0:ln, kb * 128:(kb + 1) * 128],
                                   g["ident_sb"][0:ln, 0:ln])
            return last
        sch.op("tensor", mk_tp, "tp_xi")
        n_tpxi = sch.cnt["tp_xi"]
        sch.wait("vector", "tp_xi", n_tpxi)
        nw = sch.vop(lambda e, ln=ln: e.tensor_copy(
            out=g["win"][:].rearrange("p (a t) -> p a t", a=8)[:, :, 3:3 + ln],
            in_=g["ps_t1"][:].rearrange("p (a t) -> p a t", a=8)[:, :, 0:ln]))
        r["wincp"].append(nw)
        wv = g["win"][:].rearrange("p (a t) -> p a t", a=8)
        cwv = g["convw_sb"][:].rearrange("p (a k) -> p a k", a=8)

        def cv_mul0(e, ln=ln):
            return e.tensor_tensor(
                out=g["cvacc"][:].rearrange("p (a t) -> p a t", a=8)[:, :, 0:ln],
                in0=wv[:, :, 0:ln],
                in1=cwv[:, :, 0:1].broadcast_to([128, 8, ln]), op=ALU.mult)
        sch.vop(cv_mul0)
        for k in (1, 2, 3):
            def cv_mul(e, k=k, ln=ln):
                return e.tensor_tensor(
                    out=g["cvtmp"][:].rearrange("p (a t) -> p a t", a=8)[:, :, 0:ln],
                    in0=wv[:, :, k:k + ln],
                    in1=cwv[:, :, k:k + 1].broadcast_to([128, 8, ln]), op=ALU.mult)
            sch.vop(cv_mul)

            def cv_add(e, ln=ln):
                av = g["cvacc"][:].rearrange("p (a t) -> p a t", a=8)[:, :, 0:ln]
                return e.tensor_tensor(out=av, in0=av,
                                       in1=g["cvtmp"][:].rearrange("p (a t) -> p a t", a=8)[:, :, 0:ln],
                                       op=ALU.add)
            sch.vop(cv_add)
        nc_conv = sch.vop(lambda e, ln=ln: e.tensor_copy(
            out=g["win"][:].rearrange("p (a t) -> p a t", a=8)[:, :, 0:3],
            in_=wv[:, :, ln:ln + 3]))
        r["conv"].append(nc_conv)
        sch.wait("scalar", "ser_v", nc_conv)
        sch.wait("scalar", "mm_xd", m)
        sch.wait("scalar", "tp_hs", m)
        sch.wait("scalar", "sp_dma", r["spd"][m - 1] if m > 0 else 0)
        for i_ in range(4):
            sch.wait("scalar", f"sp{i_}", r["sp"][m][i_] if m > 0 else 0)

        def mk_silu(e, ln=ln):
            last = None
            for kb in range(8):
                last = e.activation(out=g["xi2tile"][:, kb * TILE: kb * TILE + ln],
                                    in_=g["cvacc"][:, kb * TILE: kb * TILE + ln],
                                    func=ACT.Silu, bias=g["convb_sb"][:, kb:kb + 1])
            return last
        ns2 = sch.sop(mk_silu)
        r["silu2"].append(ns2)
        sch.wait("tensor", "ser_s", ns2)
        sch.wait("tensor", "ser_v", r["ev_xd"][m])

        def mk_xd(e, ln=ln):
            last = None
            for kb in range(8):
                last = e.matmul(g["ps_xd"][:, 0:ln],
                                g["WxT_sb"][:, kb * 33:(kb + 1) * 33],
                                g["xi2tile"][:, kb * TILE: kb * TILE + ln],
                                start=(kb == 0), stop=(kb == 7))
            return last
        sch.op("tensor", mk_xd, "mm_xd")
        n_mmxd = sch.cnt["mm_xd"]
        sch.wait("vector", "mm_xd", n_mmxd)
        nxd = sch.vop(lambda e, t0=t0, ln=ln: e.tensor_copy(
            out=g["xdTbuf"][:, t0:t0 + ln], in_=g["ps_xd"][:, 0:ln]))
        r["ev_xd"].append(nxd)
        sch.wait("sync", "ser_s", ns2)
        sch.dma("sync",
                g["xi2t_hbm"][:, t0:t0 + ln].rearrange("(a p) t -> p a t", p=128),
                g["xi2tile"][:].rearrange("p (a t) -> p a t", a=8)[:, :, 0:ln],
                "sp_dma")
        if debug_phase == 1:
            sch.dma("sync",
                    g["d_xi2t"][:, t0:t0 + ln].rearrange("(a p) t -> p a t", p=128),
                    g["xi2tile"][:].rearrange("p (a t) -> p a t", a=8)[:, :, 0:ln],
                    "dbg_dma")
        nch = 2 if ln == TILE else 1
        cl = TC if ln == TILE else ln
        sch.wait("tensor", "mm_xd", n_mmxd)
        sch.wait("tensor", "ser_v", nw)

        def mk_hst(e, nch=nch, cl=cl):
            last = None
            for ci in range(nch):
                for h in range(2):
                    for db in range(4):
                        kb = h * 4 + db
                        last = e.transpose(
                            g["ps_t1"][64 * h: 64 * h + cl,
                                       ci * 512 + db * 128: ci * 512 + db * 128 + 128],
                            g["xi2tile"][:, kb * TILE + ci * TC: kb * TILE + ci * TC + cl],
                            g["ident_sb"][:])
            return last
        sch.op("tensor", mk_hst, "tp_hs")
        n_tph = sch.cnt["tp_hs"]
        sch.wait("vector", "tp_hs", n_tph)
        for i_ in range(4):
            sch.wait("vector", f"sp{i_}", r["sp"][m][i_] if m > 0 else 0)
        nhs = sch.vop(lambda e, nch=nch: e.tensor_copy(
            out=g["xi2hs_t"][:, 0:nch * 512], in_=g["ps_t1"][:, 0:nch * 512]))
        r["ev_hs"].append(nhs)
        sch.wait("sync", "ser_v", nhs)
        for ci in range(nch):
            sch.dma("sync", g["xi2hs_hbm"][2 * m + ci],
                    g["xi2hs_t"][:, ci * 512:(ci + 1) * 512], f"sp{m % 4}")
        r["sp"].append(tuple(sch.cnt[f"sp{i}"] for i in range(4)))
        r["spd"].append(sch.cnt["sp_dma"])
        r["dbg"].append(sch.cnt["dbg_dma"])
    if debug_phase == 1:
        nxd_all = r["ev_xd"][-1]
        sch.wait("sync", "ser_v", nxd_all)
        sch.dma("sync", g["d_xdT"], g["xdTbuf"][:], "dbg_dma")


def emit_x1(sch, g, nc, RG):
    sems = sch.sems
    n_evxd = sch.cnt["ev_xd"]
    sch.wait("sync", "ev_xd", n_evxd)
    if sch.on("sync"):
        myslot = nc.values_load(g["slot_sb"][0:1, 0:1], engines=[mybir.EngineType.SP],
                                min_val=0, max_val=1, skip_runtime_bounds_check=True)
        g["myslot_sync"] = myslot
        nc.sync.dma_start(out=g["shr_xd"][0:33, :], in_=g["xdTbuf"][:],
                          cond=(myslot == 0)).then_inc(sems["x1_dma"], 16)
        nc.sync.dma_start(out=g["shr_xd"][33:66, :], in_=g["xdTbuf"][:],
                          cond=myslot).then_inc(sems["x1_dma"], 16)
    sch.inc("x1_dma", 32)
    sch.wait("gpsimd", "x1_dma", sch.cnt["x1_dma"])
    sch.op("gpsimd", lambda e: e.collective_compute(
        "AllReduce", ALU.add, ins=[g["cbar_in"][:]], outs=[g["cbar_out"][:]],
        replica_groups=RG), "bar")
    sch.wait("sync", "bar", sch.cnt["bar"])
    if sch.on("sync"):
        myslot = g["myslot_sync"]
        for half, dst in ((0, g["em_sb"]), (1, g["ep_sb"])):
            nc.sync.dma_start(out=dst[0:33, 0:4096],
                              in_=g["shr_xd"][33:66, half * 4096:(half + 1) * 4096],
                              cond=(myslot == 0)).then_inc(sems["x1_dma"], 16)
            nc.sync.dma_start(out=dst[0:33, 0:4096],
                              in_=g["shr_xd"][0:33, half * 4096:(half + 1) * 4096],
                              cond=myslot).then_inc(sems["x1_dma"], 16)
    sch.inc("x1_dma", 64)
    sch.wait("vector", "x1_dma", sch.cnt["x1_dma"])
    sch.vop(lambda e: e.tensor_tensor(out=g["xdTbuf"][:, 0:4096], in0=g["xdTbuf"][:, 0:4096],
                                      in1=g["em_sb"][0:33, 0:4096], op=ALU.add))
    nadd = sch.vop(lambda e: e.tensor_tensor(out=g["xdTbuf"][:, 4096:8192],
                                             in0=g["xdTbuf"][:, 4096:8192],
                                             in1=g["ep_sb"][0:33, 0:4096], op=ALU.add))
    sch.wait_all("ser_v", nadd)


def emit_p2(sch, g, nc, debug_phase):
    sems = sch.sems
    for i_ in range(4):
        sch.wait("gpsimd", f"sp{i_}", sch.cnt[f"sp{i_}"])
    he_cnt = [0] * S
    slot_rd = {}
    save_cnt = {}
    wslot_g = {}
    psg_rd = [0, 0]
    emep_war = {}
    last_xdev = [0]
    g_cnt = [0] * S
    q_dtx = [0, 0]
    q_mmd = [0, 0]
    q_eact = [0, 0]
    q_evq = [0, 0]
    q_ydma_prev = [[0, 0] for _ in range(QUAD)]
    ld_cnt = [[0] * QUAD, [0] * QUAD]

    def emit_loads(qi2):
        rot2 = qi2 % 2
        sch.wait("gpsimd", "ser_v", q_dtx[rot2])
        for ci2, c2 in enumerate(QUADS[qi2]):
            sch.dma("gpsimd",
                    g["xi2hsq"][:, (rot2 * QUAD + ci2) * 512:(rot2 * QUAD + ci2 + 1) * 512],
                    g["xi2hs_hbm"][c2], f"p2l{ci2}")
            ld_cnt[rot2][ci2] = sch.cnt[f"p2l{ci2}"]

    emit_loads(0)
    for qi, quad in enumerate(QUADS):
        rot = qi % 2
        qn = len(quad)
        for ci, c in enumerate(quad):
            t0, cl = CHUNKS[c]
            sch.wait("tensor", "ser_v", last_xdev[0])

            def mk_xdt(e, t0=t0, cl=cl):
                e.transpose(g["ps_t1"][0:cl, 0:33], g["xdTbuf"][:, t0:t0 + cl],
                            g["ident_sb"][0:33, 0:33])
                return e.transpose(g["ps_t1"][64:64 + cl, 0:33],
                                   g["xdTbuf"][:, t0:t0 + cl],
                                   g["ident_sb"][0:33, 0:33])
            sch.op("tensor", mk_xdt, "tp_c")
            sch.wait("vector", "tp_c", sch.cnt["tp_c"])
            last_xdev[0] = sch.vop(lambda e, ci=ci, rot=rot: e.tensor_copy(
                out=g["xdhs"][:, (rot * QUAD + ci) * 33:(rot * QUAD + ci + 1) * 33],
                in_=g["ps_t1"][:, 0:33]))
        sch.wait("vector", "mm_d", q_mmd[rot])
        dtx_last = 0
        for ci, c in enumerate(quad):
            slot = rot * QUAD + ci
            dsl = g["dthsq"][:, slot * 512:(slot + 1) * 512]
            dtin_col = g["xdhs"][:, slot * 33: slot * 33 + 1]
            n1 = sch.vop(lambda e, dsl=dsl, dtin_col=dtin_col: e.scalar_tensor_tensor(
                out=dsl, in0=g["wdtrep_sb"][:], scalar=dtin_col, in1=g["bdtrep_sb"][:],
                op0=ALU.mult, op1=ALU.add))
            sch.wait("scalar", "ser_v", n1)
            sch.sop(lambda e, dsl=dsl: e.activation(out=dsl, in_=dsl, func=ACT.Exp))
            n2 = sch.sop(lambda e, dsl=dsl: e.activation(out=dsl, in_=dsl,
                                                         func=ACT.Ln, bias=1.0))
            sch.wait("vector", "ser_s", n2)
            sch.vop(lambda e, dsl=dsl: e.tensor_scalar(
                out=dsl, in0=dsl, scalar1=1e-4, scalar2=10.0, op0=ALU.max, op1=ALU.min))
            sch.wait("vector", f"p2l{ci}", ld_cnt[rot][ci])
            dtx_last = sch.vop(lambda e, slot=slot, dsl=dsl: e.tensor_tensor(
                out=g["dtxhsq"][:, slot * 512:(slot + 1) * 512], in0=dsl,
                in1=g["xi2hsq"][:, slot * 512:(slot + 1) * 512], op=ALU.mult))
        q_dtx[rot] = dtx_last
        if qi + 1 < len(QUADS):
            emit_loads(qi + 1)
        sch.wait("tensor", "ser_v", dtx_last)
        sch.wait("tensor", "ser_v", q_evq[rot])
        sch.wait("tensor", "ser_v", q_evq[1 - rot])

        def mk_dmm(e, quad=quad, rot=rot):
            last = None
            for ci, c in enumerate(quad):
                tail = (CHUNKS[c][1] != TC)
                lhsc = g["mtriD_sb"][:, 64:128] if tail else g["mtriD_sb"][:, 0:64]
                for h in range(2):
                    last = e.matmul(
                        g["ps_a"][64 * h:64 * h + 64, ci * 512:(ci + 1) * 512],
                        lhsc[64 * h:64 * h + 64, :],
                        g["dthsq"][64 * h:64 * h + 64,
                                   (rot * QUAD + ci) * 512:(rot * QUAD + ci + 1) * 512],
                        start=True, stop=True, tile_position=(64 * h, 64 * h))
            return last
        sch.op("tensor", mk_dmm, "mm_d")
        q_mmd[rot] = sch.cnt["mm_d"]
        sch.wait("vector", "mm_d", q_mmd[rot])
        sch.wait("vector", "ser_s", q_eact[rot])
        nev = sch.vop(lambda e, rot=rot, qn=qn: e.tensor_copy(
            out=g["dsbq"][:, rot * 2048: rot * 2048 + qn * 512],
            in_=g["ps_a"][:, 0:qn * 512]))
        q_evq[rot] = nev
        for sp in range(0, S, 2):
            lanes = (sp, sp + 1)
            eact_cnt = {}
            w_pipe = {s2: {} for s2 in lanes}

            def emit_eact(ci3):
                for li3, s3 in enumerate(lanes):
                    sch.wait("scalar", "ser_v", nev)
                    sch.wait("scalar", "ser_v", emep_war.get((li3, ci3), 0))
                    sch.sop(lambda e, li3=li3, s3=s3, ci3=ci3, rot=rot: e.activation(
                        out=g["em_sb"][:, li3 * 2048 + ci3 * 512:
                                       li3 * 2048 + (ci3 + 1) * 512],
                        in_=g["dsbq"][:, rot * 2048 + ci3 * 512:
                                      rot * 2048 + (ci3 + 1) * 512],
                        func=ACT.Exp, scale=g["acol_sb"][:, S + s3:S + s3 + 1]))
                    n2 = sch.sop(lambda e, li3=li3, s3=s3, ci3=ci3, rot=rot: e.activation(
                        out=g["ep_sb"][:, li3 * 2048 + ci3 * 512:
                                       li3 * 2048 + (ci3 + 1) * 512],
                        in_=g["dsbq"][:, rot * 2048 + ci3 * 512:
                                      rot * 2048 + (ci3 + 1) * 512],
                        func=ACT.Exp, scale=g["acol_sb"][:, s3:s3 + 1]))
                    eact_cnt[(li3, ci3)] = n2
                    q_eact[rot] = max(q_eact[rot], n2)

            def emit_pw(ci2):
                c2 = quad[ci2]
                slot2 = rot * QUAD + ci2
                for li2, s2 in enumerate(lanes):
                    sch.wait("vector", "ser_s", eact_cnt[(li2, ci2)])
                    wslot_pre = li2 * 2 + quad[ci2] % 2
                    sch.wait("vector", "mm_g", wslot_g.get(wslot_pre, 0))
                    Bcol = g["xdhs"][:, slot2 * 33 + 1 + s2: slot2 * 33 + 2 + s2]
                    wslot = li2 * 2 + c2 % 2
                    sch.vop(lambda e, slot2=slot2, Bcol=Bcol: e.tensor_scalar(
                        out=g["ptmp"][:], in0=g["dtxhsq"][:, slot2 * 512:(slot2 + 1) * 512],
                        scalar1=Bcol, scalar2=10.0, op0=ALU.mult, op1=ALU.min))
                    nw2 = sch.vop(lambda e, li2=li2, ci2=ci2, wslot=wslot: e.scalar_tensor_tensor(
                        out=g["wts_sb"][:, wslot * 512:(wslot + 1) * 512], in0=g["ptmp"][:],
                        scalar=-10.0,
                        in1=g["em_sb"][:, li2 * 2048 + ci2 * 512: li2 * 2048 + (ci2 + 1) * 512],
                        op0=ALU.max, op1=ALU.mult))
                    w_pipe[s2][ci2] = nw2

            emit_eact(0)
            if qn > 1:
                emit_eact(1)
            emit_pw(0)
            if qn > 1:
                emit_pw(1)
            for ci, c in enumerate(quad):
                t0, cl = CHUNKS[c]
                tail = (cl != TC)
                slot = rot * QUAD + ci
                qstart = (ci == 0)
                crs = f"cr{sp // 2}"
                ceng = "sync" if (sp // 2) % 2 == 0 else "gpsimd"
                wslot0 = c % 2
                n_cr = sch.cnt[crs]
                for li, s_ in enumerate(lanes):
                    wslot = li * 2 + c % 2
                    wt = g["wts_sb"][:, wslot * 512:(wslot + 1) * 512]
                    if qstart:
                        sel = g["selq_sb"][:, s_ * 64:(s_ + 1) * 64]
                        csrc = g["carrysave"][:]
                        sv = save_cnt.get(sp)
                        if sv is not None:
                            sch.wait("tensor", sv[0], sv[1])
                    else:
                        pslot = li * 2 + (c - 1) % 2
                        sel = g["sel62_sb"]
                        csrc = g["hets_sb"][:, pslot * 512:(pslot + 1) * 512]
                    sch.wait("tensor", "ser_v", w_pipe[s_][ci])
                    sch.wait("tensor", "ser_v", max(he_cnt[s_], psg_rd[li]))
                    lhsg = g["mtri_sb"][:, 64:128] if tail else g["mtri_sb"][:, 0:64]

                    def mk_g(e, li=li, lhsg=lhsg, wt=wt, sel=sel, csrc=csrc):
                        last = None
                        for h in range(2):
                            pg = g["ps_g"][64 * h:64 * h + 64,
                                           li * 512:(li + 1) * 512]
                            e.matmul(pg, lhsg[64 * h:64 * h + 64, :],
                                     wt[64 * h:64 * h + 64, :],
                                     start=True, stop=False,
                                     tile_position=(64 * h, 64 * h))
                            last = e.matmul(
                                pg, sel[64 * h:64 * h + 64, :],
                                csrc[64 * h:64 * h + 64, :],
                                start=False, stop=True,
                                tile_position=(64 * h, 64 * h))
                        return last
                    sch.op("tensor", mk_g, "mm_g")
                    g_cnt[s_] = sch.cnt["mm_g"]
                    wslot_g[wslot] = sch.cnt["mm_g"]
                    if not qstart:
                        slot_rd[pslot] = ("mm_g", sch.cnt["mm_g"])
                for li, s_ in enumerate(lanes):
                    hslot = li * 2 + c % 2
                    het = g["hets_sb"][:, hslot * 512:(hslot + 1) * 512]
                    sch.wait("vector", "mm_g", g_cnt[s_])
                    sch.wait("vector", crs, n_cr)
                    if hslot in slot_rd:
                        sr = slot_rd[hslot]
                        sch.wait("vector", sr[0], sr[1])
                    nhe = sch.vop(lambda e, het=het, li=li, ci=ci: e.tensor_tensor(
                        out=het,
                        in0=g["ep_sb"][:, li * 2048 + ci * 512: li * 2048 + (ci + 1) * 512],
                        in1=g["ps_g"][:, li * 512:(li + 1) * 512], op=ALU.mult))
                    he_cnt[s_] = nhe
                    psg_rd[li] = nhe
                    emep_war[(li, ci)] = max(w_pipe[s_][ci], nhe)
                for li, s_ in enumerate(lanes):
                    het = g["hets_sb"][:, (li * 2 + c % 2) * 512:
                                       (li * 2 + c % 2 + 1) * 512]
                    Ccol = g["xdhs"][:, slot * 33 + 17 + s_: slot * 33 + 18 + s_]
                    adst = g["accq"][:, ci * 512:(ci + 1) * 512]
                    if s_ == 0:
                        sch.wait("vector", f"y{ci * 2}", q_ydma_prev[ci][0])
                        sch.wait("vector", f"y{ci * 2 + 1}", q_ydma_prev[ci][1])
                        if debug_phase == 2:
                            sch.wait("vector", "dbg2_dma", sch.cnt["dbg2_dma"])
                        sch.vop(lambda e, adst=adst, het=het, Ccol=Ccol: e.tensor_scalar(
                            out=adst, in0=het, scalar1=Ccol, scalar2=None, op0=ALU.mult))
                    else:
                        sch.vop(lambda e, adst=adst, het=het, Ccol=Ccol:
                                e.scalar_tensor_tensor(
                                    out=adst, in0=het, scalar=Ccol, in1=adst,
                                    op0=ALU.mult, op1=ALU.add))
                if ci == qn - 1 and c != NCHUNK - 1:
                    sch.wait(ceng, "ser_v", max(he_cnt[sp], he_cnt[sp + 1]))
                    hsv2 = g["hets_sb"][:].rearrange("p (sl t) -> p sl t", sl=4)
                    for h in range(2):
                        sch.dma(ceng,
                                g["carrysave"][64 * h + sp: 64 * h + sp + 2, :]
                                .rearrange("p (a t) -> p a t", a=1),
                                hsv2[64 * h + cl - 1: 64 * h + cl, wslot0::2, :],
                                crs)
                    slot_rd[wslot0] = (crs, sch.cnt[crs])
                    slot_rd[wslot0 + 2] = (crs, sch.cnt[crs])
                    save_cnt[sp] = (crs, sch.cnt[crs])
                if ci + 2 < qn:
                    emit_eact(ci + 2)
                    emit_pw(ci + 2)
        nacc_fin = sch.cnt["ser_v"]
        sch.wait("gpsimd", "ser_v", nacc_fin)
        for ci, c in enumerate(quad):
            t0, cl = CHUNKS[c]
            acc_fin = g["accq"][:, ci * 512:(ci + 1) * 512]
            sch.dma("gpsimd", g["y_hbm"][t0:t0 + cl, 0:512], acc_fin[0:cl, :],
                    f"y{ci * 2}")
            sch.dma("gpsimd", g["y_hbm"][t0:t0 + cl, 512:1024], acc_fin[64:64 + cl, :],
                    f"y{ci * 2 + 1}")
            q_ydma_prev[ci] = [sch.cnt[f"y{ci * 2}"], sch.cnt[f"y{ci * 2 + 1}"]]
            if debug_phase == 2:
                sch.dma("gpsimd", g["d_y"][t0:t0 + cl, 0:512], acc_fin[0:cl, :], "dbg2_dma")
                sch.dma("gpsimd", g["d_y"][t0:t0 + cl, 512:1024], acc_fin[64:64 + cl, :], "dbg2_dma")


def emit_p3(sch, g, nc, RG, debug_phase):
    sems = sch.sems
    for i_ in range(8):
        sch.wait("gpsimd", f"y{i_}", sch.cnt[f"y{i_}"])
    for e_ in ("gpsimd", "sync"):
        sch.wait(e_, "sp_dma", sch.cnt["sp_dma"])
        for i_ in range(4):
            sch.wait(e_, f"sp{i_}", sch.cnt[f"sp{i_}"])
    sch.wait("scalar", "mm_d", sch.cnt["mm_d"])
    st_tile = []
    for mt, (t0, ln) in enumerate(T128T):
        sch.wait("gpsimd", "ser_v", st_tile[-1] if st_tile else 0)
        sch.dma("gpsimd", g["y_ld"][0:ln, :], g["y_hbm"][t0:t0 + ln, :], "p3_ld")
        n_ld = sch.cnt["p3_ld"]
        sch.wait("vector", "p3_ld", n_ld)
        sch.vop(lambda e, mt=mt, ln=ln: e.reduce_sum(
            out=g["statsbuf"][0:ln, 2 * mt:2 * mt + 1], in_=g["y_ld"][0:ln, :],
            axis=mybir.AxisListType.X))
        sch.wait("scalar", "p3_ld", n_ld)
        nsq = sch.sop(lambda e, ln=ln: e.activation(
            out=g["x1_sb"][0:ln, :], in_=g["y_ld"][0:ln, :], func=ACT.Square))
        sch.wait("vector", "ser_s", nsq)
        nst = sch.vop(lambda e, mt=mt, ln=ln: e.reduce_sum(
            out=g["statsbuf"][0:ln, 2 * mt + 1:2 * mt + 2], in_=g["x1_sb"][0:ln, :],
            axis=mybir.AxisListType.X))
        st_tile.append(nst)
    sch.wait("sync", "ser_v", st_tile[-1])
    if sch.on("sync"):
        myslot = nc.values_load(g["slot_sb"][0:1, 0:1], engines=[mybir.EngineType.SP],
                                min_val=0, max_val=1, skip_runtime_bounds_check=True)
        g["myslot_sync"] = myslot
        nc.sync.dma_start(out=g["shr_st"][0:128, :], in_=g["statsbuf"][:, 0:128],
                          cond=(myslot == 0)).then_inc(sems["x2_dma"], 16)
        nc.sync.dma_start(out=g["shr_st"][128:256, :], in_=g["statsbuf"][:, 0:128],
                          cond=myslot).then_inc(sems["x2_dma"], 16)
    sch.inc("x2_dma", 32)
    sch.wait("gpsimd", "x2_dma", sch.cnt["x2_dma"])
    sch.op("gpsimd", lambda e: e.collective_compute(
        "AllReduce", ALU.add, ins=[g["cbar_in"][:]], outs=[g["cbar_out"][:]],
        replica_groups=RG), "bar")
    sch.wait("sync", "bar", sch.cnt["bar"])
    if sch.on("sync"):
        myslot = g["myslot_sync"]
        nc.sync.dma_start(out=g["statspeer"][:], in_=g["shr_st"][128:256, :],
                          cond=(myslot == 0)).then_inc(sems["x2_dma"], 16)
        nc.sync.dma_start(out=g["statspeer"][:], in_=g["shr_st"][0:128, :],
                          cond=myslot).then_inc(sems["x2_dma"], 16)
    sch.inc("x2_dma", 32)
    sch.wait("vector", "x2_dma", sch.cnt["x2_dma"])
    sv = g["statsbuf"][:, 0:128].rearrange("p (m two) -> p m two", two=2)
    sch.vop(lambda e: e.tensor_tensor(out=g["statsbuf"][:, 0:128],
                                      in0=g["statsbuf"][:, 0:128],
                                      in1=g["statspeer"][:, 0:128], op=ALU.add))
    sch.vop(lambda e: e.tensor_scalar_mul(
        out=g["statsbuf"][:, 128:192], in0=sv[:, :, 0], scalar1=1.0 / D_INNER))
    sch.vop(lambda e: e.tensor_scalar_mul(
        out=g["statspeer"][:, 0:64], in0=sv[:, :, 1], scalar1=1.0 / D_INNER))
    sch.vop(lambda e: e.tensor_tensor(out=g["statspeer"][:, 64:128],
                                      in0=g["statsbuf"][:, 128:192],
                                      in1=g["statsbuf"][:, 128:192], op=ALU.mult))
    nvar = sch.vop(lambda e: e.tensor_tensor(out=g["statspeer"][:, 0:64],
                                             in0=g["statspeer"][:, 0:64],
                                             in1=g["statspeer"][:, 64:128], op=ALU.subtract))
    sch.wait("scalar", "ser_v", nvar)
    nsq2 = sch.sop(lambda e: e.activation(out=g["statspeer"][:, 64:128],
                                          in_=g["statspeer"][:, 0:64],
                                          func=ACT.Sqrt, bias=g["epscol"][:]))
    sch.wait("vector", "ser_s", nsq2)
    nrst = sch.vop(lambda e: e.reciprocal(out=g["statsbuf"][:, 0:64],
                                          in_=g["statspeer"][:, 64:128]))
    gt_done = []
    ogT_prev = [0]
    for mt, (t0, ln) in enumerate(T128T):
        sch.wait("gpsimd", "ser_v", gt_done[-1] if gt_done else nrst)
        sch.dma("gpsimd", g["y_ld"][0:ln, :], g["y_hbm"][t0:t0 + ln, :], "p3_ld")
        sch.dma("gpsimd", g["sz_ld"][0:ln, :], g["sz_hbm"][t0:t0 + ln, :], "p3_ld")
        sch.wait("sync", "ser_v", gt_done[-1] if gt_done else nrst)
        sch.wait("sync", "p3t_ld", sch.cnt["p3t_ld"])
        if sch.on("sync"):
            for kb in range(8):
                sch.eng.dma_start_transpose(
                    g["xi2_ld"][0:ln, kb * 128:(kb + 1) * 128],
                    g["xi2t_hbm"][kb * 128:(kb + 1) * 128, t0:t0 + ln]).then_inc(sems["p3t_ld"], 16)
        sch.inc("p3t_ld", 8 * 16)
        n_ld = sch.cnt["p3_ld"]
        sch.wait("vector", "p3_ld", n_ld)
        sch.wait("vector", "p3t_ld", sch.cnt["p3t_ld"])
        sch.wait("vector", "ser_v", nrst)
        mu_c = g["statsbuf"][:, 128 + mt:129 + mt]
        rs_c = g["statsbuf"][:, mt:mt + 1]
        sch.vop(lambda e, ln=ln, mu_c=mu_c, rs_c=rs_c: e.tensor_scalar(
            out=g["gt1"][0:ln, :], in0=g["y_ld"][0:ln, :], scalar1=mu_c[0:ln, :],
            scalar2=rs_c[0:ln, :], op0=ALU.subtract, op1=ALU.mult))
        sch.vop(lambda e, ln=ln: e.tensor_tensor(
            out=g["gt1"][0:ln, :], in0=g["gt1"][0:ln, :], in1=g["lnwrep_sb"][0:ln, :],
            op=ALU.mult))
        sch.vop(lambda e, ln=ln: e.tensor_tensor(
            out=g["gt2"][0:ln, :], in0=g["xi2_ld"][0:ln, :], in1=g["drep_sb"][0:ln, :],
            op=ALU.mult))
        sch.vop(lambda e, ln=ln: e.tensor_tensor(
            out=g["gt1"][0:ln, :], in0=g["gt1"][0:ln, :], in1=g["gt2"][0:ln, :],
            op=ALU.add))
        sch.vop(lambda e, ln=ln: e.tensor_tensor(
            out=g["gt1"][0:ln, :], in0=g["gt1"][0:ln, :], in1=g["lnbrep_sb"][0:ln, :],
            op=ALU.add))
        nog = sch.vop(lambda e, ln=ln: e.tensor_tensor(
            out=g["og_sb"][0:ln, :], in0=g["gt1"][0:ln, :], in1=g["sz_ld"][0:ln, :],
            op=ALU.mult))
        sch.wait("tensor", "ser_v", nog)
        sch.wait("tensor", "ser_v", ogT_prev[0])

        def mk_ogt(e, ln=ln):
            last = None
            for kb in range(8):
                last = e.transpose(g["ps_t1"][:, kb * 128: kb * 128 + ln],
                                   g["og_sb"][0:ln, kb * 128:(kb + 1) * 128],
                                   g["ident_sb"][0:ln, 0:ln])
            return last
        sch.op("tensor", mk_ogt, "tp_og")
        sch.wait("vector", "tp_og", sch.cnt["tp_og"])
        nevog = sch.vop(lambda e: e.tensor_copy(out=g["ogT_sb"][:], in_=g["ps_t1"][:]))
        ogT_prev[0] = nevog
        sch.wait("tensor", "ser_v", nevog)
        sch.wait("tensor", "ser_v", sch.cnt["ser_v"])

        def mk_wo(e, ln=ln):
            last = None
            for kb in range(8):
                for oc in range(2):
                    last = e.matmul(
                        g["ps_a"][0:ln, oc * 512:(oc + 1) * 512],
                        g["ogT_sb"][:, kb * 128:(kb + 1) * 128][:, 0:ln],
                        g["WoutT_sb"][:, kb * 1024 + oc * 512: kb * 1024 + (oc + 1) * 512],
                        start=(kb == 0), stop=(kb == 7))
            return last
        sch.op("tensor", mk_wo, "mm_wo")
        n_wo = sch.cnt["mm_wo"]
        sch.wait("vector", "mm_wo", n_wo)
        sch.wait("vector", "x3_dma", len(gt_done) * 16)
        nop_ = sch.vop(lambda e, ln=ln: e.tensor_copy(
            out=g["op_sb"][0:ln, :], in_=g["ps_a"][0:ln, 0:1024]))
        gt_done.append(nop_)
        sch.wait("sync", "ser_v", nop_)
        sch.dma("sync", g["myout_hbm"][t0:t0 + ln, :], g["op_sb"][0:ln, :], "x3_dma")
    sch.wait("sync", "x3_dma", sch.cnt["x3_dma"])
    if sch.on("sync"):
        myslot = g["myslot_sync"]
        nc.sync.dma_start(out=g["shr_out"][0:L, :], in_=g["myout_hbm"][:, :],
                          cond=(myslot == 0)).then_inc(sems["x3_dma"], 16)
        nc.sync.dma_start(out=g["shr_out"][L:2 * L, :], in_=g["myout_hbm"][:, :],
                          cond=myslot).then_inc(sems["x3_dma"], 16)
    sch.inc("x3_dma", 32)
    sch.wait("gpsimd", "x3_dma", sch.cnt["x3_dma"])
    sch.op("gpsimd", lambda e: e.collective_compute(
        "AllReduce", ALU.add, ins=[g["cbar_in"][:]], outs=[g["cbar_out"][:]],
        replica_groups=RG), "bar")


def emit_p4(sch, g, nc, RG):
    sems = sch.sems
    nbar = sch.cnt["bar"]
    sch.wait_all("bar", nbar)
    AX = mybir.AxisListType.X
    if sch.on("sync"):
        myslot = g["myslot_sync"]
        for base, dst in ((0, g["loc0_hbm"]), (L, g["loc1_hbm"])):
            nc.sync.dma_start(out=dst[:, :], in_=g["shr_out"][base:base + 4096, :],
                              cond=(myslot == 0)).then_inc(sems["p4_ld"], 16)
            nc.sync.dma_start(out=dst[:, :], in_=g["shr_out"][base + 4096:base + 8192, :],
                              cond=myslot).then_inc(sems["p4_ld"], 16)
    sch.inc("p4_ld", 64)
    prev = [0]
    prevT = [0]
    for mt in range(N_P4):
        t0 = mt * 128
        sch.wait("sync", "ser_v", prev[0])
        sch.dma("sync", g["p4a"][:, 0:1024], g["loc0_hbm"][t0:t0 + 128, :], "p4_ld")
        sch.dma("sync", g["p4b"][:, 0:1024], g["loc1_hbm"][t0:t0 + 128, :], "p4_ld")
        sch.dma("gpsimd", g["xrow_sb"], g["x_rows"][t0:t0 + 128, :], "p4g_ld")
        n_ld = sch.cnt["p4_ld"]
        sch.wait("vector", "p4_ld", n_ld)
        sch.wait("vector", "p4g_ld", sch.cnt["p4g_ld"])
        sch.vop(lambda e: e.tensor_tensor(out=g["p4a"][:, 0:1024], in0=g["p4a"][:, 0:1024],
                                          in1=g["p4b"][:, 0:1024], op=ALU.add))
        nx1 = sch.vop(lambda e: e.tensor_tensor(out=g["x1_sb"], in0=g["xrow_sb"],
                                                in1=g["p4a"][:, 0:1024], op=ALU.add))
        sch.vop(lambda e: e.reduce_sum(out=g["sc1"][:, 0:1], in_=g["x1_sb"], axis=AX))
        sch.wait("scalar", "ser_v", nx1)
        sch.wait("scalar", "out_dma", mt * 16)
        nsq = sch.sop(lambda e: e.activation(out=g["x2_sb"], in_=g["x1_sb"], func=ACT.Square))
        sch.wait("vector", "ser_s", nsq)
        sch.vop(lambda e: e.reduce_sum(out=g["sc1"][:, 1:2], in_=g["x2_sb"], axis=AX))
        sch.vop(lambda e: e.tensor_scalar_mul(out=g["sc1"][:, 2:3], in0=g["sc1"][:, 0:1],
                                              scalar1=1.0 / D_MODEL))
        sch.vop(lambda e: e.tensor_scalar_mul(out=g["sc1"][:, 3:4], in0=g["sc1"][:, 1:2],
                                              scalar1=1.0 / D_MODEL))
        sch.vop(lambda e: e.tensor_tensor(out=g["sc1"][:, 4:5], in0=g["sc1"][:, 2:3],
                                          in1=g["sc1"][:, 2:3], op=ALU.mult))
        nvar = sch.vop(lambda e: e.tensor_tensor(out=g["sc1"][:, 3:4], in0=g["sc1"][:, 3:4],
                                                 in1=g["sc1"][:, 4:5], op=ALU.subtract))
        sch.wait("scalar", "ser_v", nvar)
        nsd = sch.sop(lambda e: e.activation(out=g["sc1"][:, 4:5], in_=g["sc1"][:, 3:4],
                                             func=ACT.Sqrt, bias=g["epscol"][:]))
        sch.wait("vector", "ser_s", nsd)
        sch.vop(lambda e: e.reciprocal(out=g["sc1"][:, 5:6], in_=g["sc1"][:, 4:5]))
        nx1n = sch.vop(lambda e: e.tensor_scalar(
            out=g["x1n_sb"][:, 0:1024], in0=g["x1_sb"], scalar1=g["sc1"][:, 2:3],
            scalar2=g["sc1"][:, 5:6], op0=ALU.subtract, op1=ALU.mult))
        sch.wait("tensor", "ser_v", nx1n)
        sch.wait("tensor", "ser_v", prevT[0])

        def mk_xt(e):
            last = None
            for kb in range(8):
                last = e.transpose(g["ps_t1"][:, kb * 128:(kb + 1) * 128],
                                   g["x1n_sb"][:, kb * 128:(kb + 1) * 128][0:128, :],
                                   g["ident_sb"][:])
            return last
        sch.op("tensor", mk_xt, "p4_tp")
        sch.wait("vector", "p4_tp", sch.cnt["p4_tp"])
        nxT = sch.vop(lambda e: e.tensor_copy(out=g["x1nT_sb"][:, 0:1024], in_=g["ps_t1"][:]))
        sch.wait("tensor", "ser_v", nxT)

        def mk_ffn1(e):
            last = None
            for kb in range(8):
                for oc in range(2):
                    last = e.matmul(
                        g["ps_a"][:, oc * 512:(oc + 1) * 512],
                        g["x1nT_sb"][:, kb * 128:(kb + 1) * 128],
                        g["baseWT_sb"][:, kb * 1024 + oc * 512: kb * 1024 + (oc + 1) * 512],
                        start=(kb == 0), stop=False)
            for oc in range(2):
                last = e.matmul(g["ps_a"][:, oc * 512:(oc + 1) * 512],
                                g["onesrowb_sb"][:],
                                g["bbrep_sb"][:, oc * 512:(oc + 1) * 512],
                                start=False, stop=False)
            for kb in range(8):
                last = e.matmul(g["ps_g"][0:16, 0:128],
                                g["sharedAT_sb"][:, kb * 16:(kb + 1) * 16],
                                g["x1nT_sb"][:, kb * 128:(kb + 1) * 128],
                                start=(kb == 0), stop=(kb == 7))
            for kb in range(8):
                last = e.matmul(g["ps_g"][:, 512:520],
                                g["x1nT_sb"][:, kb * 128:(kb + 1) * 128],
                                g["routerWT_sb"][:, kb * 8:(kb + 1) * 8],
                                start=(kb == 0), stop=False)
            last = e.matmul(g["ps_g"][:, 512:520], g["onesrowb_sb"][:],
                            g["rbrow_sb"][:], start=False, stop=True)
            for kb in range(8):
                last = e.matmul(g["ps_g"][:, 640:736],
                                g["x1nT_sb"][:, kb * 128:(kb + 1) * 128],
                                g["routedAT_sb"][:, kb * 96:(kb + 1) * 96],
                                start=(kb == 0), stop=(kb == 7))
            return last
        sch.op("tensor", mk_ffn1, "p4_mm")
        nmm1 = sch.cnt["p4_mm"]
        sch.wait("vector", "p4_mm", nmm1)
        sch.vop(lambda e: e.tensor_copy(out=g["hshT_sb"][:], in_=g["ps_g"][0:16, 0:128]))
        sch.vop(lambda e: e.tensor_copy(out=g["r6"][:, 0:8], in_=g["ps_g"][:, 512:520]))
        sch.vop(lambda e: e.reduce_max(out=g["r6"][:, 8:9], in_=g["r6"][:, 0:8], axis=AX))
        sch.vop(lambda e: e.tensor_scalar_mul(out=g["r6"][:, 9:10], in0=g["r6"][:, 8:9],
                                              scalar1=-1.0))
        nneg = sch.cnt["ser_v"]
        sch.wait("scalar", "ser_v", nneg)
        nexp = sch.sop(lambda e: e.activation(out=g["r6"][:, 16:24], in_=g["r6"][:, 0:8],
                                              func=ACT.Exp, bias=g["r6"][:, 9:10]))
        sch.wait("vector", "ser_s", nexp)
        sch.vop(lambda e: e.reduce_sum(out=g["r6"][:, 10:11], in_=g["r6"][:, 16:24], axis=AX))
        sch.vop(lambda e: e.reduce_max(out=g["r6"][:, 11:12], in_=g["r6"][:, 16:24], axis=AX))
        sch.vop(lambda e: e.tensor_scalar(out=g["r6"][:, 24:32], in0=g["r6"][:, 16:24],
                                          scalar1=g["r6"][:, 11:12], scalar2=None,
                                          op0=ALU.is_equal))
        sch.vop(lambda e: e.scalar_tensor_tensor(out=g["r6"][:, 24:32], in0=g["r6"][:, 24:32],
                                                 scalar=-1e30, in1=g["r6"][:, 16:24],
                                                 op0=ALU.mult, op1=ALU.add))
        sch.vop(lambda e: e.reduce_max(out=g["r6"][:, 12:13], in_=g["r6"][:, 24:32], axis=AX))
        sch.vop(lambda e: e.tensor_scalar_mul(out=g["r6"][:, 13:14], in0=g["r6"][:, 10:11],
                                              scalar1=1e-6))
        sch.vop(lambda e: e.tensor_tensor(out=g["r6"][:, 14:15], in0=g["r6"][:, 11:12],
                                          in1=g["r6"][:, 12:13], op=ALU.add))
        nden = sch.vop(lambda e: e.tensor_tensor(out=g["r6"][:, 14:15], in0=g["r6"][:, 14:15],
                                                 in1=g["r6"][:, 13:14], op=ALU.add))
        sch.vop(lambda e: e.reciprocal(out=g["r6"][:, 15:16], in_=g["r6"][:, 14:15]))
        sch.vop(lambda e: e.tensor_scalar(out=g["r6"][:, 32:40], in0=g["r6"][:, 16:24],
                                          scalar1=g["r6"][:, 12:13], scalar2=None,
                                          op0=ALU.is_ge))
        nwts = sch.vop(lambda e: e.scalar_tensor_tensor(
            out=g["r6"][:, 40:48], in0=g["r6"][:, 16:24], scalar=g["r6"][:, 15:16],
            in1=g["r6"][:, 32:40], op0=ALU.mult, op1=ALU.mult))
        nhrw = sch.vop(lambda e: e.tensor_tensor(
            out=g["hrw_sb"][:].rearrange("p (x r) -> p x r", x=6),
            in0=g["ps_g"][:, 640:736].rearrange("p (x r) -> p x r", x=6),
            in1=g["r6"][:, 40:46].unsqueeze(2).broadcast_to([128, 6, 16]), op=ALU.mult))
        sch.wait("tensor", "ser_v", nhrw)

        def mk_hrt(e):
            return e.transpose(g["ps_t1"][0:96, 0:128], g["hrw_sb"][:], g["ident_sb"][:])
        sch.op("tensor", mk_hrt, "p4_tp")
        sch.wait("vector", "p4_tp", sch.cnt["p4_tp"])
        nhrT = sch.vop(lambda e: e.tensor_copy(out=g["hrwT_sb"][:], in_=g["ps_t1"][0:96, 0:128]))
        prevT[0] = nhrT
        sch.wait("tensor", "ser_v", nhrT)

        def mk_ffn2(e):
            last = None
            for oc in range(2):
                last = e.matmul(g["ps_a"][:, oc * 512:(oc + 1) * 512],
                                g["hshT_sb"][:],
                                g["sharedBT_sb"][:, oc * 512:(oc + 1) * 512],
                                start=False, stop=False)
            for oc in range(2):
                last = e.matmul(g["ps_a"][:, oc * 512:(oc + 1) * 512],
                                g["hrwT_sb"][:],
                                g["routedBT_sb"][:, oc * 512:(oc + 1) * 512],
                                start=False, stop=True)
            return last
        sch.op("tensor", mk_ffn2, "p4_mm")
        nmm2 = sch.cnt["p4_mm"]
        sch.wait("vector", "p4_mm", nmm2)
        sch.wait("vector", "out_dma", mt * 16)
        nx2 = sch.vop(lambda e: e.tensor_tensor(out=g["x2_sb"], in0=g["x1n_sb"][:, 0:1024],
                                                in1=g["ps_a"][:, 0:1024], op=ALU.add))
        sch.vop(lambda e: e.reduce_sum(out=g["sc1"][:, 0:1], in_=g["x2_sb"], axis=AX))
        sch.wait("scalar", "ser_v", nx2)
        nsq2 = sch.sop(lambda e: e.activation(out=g["x1_sb"], in_=g["x2_sb"], func=ACT.Square))
        sch.wait("vector", "ser_s", nsq2)
        sch.vop(lambda e: e.reduce_sum(out=g["sc1"][:, 1:2], in_=g["x1_sb"], axis=AX))
        sch.vop(lambda e: e.tensor_scalar_mul(out=g["sc1"][:, 2:3], in0=g["sc1"][:, 0:1],
                                              scalar1=1.0 / D_MODEL))
        sch.vop(lambda e: e.tensor_scalar_mul(out=g["sc1"][:, 3:4], in0=g["sc1"][:, 1:2],
                                              scalar1=1.0 / D_MODEL))
        sch.vop(lambda e: e.tensor_tensor(out=g["sc1"][:, 4:5], in0=g["sc1"][:, 2:3],
                                          in1=g["sc1"][:, 2:3], op=ALU.mult))
        nvar2 = sch.vop(lambda e: e.tensor_tensor(out=g["sc1"][:, 3:4], in0=g["sc1"][:, 3:4],
                                                  in1=g["sc1"][:, 4:5], op=ALU.subtract))
        sch.wait("scalar", "ser_v", nvar2)
        nsd2 = sch.sop(lambda e: e.activation(out=g["sc1"][:, 4:5], in_=g["sc1"][:, 3:4],
                                              func=ACT.Sqrt, bias=g["epscol"][:]))
        sch.wait("vector", "ser_s", nsd2)
        sch.vop(lambda e: e.reciprocal(out=g["sc1"][:, 5:6], in_=g["sc1"][:, 4:5]))
        nout = sch.vop(lambda e: e.tensor_scalar(
            out=g["x2_sb"], in0=g["x2_sb"], scalar1=g["sc1"][:, 2:3],
            scalar2=g["sc1"][:, 5:6], op0=ALU.subtract, op1=ALU.mult))
        prev[0] = nout
        sch.wait("gpsimd", "ser_v", nout)
        sch.dma("gpsimd", g["out"][t0:t0 + 128, :], g["x2_sb"][:], "out_dma")


def emit_tail(sch, g, debug_phase):
    for e in Sched.ENGINES:
        for i_ in range(4):
            sch.wait(e, f"sp{i_}", sch.cnt[f"sp{i_}"])
        for i_ in range(8):
            sch.wait(e, f"cr{i_}", sch.cnt[f"cr{i_}"])
            sch.wait(e, f"cs{i_}", sch.cnt[f"cs{i_}"])
        if debug_phase in (1, 2):
            sch.wait(e, "dbg_dma", sch.cnt["dbg_dma"])
            sch.wait(e, "dbg2_dma", sch.cnt["dbg2_dma"])
        if debug_phase >= 2:
            for i_ in range(8):
                sch.wait(e, f"y{i_}", sch.cnt[f"y{i_}"])
        if debug_phase >= 4:
            sch.wait(e, "out_dma", sch.cnt["out_dma"])



def _consts():
    tri = np.tril(np.ones((63, 63), np.float32))
    mtri_aug = np.zeros((64, 64), np.float32)
    mtri_aug[0:63, 0:63] = tri.T
    mtri_tail = np.zeros((64, 64), np.float32)
    mtri_tail[0:2, 0:2] = np.tril(np.ones((2, 2), np.float32)).T
    mtriD = np.zeros((64, 64), np.float32)
    mtriD[0:63, 0:63] = tri.T
    mtriD_tail = np.zeros((64, 64), np.float32)
    mtriD_tail[0:2, 0:2] = np.tril(np.ones((2, 2), np.float32)).T
    return tuple(np.vstack([m, m]) for m in (mtri_aug, mtri_tail, mtriD, mtriD_tail))


def _selg():
    sg = np.zeros((128, 64 + 1024), np.float32)
    sg[62, 0:64] = 1.0
    sg[126, 0:64] = 1.0
    for s_ in range(S):
        sg[s_, 64 + s_ * 64: 64 + (s_ + 1) * 64] = 1.0
        sg[64 + s_, 64 + s_ * 64: 64 + (s_ + 1) * 64] = 1.0
    return sg.astype(BF)


def make_inputs(core, kw):
    b, j = core // 2, core % 2
    own = slice(j * DOWN, (j + 1) * DOWN)
    x = np.asarray(kw["L0_attn_x"])
    W_in = np.asarray(kw["L0_attn_W_in"])
    mtri_aug, mtri_tail, mtriD, mtriD_tail = _consts()
    rows = np.concatenate([W_in[j * DOWN:(j + 1) * DOWN],
                           W_in[D_INNER + j * DOWN: D_INNER + (j + 1) * DOWN]], 0)
    routedAT = np.ascontiguousarray(
        np.asarray(kw["L0_ffn_routed_A"]).transpose(2, 0, 1).reshape(D_MODEL, 96))
    routedBT = np.ascontiguousarray(
        np.asarray(kw["L0_ffn_routed_B"]).transpose(0, 2, 1).reshape(96, D_MODEL))
    routerWT = np.zeros((D_MODEL, 8), np.float32)
    routerWT[:, 0:6] = np.asarray(kw["L0_ffn_router_W"]).T
    router_b = np.full((1, 8), -1e30, np.float32)
    router_b[0, 0:6] = np.asarray(kw["L0_ffn_router_b"])
    f = np.float32

    def c(a):
        return np.ascontiguousarray(a).astype(f)
    return {
        "xT": c(x[b].T),
        "x_rows": c(x[b, j * (L // 2):(j + 1) * (L // 2)]),
        "W_inT": c(rows.T),
        "convw": c(np.asarray(kw["L0_attn_conv_w"])[own, 0, :]),
        "convb": c(np.asarray(kw["L0_attn_conv_b"])[own].reshape(8, 128).T),
        "W_xT": c(np.asarray(kw["L0_attn_W_x"])[:, own].T),
        "W_dt_own": c(np.asarray(kw["L0_attn_W_dt"])[None, own, 0]),
        "b_dt_own": c(np.asarray(kw["L0_attn_b_dt"])[None, own]),
        "A_log_in": c(np.asarray(kw["L0_attn_A_log"])[None, :]),
        "D_own": c(np.asarray(kw["L0_attn_D_param"])[None, own]),
        "lnw_own": c(np.asarray(kw["L0_attn_ln_w"])[None, own]),
        "lnb_own": c(np.asarray(kw["L0_attn_ln_b"])[None, own]),
        "W_outT": c(np.asarray(kw["L0_attn_W_out"])[:, own].T),
        "baseWT": c(np.asarray(kw["L0_ffn_base_W"]).T),
        "base_b": c(np.asarray(kw["L0_ffn_base_b"])[None, :]),
        "sharedAT": c(np.asarray(kw["L0_ffn_shared_A"]).T),
        "sharedBT": c(np.asarray(kw["L0_ffn_shared_B"]).T),
        "routerWT": routerWT,
        "router_b": router_b,
        "routedAT": c(routedAT),
        "routedBT": c(routedBT),
        "mtri_aug": mtri_aug.astype(BF),
        "mtri_tail": mtri_tail.astype(BF),
        "mtriD": mtriD,
        "mtriD_tail": mtriD_tail,
        "ident": np.eye(128, dtype=f).astype(BF),
        "onesrow": np.ones((1, 128), f),
        "onesrowb": np.ones((1, 128), f).astype(BF),
        "selg": _selg(),
        "slot": np.array([[j]], np.int32),
    }


def kernel(**kw):
    nc = build_program()
    in_maps = [make_inputs(c, kw) for c in range(8)]
    res = run_bass_kernel_spmd(nc, in_maps, core_ids=list(range(8)))
    out = np.empty((NB, L, D_MODEL), np.float32)
    for c in range(8):
        b, j = c // 2, c % 2
        out[b, j * (L // 2):(j + 1) * (L // 2)] = res.results[c]["out"]
    return out
